# version 07
# baseline (speedup 1.0000x reference)
import numpy as np
from contextlib import ExitStack

import concourse.bass as bass
import concourse.mybir as mybir
from concourse.bass_utils import run_bass_kernel_spmd

F32 = mybir.dt.float32
BF16 = mybir.dt.bfloat16
FP16 = mybir.dt.float16
AF = mybir.ActivationFunctionType
ALU = mybir.AluOpType
AX = mybir.AxisListType

T = 2048
D = 1024
NH = 16
DK = 64
DF = 4096
NDC = D // 128
NFC = DF // 128
NT = T // 128
CSH = 32.0
EPS = 1e-5
WS = 4095
WA = 2560
NG = NH * NT
PIECES = [(0, 512), (512, 512), (1024, 512), (1536, 511)]


class Ctr:
    def __init__(self, h):
        self.h = h
        self.n = 0


def build_nc(dbg=False):
    nc = bass.Bass("TRN2", target_bir_lowering=False, detect_race_conditions=False)

    xT = nc.dram_tensor("xT", [D, T], F32, kind="ExternalInput").ap()
    wqT = nc.dram_tensor("wqT", [D, D], F32, kind="ExternalInput").ap()
    wkT = nc.dram_tensor("wkT", [D, D], F32, kind="ExternalInput").ap()
    wvT = nc.dram_tensor("wvT", [D, D], F32, kind="ExternalInput").ap()
    woT = nc.dram_tensor("woT", [D, D], F32, kind="ExternalInput").ap()
    w1T = nc.dram_tensor("w1T", [D, DF], F32, kind="ExternalInput").ap()
    w2T = nc.dram_tensor("w2T", [DF, D], F32, kind="ExternalInput").ap()
    bq_c = nc.dram_tensor("bq_c", [128, NDC], F32, kind="ExternalInput").ap()
    bk_c = nc.dram_tensor("bk_c", [128, NDC], F32, kind="ExternalInput").ap()
    bo_c = nc.dram_tensor("bo_c", [128, NDC], F32, kind="ExternalInput").ap()
    b1_c = nc.dram_tensor("b1_c", [128, NFC], F32, kind="ExternalInput").ap()
    bv_r = nc.dram_tensor("bv_r", [1, D], F32, kind="ExternalInput").ap()
    b2_r = nc.dram_tensor("b2_r", [1, D], F32, kind="ExternalInput").ap()
    relkT = nc.dram_tensor("relkT", [64, 2047], F32, kind="ExternalInput").ap()
    rvpad = nc.dram_tensor("rvpad", [4096, 64], F32, kind="ExternalInput").ap()
    yT = nc.dram_tensor("yT", [D, T], F32, kind="ExternalOutput").ap()
    dbg_t = {}
    if dbg:
        for name in ["dbg_ffn1", "dbg_ln2", "dbg_attn", "dbg_ho",
                     "dbg_x0", "dbg_xln1", "dbg_w1q"]:
            dbg_t[name] = nc.dram_tensor(name, [D, T], F32,
                                         kind="ExternalOutput").ap()
    qd = nc.dram_tensor("qd", [D, T], BF16, kind="Internal")
    kd = nc.dram_tensor("kd", [D, T], BF16, kind="Internal")
    vd = nc.dram_tensor("vd", [T, D], BF16, kind="Internal")
    w1q = nc.dram_tensor("w1q", [D, DF], BF16, kind="Internal")
    w2q = nc.dram_tensor("w2q", [DF, D], BF16, kind="Internal")
    spr = [nc.dram_tensor(f"spr{i}", [T * WS], FP16, kind="Internal")
           for i in range(2)]
    esc = [nc.dram_tensor(f"esc{i}", [T * WA], BF16, kind="Internal")
           for i in range(2)]
    rowscr = nc.dram_tensor("rowscr", [2 * T], F32, kind="Internal")
    hod = nc.dram_tensor("hod", [D, T], BF16, kind="Internal")
    dbg_rows = None
    if dbg:
        dbg_rows = nc.dram_tensor("dbg_rows", [2, T], F32,
                                  kind="ExternalOutput").ap()

    qdA = qd.ap()
    kdA = kd.ap()
    vdA = vd.ap()
    w1qA = w1q.ap()
    w2qA = w2q.ap()

    with ExitStack() as top:
        en = top.enter_context
        X = [en(nc.sbuf_tensor(f"x{i}", [128, T], F32)) for i in range(NDC)]
        tblA = en(nc.sbuf_tensor("tblA", [128, 2047], BF16))
        tblB = en(nc.sbuf_tensor("tblB", [128, 2047], BF16))
        RVsbE = en(nc.sbuf_tensor("rvsbe", [128, 16 * 64], BF16))
        RVsbO = en(nc.sbuf_tensor("rvsbo", [128, 16 * 64], BF16))
        ident = en(nc.sbuf_tensor("idnt", [128, 128], BF16))
        identF = en(nc.sbuf_tensor("idntf", [128, 128], F32))
        onec = en(nc.sbuf_tensor("onec", [128, 2], BF16))
        oner = en(nc.sbuf_tensor("oner", [1, 512], BF16))
        onerf = en(nc.sbuf_tensor("onerf", [1, 128], F32))
        bqs = en(nc.sbuf_tensor("bqs", [128, NDC], F32))
        bks = en(nc.sbuf_tensor("bks", [128, NDC], F32))
        bos = en(nc.sbuf_tensor("bos", [128, NDC], F32))
        b1s = en(nc.sbuf_tensor("b1s", [128, NFC], F32))
        bvr = en(nc.sbuf_tensor("bvr", [1, D], BF16))
        hb2r = en(nc.sbuf_tensor("hb2r", [1, D], BF16))
        thrs = en(nc.sbuf_tensor("thrs", [128, 4], F32))
        czs = en(nc.sbuf_tensor("czs", [128, 4], F32))
        SS = [Ctr(en(nc.semaphore(f"s{i}"))) for i in range(80)]

        def snap():
            return [s.n for s in SS]

        def snapd(*sems):
            return {id(s): s.n for s in sems}

        with ExitStack() as sc:
            e2 = sc.enter_context
            ztile = e2(nc.sbuf_tensor("ztile", [128, 1024], BF16))
            rtmp = e2(nc.sbuf_tensor("rtmp0", [128, 2048], F32))
            itmp = e2(nc.sbuf_tensor("itmp0", [128, 256], F32))
            rwb1 = e2(nc.sbuf_tensor("rwb1", [1, D], F32))
            rwb2 = e2(nc.sbuf_tensor("rwb2", [1, D], F32))
            s0, s1, s2, s3, s4 = SS[0], SS[1], SS[2], SS[3], SS[4]
            b = snap()
            with nc.Block() as blk:
                @blk.gpsimd
                def _(g):
                    for i in range(NDC):
                        g.dma_start(out=X[i][:, :],
                                    in_=xT[i * 128:(i + 1) * 128, :]
                                    ).then_inc(s0.h, 16)
                        s0.n += 16
                    for dst, srcd in [(bqs, bq_c), (bks, bk_c), (bos, bo_c),
                                      (b1s, b1_c)]:
                        g.dma_start(out=dst[:, :], in_=srcd).then_inc(s0.h, 16)
                        s0.n += 16
                    g.dma_start(out=rwb1[0:1, :], in_=bv_r).then_inc(s0.h, 16)
                    s0.n += 16
                    g.dma_start(out=rwb2[0:1, :], in_=b2_r).then_inc(s0.h, 16)
                    s0.n += 16
                    g.iota(itmp[:, 0:128], [[1, 128]], channel_multiplier=0,
                           allow_small_or_imprecise_dtypes=True)
                    g.iota(itmp[:, 128:256], [[0, 128]], channel_multiplier=1,
                           allow_small_or_imprecise_dtypes=True)
                    g.memset(ztile[:, :], 0)
                    g.memset(onec[:, :], 1.0)
                    g.memset(oner[:, :], 1.0)
                    g.memset(czs[:, 0:1], 0.0)
                    g.memset(czs[:, 1:2], -CSH)
                    g.memset(czs[:, 2:3], EPS)
                    g.memset(onerf[:, :], 1.0).then_inc(s1.h, 1)
                    s1.n += 1
                    g.wait_ge(s0.h, s0.n)

                @blk.sync
                def _(s):
                    s.dma_start(out=rtmp[0:64, 0:2047],
                                in_=relkT).then_inc(s2.h, 16)
                    s2.n += 16
                    s.dma_start(out=rtmp[64:128, 0:2047],
                                in_=relkT).then_inc(s2.h, 16)
                    s2.n += 16

                @blk.vector
                def _(v):
                    v.wait_ge(s1.h, b[1] + 1)
                    v.tensor_tensor(out=ident[:, :], in0=itmp[:, 0:128],
                                    in1=itmp[:, 128:256], op=ALU.is_equal)
                    v.tensor_tensor(out=identF[:, :], in0=itmp[:, 0:128],
                                    in1=itmp[:, 128:256], op=ALU.is_equal)
                    v.wait_ge(s0.h, b[0] + 16 * 14)
                    v.tensor_scalar(out=bks[:, :], in0=bks[:, :], scalar1=0.125,
                                    scalar2=None, op0=ALU.mult)
                    v.tensor_copy(out=bvr[:, :], in_=rwb1[0:1, :])
                    v.tensor_scalar(out=hb2r[:, :], in0=rwb2[0:1, :],
                                    scalar1=0.5, scalar2=None, op0=ALU.mult)
                    v.wait_ge(s2.h, b[2] + 32)
                    v.scalar_tensor_tensor(
                        out=tblA[:, :], in0=rtmp[:, 0:2047], scalar=1.0,
                        in1=rtmp[:, 2046:2047].to_broadcast([128, 2047]),
                        op0=ALU.mult, op1=ALU.subtract)
                    v.scalar_tensor_tensor(
                        out=tblB[:, :], in0=rtmp[:, 0:2047], scalar=1.0,
                        in1=rtmp[:, 0:1].to_broadcast([128, 2047]),
                        op0=ALU.mult, op1=ALU.subtract).then_inc(s3.h, 1)
                    s3.n += 1
                    v.wait_ge(s2.h, b[2] + 64)
                    v.tensor_copy(out=RVsbE[:, :], in_=rtmp[:, 0:1024])
                    v.tensor_copy(out=RVsbO[:, :], in_=rtmp[:, 1024:2048]
                                  ).then_inc(s3.h, 1)
                    s3.n += 1

                @blk.sync
                def _(s):
                    s.wait_ge(s3.h, b[3] + 1)
                    s.dma_start(
                        out=rtmp[:, 0:1024].rearrange("p (c k) -> p c k", k=64),
                        in_=bass.AP(rvpad.tensor, 0,
                                    [[128, 128], [128 * 128, 16], [1, 64]])
                    ).then_inc(s2.h, 16)
                    s2.n += 16
                    s.dma_start(
                        out=rtmp[:, 1024:2048].rearrange("p (c k) -> p c k", k=64),
                        in_=bass.AP(rvpad.tensor, 64,
                                    [[128, 128], [128 * 128, 16], [1, 64]])
                    ).then_inc(s2.h, 16)
                    s2.n += 16

                @blk.scalar
                def _(s):
                    pass
                    s.wait_ge(s1.h, b[1] + 1)
                    for bi in range(2):
                        for rt in range(NT):
                            for off_, wid_, tsr in [
                                    (rt * 128 * WS, 1024, spr[bi]),
                                    (rt * 128 * WS + 3071, 1024, spr[bi]),
                                    (rt * 128 * WA, 256, esc[bi]),
                                    (rt * 128 * WA + 2304, 256, esc[bi])]:
                                zsrc = (ztile[:, 0:wid_].bitcast(FP16)[:, 0:wid_]
                                        if wid_ == 1024 else ztile[:, 0:wid_])
                                s.dma_start(out=bass.AP(tsr, off_,
                                                        [[WS if wid_ == 1024 else WA, 128],
                                                         [1, wid_]]),
                                            in_=zsrc).then_inc(s4.h, 16)
                                s4.n += 16
                    s.wait_ge(s4.h, s4.n)
                    s.wait_ge(s3.h, b[3] + 2)

        with ExitStack() as sc:
            e2 = sc.enter_context
            wtmp = [e2(nc.sbuf_tensor(f"wt{i}", [128, DF], F32)) for i in range(2)]
            wabs = e2(nc.sbuf_tensor("wabs", [128, DF], FP16))
            wqo = [e2(nc.sbuf_tensor(f"wqo{i}", [128, DF], BF16)) for i in range(2)]
            qrows = e2(nc.sbuf_tensor("qrows", [1, 600], F32))
            psQ = e2(nc.psum_tensor("psq", [1, 512], F32))
            psB = e2(nc.psum_tensor("psb", [128, 2], F32))

            def quant_pass(blk, which):
                src, dstA = (w1T, w1qA) if which == 0 else (w2T, w2qA)
                nch = NDC if which == 0 else NFC
                wid = DF if which == 0 else D
                npc = wid // 512
                ntot = float(D * DF)
                thr_col = thrs[:, 2 * which:2 * which + 1]
                nthr_col = thrs[:, 2 * which + 1:2 * which + 2]
                sLD, sAB, sMM, sRW, sBC, sTH, sQO, sSO = SS[5:13]
                sLDb = [sLD, SS[67]]
                sSOb = [sSO, SS[68]]
                b = snapd(sLD, sAB, sMM, sRW, sBC, sTH, sQO, sSO,
                          SS[67], SS[68])

                @blk.sync
                def _(s):
                    for c in range(nch):
                        if c >= 2:
                            s.wait_ge(sAB.h, b[id(sAB)] + c - 1)
                        s.dma_start(out=wtmp[c % 2][:, 0:wid],
                                    in_=src[c * 128:(c + 1) * 128, :]
                                    ).then_inc(sLDb[c % 2].h, 16)
                        sLDb[c % 2].n += 16
                    for c in range(nch):
                        s.wait_ge(sTH.h, b[id(sTH)] + 1)
                        if c >= 2:
                            s.wait_ge(sQO.h, b[id(sQO)] + c - 1)
                        s.dma_start(out=wtmp[c % 2][:, 0:wid],
                                    in_=src[c * 128:(c + 1) * 128, :]
                                    ).then_inc(sLDb[c % 2].h, 16)
                        sLDb[c % 2].n += 16

                @blk.scalar
                def _(s):
                    for c in range(nch):
                        s.wait_ge(sLDb[c % 2].h,
                                  b[id(sLDb[c % 2])] + 16 * (c // 2 + 1))
                        if c >= 1:
                            s.wait_ge(sMM.h, b[id(sMM)] + c * npc)
                        s.activation(out=wabs[:, 0:wid], in_=wtmp[c % 2][:, 0:wid],
                                     func=AF.Abs, bias=czs[:, 0:1], scale=1.0)
                        s.activation(out=czs[:, 3:4], in_=czs[:, 0:1],
                                     func=AF.Copy, bias=0.0, scale=1.0
                                     ).then_inc(sAB.h, 1)
                        sAB.n += 1

                @blk.tensor
                def _(tn):
                    for c in range(nch):
                        tn.wait_ge(sAB.h, b[id(sAB)] + c + 1)
                        for p in range(npc):
                            tn.matmul(psQ[0:1, :],
                                      onec[:, 0:1], wabs[:, p * 512:(p + 1) * 512],
                                      start=(c == 0 and p == 0),
                                      stop=(c == nch - 1 and p == npc - 1)
                                      ).then_inc(sMM.h, 1)
                            sMM.n += 1
                    tn.wait_ge(sRW.h, b[id(sRW)] + 1)
                    tn.matmul(psB[:, 0:1], onerf[0:1, :], qrows[0:1, 577:578],
                              start=True, stop=True)
                    tn.matmul(psB[:, 1:2], onerf[0:1, :], qrows[0:1, 578:579],
                              start=True, stop=True).then_inc(sBC.h, 1)
                    sBC.n += 1

                @blk.vector
                def _(v):
                    def gap():
                        v.tensor_copy(out=czs[:, 3:4], in_=czs[:, 0:1])
                        v.tensor_copy(out=czs[:, 3:4], in_=czs[:, 0:1])
                    v.wait_ge(sMM.h, b[id(sMM)] + nch * npc)
                    v.reduce_sum(out=qrows[0:1, 0:64],
                                 in_=psQ[0:1, :].rearrange(
                                     "p (a c) -> p a c", c=8), axis=AX.X)
                    gap()
                    v.reduce_sum(out=qrows[0:1, 512:520],
                                 in_=qrows[0:1, 0:64].rearrange(
                                     "p (a c) -> p a c", c=8), axis=AX.X)
                    gap()
                    v.reduce_sum(out=qrows[0:1, 576:577],
                                 in_=qrows[0:1, 512:520], axis=AX.X)
                    gap()
                    v.tensor_scalar(out=qrows[0:1, 577:578],
                                    in0=qrows[0:1, 576:577],
                                    scalar1=0.5 / ntot, scalar2=0.5 * EPS,
                                    op0=ALU.mult, op1=ALU.add)
                    v.tensor_scalar(out=qrows[0:1, 578:579],
                                    in0=qrows[0:1, 576:577],
                                    scalar1=-0.5 / ntot, scalar2=-0.5 * EPS,
                                    op0=ALU.mult, op1=ALU.add)
                    v.tensor_copy(out=czs[:, 3:4],
                                  in_=czs[:, 0:1]).then_inc(sRW.h, 1)
                    sRW.n += 1
                    v.wait_ge(sBC.h, b[id(sBC)] + 1)
                    v.tensor_copy(out=thr_col, in_=psB[:, 0:1])
                    v.tensor_copy(out=nthr_col, in_=psB[:, 1:2]).then_inc(sTH.h, 1)
                    sTH.n += 1
                    sca = 0.5 if which == 1 else 1.0
                    for c in range(nch):
                        v.wait_ge(sLDb[c % 2].h,
                                  b[id(sLDb[c % 2])] + 16 * ((nch + c) // 2 + 1))
                        if c >= 2:
                            v.wait_ge(sSOb[c % 2].h,
                                      b[id(sSOb[c % 2])] + 16 * ((c - 2) // 2 + 1))
                        v.tensor_scalar(out=wabs[:, 0:wid],
                                        in0=wtmp[c % 2][:, 0:wid],
                                        scalar1=thr_col, scalar2=sca,
                                        op0=ALU.is_gt, op1=ALU.mult)
                        v.tensor_scalar(out=wqo[c % 2][:, 0:wid],
                                        in0=wtmp[c % 2][:, 0:wid],
                                        scalar1=nthr_col, scalar2=sca,
                                        op0=ALU.is_lt, op1=ALU.mult)
                        v.tensor_tensor(out=wqo[c % 2][:, 0:wid],
                                        in0=wabs[:, 0:wid],
                                        in1=wqo[c % 2][:, 0:wid],
                                        op=ALU.subtract)
                        v.tensor_copy(out=czs[:, 3:4],
                                      in_=czs[:, 0:1]).then_inc(sQO.h, 1)
                        sQO.n += 1

                @blk.gpsimd
                def _(g):
                    for c in range(nch):
                        g.wait_ge(sQO.h, b[id(sQO)] + c + 1)
                        g.dma_start(out=dstA[c * 128:(c + 1) * 128, :],
                                    in_=wqo[c % 2][:, 0:wid]
                                    ).then_inc(sSOb[c % 2].h, 16)
                        sSOb[c % 2].n += 16
                    g.wait_ge(sSOb[0].h, sSOb[0].n)
                    g.wait_ge(sSOb[1].h, sSOb[1].n)

            with nc.Block() as blk:
                quant_pass(blk, 0)
            with nc.Block() as blk:
                quant_pass(blk, 1)
            if dbg:
                with nc.Block() as blk:
                    @blk.gpsimd
                    def _(g):
                        sst = SS[51]
                        for i in range(NDC):
                            g.dma_start(out=dbg_t["dbg_x0"][i * 128:(i + 1) * 128, :],
                                        in_=X[i][:, :]).then_inc(sst.h, 16)
                            sst.n += 16
                        g.dma_start(out=dbg_t["dbg_w1q"][0:128, :].bitcast(BF16)[:, 0:T],
                                    in_=w1qA[0:128, 0:T]).then_inc(sst.h, 16)
                        sst.n += 16
                        g.wait_ge(sst.h, sst.n)

        def ln_phase(blk, Xln, psM, psS, rowsA, rowsB, vrow, bcA, bcB,
                     xsq, tmpf):
            sCS, sMM, sVAR, sRSD, sMR, sRST, sBCL, sAPP = SS[13:21]
            b = snapd(sCS, sMM, sVAR, sRSD, sMR, sRST, sBCL, sAPP)

            @blk.scalar
            def _(s):
                for dc in range(NDC):
                    if dc >= 2:
                        s.wait_ge(sMM.h, b[id(sMM)] + 4 * (dc - 1))
                    s.copy(out=Xln[dc][:, :], in_=X[dc][:, :])
                    s.activation(out=xsq[dc % 2][:, :], in_=X[dc][:, :],
                                 func=AF.Square, bias=czs[:, 0:1], scale=1.0)
                    s.activation(out=czs[:, 3:4], in_=czs[:, 0:1],
                                 func=AF.Copy, bias=0.0, scale=1.0
                                 ).then_inc(sCS.h, 1)
                    sCS.n += 1
                s.wait_ge(sVAR.h, b[id(sVAR)] + 1)
                s.activation(out=rowsB[0:1, :], in_=vrow[0:1, :], func=AF.Ln,
                             bias=czs[0:1, 2:3], scale=1.0)
                s.activation(out=rowsB[0:1, :], in_=rowsB[0:1, :], func=AF.Exp,
                             bias=czs[0:1, 0:1], scale=-0.5)
                s.activation(out=czs[:, 3:4], in_=czs[:, 0:1],
                             func=AF.Copy, bias=0.0, scale=1.0
                             ).then_inc(sRSD.h, 1)
                sRSD.n += 1
                s.dma_start(out=bass.AP(rowscr, 0, [[1, 1], [1, T]]),
                            in_=rowsB[0:1, :]).then_inc(sRST.h, 16)
                sRST.n += 16
                s.dma_start(out=bass.AP(rowscr, T, [[1, 1], [1, T]]),
                            in_=vrow[0:1, :]).then_inc(sRST.h, 16)
                sRST.n += 16
                s.wait_ge(sRST.h, sRST.n)

            @blk.tensor
            def _(tn):
                for dc in range(NDC):
                    tn.wait_ge(sCS.h, b[id(sCS)] + dc + 1)
                    for p in range(4):
                        tn.matmul(psM[p][0:1, :], onec[:, 0:1],
                                  Xln[dc][:, p * 512:(p + 1) * 512],
                                  start=(dc == 0), stop=(dc == NDC - 1))
                        tn.matmul(psS[p][0:1, :], onec[:, 0:1],
                                  xsq[dc % 2][:, p * 512:(p + 1) * 512],
                                  start=(dc == 0), stop=(dc == NDC - 1)
                                  ).then_inc(sMM.h, 1)
                        sMM.n += 1
                tn.wait_ge(sMR.h, b[id(sMR)] + 1)
                for p in range(4):
                    tn.matmul(psM[p][:, :], onerf[0:1, :],
                              rowsB[0:1, p * 512:(p + 1) * 512],
                              start=True, stop=True)
                    tn.matmul(psS[p][:, :], onerf[0:1, :],
                              rowsA[0:1, p * 512:(p + 1) * 512],
                              start=True, stop=True).then_inc(sBCL.h, 1)
                    sBCL.n += 1

            @blk.vector
            def _(v):
                def gap():
                    v.tensor_copy(out=czs[:, 3:4], in_=czs[:, 0:1])
                    v.tensor_copy(out=czs[:, 3:4], in_=czs[:, 0:1])
                v.wait_ge(sMM.h, b[id(sMM)] + 4 * NDC)
                for p in range(4):
                    v.tensor_scalar(out=rowsA[0:1, p * 512:(p + 1) * 512],
                                    in0=psM[p][0:1, :], scalar1=1.0 / D,
                                    scalar2=None, op0=ALU.mult)
                gap()
                v.tensor_tensor(out=rowsB[0:1, :], in0=rowsA[0:1, :],
                                in1=rowsA[0:1, :], op=ALU.mult)
                gap()
                for p in range(4):
                    inst = v.scalar_tensor_tensor(
                        out=vrow[0:1, p * 512:(p + 1) * 512],
                        in0=psS[p][0:1, :], scalar=1.0 / D,
                        in1=rowsB[0:1, p * 512:(p + 1) * 512],
                        op0=ALU.mult, op1=ALU.subtract)
                gap()
                v.tensor_copy(out=czs[:, 3:4],
                              in_=czs[:, 0:1]).then_inc(sVAR.h, 1)
                sVAR.n += 1
                inst = inst
                v.wait_ge(sRSD.h, b[id(sRSD)] + 1)
                v.tensor_tensor(out=rowsA[0:1, :], in0=rowsA[0:1, :],
                                in1=rowsB[0:1, :], op=ALU.mult)
                gap()
                nc.vector.sem_inc(sMR.h, 1)
                sMR.n += 1
                v.wait_ge(sBCL.h, b[id(sBCL)] + 4)
                for p in range(4):
                    v.tensor_copy(out=bcA[:, p * 512:(p + 1) * 512],
                                  in_=psM[p][:, :])
                    v.tensor_copy(out=bcB[:, p * 512:(p + 1) * 512],
                                  in_=psS[p][:, :])
                for dc in range(NDC):
                    v.tensor_tensor(out=tmpf[:, :], in0=X[dc][:, :],
                                    in1=bcA[:, :], op=ALU.mult)
                    v.tensor_tensor(out=Xln[dc][:, :], in0=tmpf[:, :],
                                    in1=bcB[:, :], op=ALU.subtract
                                    ).then_inc(sAPP.h, 1)
                    sAPP.n += 1

            @blk.gpsimd
            def _(g):
                g.wait_ge(sAPP.h, b[id(sAPP)] + NDC)

        def ffn_phase(blk, Xln, H, w1s, w2s, psF, evt):
            sL1, sL2, sM1, sRL, sM2, sXA = SS[21:27]
            sL1b = [sL1, SS[65]]
            sL2b = [sL2, SS[66]]
            b = snapd(sL1, sL2, sM1, sRL, sM2, sXA, SS[65], SS[66])

            @blk.sync
            def _(s):
                for th in range(2):
                    for fc in range(NFC):
                        i = th * NFC + fc
                        if i >= 2:
                            s.wait_ge(sM1.h, b[id(sM1)] + 2 * (i - 1))
                        s.dma_start(out=w1s[i % 2][:, :, :],
                                    in_=bass.AP(w1q, 128 * fc,
                                                [[DF, 128], [128 * DF, 8],
                                                 [1, 128]])
                                    ).then_inc(sL1b[i % 2].h, 16)
                        sL1b[i % 2].n += 16
                    for dc2 in range(NDC):
                        i = th * NDC + dc2
                        if i >= 2:
                            s.wait_ge(sM2.h, b[id(sM2)] + 2 * (i - 1))
                        s.dma_start(out=w2s[i % 2][:, :, :],
                                    in_=bass.AP(w2q, 128 * dc2,
                                                [[D, 128], [128 * D, 32],
                                                 [1, 128]])
                                    ).then_inc(sL2b[i % 2].h, 16)
                        sL2b[i % 2].n += 16

            @blk.tensor
            def _(tn):
                for th in range(2):
                    for fc in range(NFC):
                        i = th * NFC + fc
                        tn.wait_ge(sL1b[i % 2].h,
                                   b[id(sL1b[i % 2])] + 16 * (i // 2 + 1))
                        for p in range(2):
                            if i >= 1:
                                tn.wait_ge(sRL.h, b[id(sRL)] + 2 * i - 1 + p)
                            for dc in range(NDC):
                                mi = tn.matmul(
                                    psF[p][:, :], w1s[i % 2][:, dc, :],
                                    Xln[dc][:, th * 1024 + p * 512:
                                            th * 1024 + (p + 1) * 512],
                                    start=(dc == 0), stop=(dc == NDC - 1))
                            mi.then_inc(sM1.h, 1)
                            sM1.n += 1
                    for dc2 in range(NDC):
                        i = th * NDC + dc2
                        tn.wait_ge(sL2b[i % 2].h,
                                   b[id(sL2b[i % 2])] + 16 * (i // 2 + 1))
                        tn.wait_ge(sRL.h, b[id(sRL)] + 64 * (th + 1))
                        for p in range(2):
                            if i >= 1:
                                tn.wait_ge(sXA.h, b[id(sXA)] + 2 * i - 1 + p)
                            for fc in range(NFC):
                                tn.matmul(psF[2 + p][:, :], w2s[i % 2][:, fc, :],
                                          H[fc][:, p * 512:(p + 1) * 512],
                                          start=(fc == 0), stop=False)
                            tn.matmul(psF[2 + p][:, :],
                                      hb2r[0:1, dc2 * 128:(dc2 + 1) * 128],
                                      oner[0:1, :], start=False, stop=True
                                      ).then_inc(sM2.h, 1)
                            sM2.n += 1

            @blk.scalar
            def _(s):
                prevR = [None]
                for th in range(2):
                    if th == 1:
                        s.wait_ge(sM2.h, b[id(sM2)] + 16)
                    for fc in range(NFC):
                        i = th * NFC + fc
                        for p in range(2):
                            s.wait_ge(sM1.h, b[id(sM1)] + 2 * i + p + 1)
                            ri = s.activation(out=H[fc][:, p * 512:(p + 1) * 512],
                                              in_=psF[p][:, :], func=AF.Relu,
                                              bias=b1s[:, fc:fc + 1], scale=1.0)
                            if prevR[0] is not None:
                                ri.then_inc(sRL.h, 1)
                                sRL.n += 1
                            prevR[0] = ri
                    s.activation(out=czs[:, 3:4], in_=czs[:, 0:1],
                                 func=AF.Copy, bias=0.0, scale=1.0
                                 ).then_inc(sRL.h, 1)
                    sRL.n += 1
                    prevR[0] = None

            @blk.vector
            def _(v):
                for th in range(2):
                    for dc2 in range(NDC):
                        i = th * NDC + dc2
                        for p in range(2):
                            v.wait_ge(sM2.h, b[id(sM2)] + 2 * i + p + 1)
                            xsl = X[dc2][:, th * 1024 + p * 512:
                                         th * 1024 + (p + 1) * 512]
                            v.tensor_tensor(out=xsl, in0=psF[2 + p][:, :],
                                            in1=xsl, op=ALU.add).then_inc(sXA.h, 1)
                            sXA.n += 1

            @blk.gpsimd
            def _(g):
                g.wait_ge(sXA.h, b[id(sXA)] + 32)
                evt(g)

        with ExitStack() as sc:
            e2 = sc.enter_context
            Xln = [e2(nc.sbuf_tensor(f"xln{i}", [128, T], BF16))
                   for i in range(NDC)]
            psM = [e2(nc.psum_tensor(f"psm{i}", [128, 512], F32))
                   for i in range(4)]
            psS = [e2(nc.psum_tensor(f"pss{i}", [128, 512], F32))
                   for i in range(4)]

            lnctr = [0]

            def run_ln():
                lnctr[0] += 1
                u = lnctr[0]
                with ExitStack() as scl:
                    e3 = scl.enter_context
                    rowsA = e3(nc.sbuf_tensor(f"rowsA_{u}", [1, T], F32))
                    rowsB = e3(nc.sbuf_tensor(f"rowsB_{u}", [1, T], F32))
                    bcA = e3(nc.sbuf_tensor(f"bcA_{u}", [128, T], F32))
                    bcB = e3(nc.sbuf_tensor(f"bcB_{u}", [128, T], F32))
                    xsq = [e3(nc.sbuf_tensor(f"xsq_{u}_{i}", [128, T], BF16))
                           for i in range(2)]
                    vrow = e3(nc.sbuf_tensor(f"vrow_{u}", [1, T], F32))
                    tmpf = e3(nc.sbuf_tensor(f"tmpfl_{u}", [128, T], F32))
                    with nc.Block() as blk:
                        ln_phase(blk, Xln, psM, psS, rowsA, rowsB, vrow,
                                 bcA, bcB, xsq, tmpf)

            run_ln()
            if dbg:
                with nc.Block() as blk:
                    @blk.gpsimd
                    def _(g):
                        sst = SS[51]
                        for i in range(NDC):
                            g.dma_start(out=dbg_t["dbg_xln1"][i * 128:(i + 1) * 128, :]
                                        .bitcast(BF16)[:, 0:T],
                                        in_=Xln[i][:, :]).then_inc(sst.h, 16)
                            sst.n += 16
                        g.dma_start(out=dbg_rows,
                                    in_=bass.AP(rowscr, 0, [[T, 2], [1, T]])
                                    ).then_inc(sst.h, 16)
                        sst.n += 16
                        g.wait_ge(sst.h, sst.n)
            with ExitStack() as scf:
                e3 = scf.enter_context
                H = [e3(nc.sbuf_tensor(f"ha_{i}", [128, 1024], BF16))
                     for i in range(NFC)]
                w1s = [e3(nc.sbuf_tensor(f"w1s{i}", [128, 8, 128], BF16))
                       for i in range(2)]
                w2s = [e3(nc.sbuf_tensor(f"w2s{i}", [128, 32, 128], BF16))
                       for i in range(2)]

                def evt1(g):
                    if dbg:
                        sst = SS[51]
                        for i in range(NDC):
                            g.dma_start(out=dbg_t["dbg_ffn1"][i * 128:(i + 1) * 128, :],
                                        in_=X[i][:, :]).then_inc(sst.h, 16)
                            sst.n += 16
                        g.wait_ge(sst.h, sst.n)
                with nc.Block() as blk:
                    ffn_phase(blk, Xln, H, w1s, w2s, psM, evt1)

            run_ln()
            if dbg:
                with nc.Block() as blk:
                    @blk.gpsimd
                    def _(g):
                        sst = SS[51]
                        for i in range(NDC):
                            g.dma_start(out=dbg_t["dbg_ln2"][i * 128:(i + 1) * 128, :]
                                        .bitcast(BF16)[:, 0:T],
                                        in_=Xln[i][:, :]).then_inc(sst.h, 16)
                            sst.n += 16
                        g.wait_ge(sst.h, sst.n)

            with ExitStack() as scp:
                e3 = scp.enter_context
                wbf = [e3(nc.sbuf_tensor(f"wbf{i}", [128, D], BF16))
                       for i in range(NDC)]
                wtm = [e3(nc.sbuf_tensor(f"wtm{i}", [128, D], F32))
                       for i in range(2)]
                otile = [e3(nc.sbuf_tensor(f"ot{i}", [128, 512], BF16))
                         for i in range(2)]
                psP = [psM[0], psM[1]]
                sWL, sWC, sPM, sPE, sPS = SS[27:32]
                sWCF = SS[58]
                sWLb = [sWL, SS[69]]
                sPSb = [sPS, SS[70]]
                b = snapd(sWL, sWC, sPM, sPE, sPS, sWCF, SS[69], SS[70])
                with nc.Block() as blk:
                    @blk.sync
                    def _(s):
                        for wi, wsrc in enumerate([wqT, wkT, wvT]):
                            for c in range(NDC):
                                i = wi * NDC + c
                                if i >= 2:
                                    s.wait_ge(sWC.h, b[id(sWC)] + i - 1)
                                s.dma_start(out=wtm[c % 2][:, :],
                                            in_=wsrc[c * 128:(c + 1) * 128, :]
                                            ).then_inc(sWLb[i % 2].h, 16)
                                sWLb[i % 2].n += 16

                    @blk.vector
                    def _(v):
                        for wi in range(3):
                            for c in range(NDC):
                                i = wi * NDC + c
                                v.wait_ge(sWLb[i % 2].h,
                                           b[id(sWLb[i % 2])] + 16 * (i // 2 + 1))
                                if wi > 0:
                                    v.wait_ge(sPM.h, b[id(sPM)] + wi * 32)
                                v.tensor_copy(out=wbf[c][:, :],
                                              in_=wtm[c % 2][:, :]
                                              ).then_inc(sWC.h, 1)
                                sWC.n += 1
                            v.tensor_copy(out=czs[:, 3:4],
                                          in_=czs[:, 0:1]).then_inc(sWCF.h, 1)
                            sWCF.n += 1
                            for j in range(32):
                                jj = wi * 32 + j
                                v.wait_ge(sPM.h, b[id(sPM)] + jj + 1)
                                if jj >= 2:
                                    v.wait_ge(sPSb[jj % 2].h,
                                              b[id(sPSb[jj % 2])]
                                              + 16 * ((jj - 2) // 2 + 1))
                                if wi == 0:
                                    oc = j // 4
                                    ei = v.tensor_scalar(out=otile[j % 2][:, :],
                                                         in0=psP[j % 2][:, :],
                                                         scalar1=bqs[:, oc:oc + 1],
                                                         scalar2=None, op0=ALU.add)
                                elif wi == 1:
                                    oc = j // 4
                                    ei = v.tensor_scalar(out=otile[j % 2][:, :],
                                                         in0=psP[j % 2][:, :],
                                                         scalar1=0.125,
                                                         scalar2=bks[:, oc:oc + 1],
                                                         op0=ALU.mult, op1=ALU.add)
                                else:
                                    ei = v.tensor_copy(out=otile[j % 2][:, :],
                                                       in_=psP[j % 2][:, :])
                                v.tensor_copy(out=czs[:, 3:4],
                                              in_=czs[:, 0:1]).then_inc(sPE.h, 1)
                                sPE.n += 1
                                ei = ei

                    @blk.tensor
                    def _(tn):
                        tn.wait_ge(sWCF.h, b[id(sWCF)] + 1)
                        for wi in range(3):
                            if wi > 0:
                                tn.wait_ge(sWCF.h, b[id(sWCF)] + wi + 1)
                            for j in range(32):
                                jj = wi * 32 + j
                                if j >= 2:
                                    tn.wait_ge(sPE.h, b[id(sPE)] + jj - 1)
                                if wi < 2:
                                    oc, p = divmod(j, 4)
                                    for ic in range(NDC):
                                        mi = tn.matmul(
                                            psP[j % 2][:, :],
                                            wbf[ic][:, oc * 128:(oc + 1) * 128],
                                            Xln[ic][:, p * 512:(p + 1) * 512],
                                            start=(ic == 0), stop=(ic == NDC - 1))
                                else:
                                    tc, p = divmod(j, 2)
                                    for dc in range(NDC):
                                        tn.matmul(
                                            psP[j % 2][:, :],
                                            Xln[dc][:, tc * 128:(tc + 1) * 128],
                                            wbf[dc][:, p * 512:(p + 1) * 512],
                                            start=(dc == 0), stop=False)
                                    mi = tn.matmul(psP[j % 2][:, :],
                                                   oner[0:1, 0:128],
                                                   bvr[0:1, p * 512:(p + 1) * 512],
                                                   start=False, stop=True)
                                mi.then_inc(sPM.h, 1)
                                sPM.n += 1

                    @blk.scalar
                    def _(s):
                        for wi in range(3):
                            for j in range(32):
                                jj = wi * 32 + j
                                s.wait_ge(sPE.h, b[id(sPE)] + jj + 1)
                                if wi == 0:
                                    oc, p = divmod(j, 4)
                                    dst = qdA[oc * 128:(oc + 1) * 128,
                                              p * 512:(p + 1) * 512]
                                elif wi == 1:
                                    oc, p = divmod(j, 4)
                                    dst = kdA[oc * 128:(oc + 1) * 128,
                                              p * 512:(p + 1) * 512]
                                else:
                                    tc, p = divmod(j, 2)
                                    dst = vdA[tc * 128:(tc + 1) * 128,
                                              p * 512:(p + 1) * 512]
                                s.dma_start(out=dst, in_=otile[j % 2][:, :]
                                            ).then_inc(sPSb[jj % 2].h, 16)
                                sPSb[jj % 2].n += 16
                        s.wait_ge(sPSb[0].h, sPSb[0].n)
                        s.wait_ge(sPSb[1].h, sPSb[1].n)

        with ExitStack() as sc:
            e2 = sc.enter_context
            with ExitStack() as sca:
                e3 = sca.enter_context
                Qp = [e3(nc.sbuf_tensor(f"qp{i}", [128, T], BF16))
                      for i in range(2)]
                Kp = [e3(nc.sbuf_tensor(f"kp{i}", [128, T], BF16))
                      for i in range(2)]
                Vpe = [e3(nc.sbuf_tensor(f"vpe{i}", [128, 8 * 128], BF16))
                       for i in range(2)]
                Vpo = [e3(nc.sbuf_tensor(f"vpo{i}", [128, 8 * 128], BF16))
                       for i in range(2)]
                scoA = [e3(nc.sbuf_tensor(f"sco{i}", [128, T], F32))
                        for i in range(2)]
                bndE = [e3(nc.sbuf_tensor(f"bnd{i}", [128, T], BF16))
                        for i in range(2)]
                bndF = [e3(nc.sbuf_tensor(f"bnf{i}", [128, T], FP16))
                        for i in range(2)]
                Ssb = [e3(nc.sbuf_tensor(f"ssb{i}", [128, 2047], FP16))
                       for i in range(2)]
                ATsP = e3(nc.sbuf_tensor("atsp", [128, 2048], F32))
                DTsP = e3(nc.sbuf_tensor("dtsp", [128, 2304], F32))
                Adg = [e3(nc.sbuf_tensor(f"adg{i}", [128, 2304], BF16))
                       for i in range(2)]
                HOp = [e3(nc.sbuf_tensor(f"hop{i}", [128, T], BF16))
                       for i in range(2)]
                rsc = e3(nc.sbuf_tensor("rsc", [128, 4], F32))
                PS = [e3(nc.psum_tensor(f"aps{i}", [128, 512], F32))
                      for i in range(2)]
                psT = [e3(nc.psum_tensor(f"apt{i}", [128, 512], F32))
                       for i in range(2)]
                psO = [e3(nc.psum_tensor(f"apo{i}", [128, 128], F32))
                       for i in range(2)]
                (sQPL, sSR, sSRE, sSST, sBRD, sQK, sBAD, sEXP, sNRM, sAST,
                 sDRD, sTRG, sGEV, sAV, sOEV) = SS[32:47]
                sHOD = SS[47]
                sSSTb = [sSST, SS[59]]
                sBRDb = [sBRD, SS[60]]
                sASTb = [sAST, SS[61]]
                sDRDb = [sDRD, SS[62]]
                sQPLb = [sQPL, SS[63]]
                sHODb = [sHOD, SS[64]]
                b = snapd(sQPL, sSR, sSRE, sSST, sBRD, sQK, sBAD, sEXP,
                          sNRM, sAST, sDRD, sTRG, sGEV, sAV, sOEV, sHOD,
                          SS[59], SS[60], SS[61], SS[62], SS[63], SS[64])

                def consumed_wait(eng, q):
                    prev = q - 2
                    if prev < 0:
                        return
                    gp, jp = divmod(prev, 8)
                    if jp < 4:
                        eng.wait_ge(sSRE.h, b[id(sSRE)] + 4 * gp + jp + 1)
                    else:
                        eng.wait_ge(sBAD.h, b[id(sBAD)] + 4 * gp + (jp - 4) + 1)

                def aw0_of(t):
                    w0 = 128 * (15 - t)
                    return w0 - 128 * ((15 - t) % 2)

                with nc.Block() as blk:
                    @blk.sync
                    def _(s):
                        for g in range(NG):
                            h, t = divmod(g, NT)
                            if t == 0 and h % 2 == 0:
                                hp = h // 2
                                if hp >= 1:
                                    s.wait_ge(sOEV.h, b[id(sOEV)] + 32 * hp)
                                    sh = sHODb[(hp - 1) % 2]
                                    s.dma_start(
                                        out=bass.AP(hod, (hp - 1) * 128 * T,
                                                    [[T, 128], [1, T]]),
                                        in_=HOp[(hp - 1) % 2][:, :]
                                    ).then_inc(sh.h, 16)
                                    sh.n += 16
                                if h >= 4:
                                    s.wait_ge(sQK.h, b[id(sQK)] + 64 * (h - 2))
                                    s.wait_ge(sAV.h, b[id(sAV)] + 16 * (h - 2))
                                sq = sQPLb[hp % 2]
                                s.dma_start(out=Qp[hp % 2][:, :],
                                            in_=qdA[h * 64:(h + 2) * 64, :]
                                            ).then_inc(sq.h, 16)
                                s.dma_start(out=Kp[hp % 2][:, :],
                                            in_=kdA[h * 64:(h + 2) * 64, :]
                                            ).then_inc(sq.h, 16)
                                for so, vbuf in [(0, Vpe), (1, Vpo)]:
                                    s.dma_start(
                                        out=vbuf[hp % 2][:, :].rearrange(
                                            "p (c k) -> p c k", k=128),
                                        in_=bass.AP(vd, so * D + h * 64,
                                                    [[2 * D, 128],
                                                     [256 * D, 8], [1, 128]])
                                    ).then_inc(sq.h, 16)
                                sq.n += 64
                            s.wait_ge(sSSTb[g % 2].h,
                                      b[id(sSSTb[g % 2])] + 16 * (g // 2 + 1))
                            if g >= 2:
                                s.wait_ge(sNRM.h, b[id(sNRM)] + g - 1)
                            s.dma_start(out=bndF[g % 2][:, :],
                                        in_=bass.AP(spr[h % 2],
                                                    t * 128 * (WS - 1) + 2047,
                                                    [[WS - 1, 128], [1, 2048]])
                                        ).then_inc(sBRDb[g % 2].h, 16)
                            sBRDb[g % 2].n += 16
                            s.wait_ge(sASTb[g % 2].h,
                                      b[id(sASTb[g % 2])] + 16 * (g // 2 + 1))
                            if g >= 2:
                                s.wait_ge(sTRG.h, b[id(sTRG)] + 5 * (g - 1))
                            s.dma_start(out=Adg[g % 2][:, :],
                                        in_=bass.AP(esc[h % 2],
                                                    t * 128 * (WA + 1)
                                                    + aw0_of(t) - 1791,
                                                    [[WA + 1, 128], [1, 2304]])
                                        ).then_inc(sDRDb[g % 2].h, 16)
                            sDRDb[g % 2].n += 16
                        s.wait_ge(sOEV.h, b[id(sOEV)] + 32 * 8)
                        s.dma_start(out=bass.AP(hod, 7 * 128 * T,
                                                [[T, 128], [1, T]]),
                                    in_=HOp[7 % 2][:, :]).then_inc(sHODb[1].h, 16)
                        sHODb[1].n += 16
                        s.wait_ge(sHODb[0].h, sHODb[0].n)
                        s.wait_ge(sHODb[1].h, sHODb[1].n)

                    @blk.scalar
                    def _(s):
                        for g in range(NG):
                            h, t = divmod(g, NT)
                            s.wait_ge(sSRE.h, b[id(sSRE)] + 4 * g + 4)
                            if h >= 2:
                                gp = (h - 2) * NT + t
                                s.wait_ge(sBRDb[gp % 2].h,
                                          b[id(sBRDb[gp % 2])] + 16 * (gp // 2 + 1))
                            s.dma_start(out=bass.AP(spr[h % 2],
                                                    t * 128 * WS + 1024,
                                                    [[WS, 128], [1, 2047]]),
                                        in_=Ssb[g % 2][:, :]
                                        ).then_inc(sSSTb[g % 2].h, 16)
                            sSSTb[g % 2].n += 16
                            s.wait_ge(sBAD.h, b[id(sBAD)] + 4 * g + 4)
                            s.activation(out=bndE[g % 2][:, :],
                                         in_=scoA[g % 2][:, :], func=AF.Exp,
                                         bias=czs[:, 1:2], scale=1.0,
                                         accum_out=rsc[:, g % 2:g % 2 + 1])
                            s.activation(out=czs[:, 3:4], in_=czs[:, 0:1],
                                         func=AF.Copy, bias=0.0, scale=1.0
                                         ).then_inc(sEXP.h, 1)
                            sEXP.n += 1
                            s.wait_ge(sNRM.h, b[id(sNRM)] + g + 1)
                            if h >= 2:
                                gp = (h - 2) * NT + t
                                s.wait_ge(sDRDb[gp % 2].h,
                                          b[id(sDRDb[gp % 2])] + 16 * (gp // 2 + 1))
                            Av = scoA[g % 2][:, :].bitcast(BF16)[:, 0:T]
                            s.dma_start(out=bass.AP(esc[h % 2],
                                                    t * 128 * WA + 256,
                                                    [[WA, 128], [1, 2048]]),
                                        in_=Av).then_inc(sASTb[g % 2].h, 16)
                            sASTb[g % 2].n += 16

                    @blk.tensor
                    def _(tn):
                        for g in range(NG + 1):
                            if g >= 1:
                                g1 = g - 1
                                h1, t1 = divmod(g1, NT)
                                off1 = 64 * (h1 % 2)
                                hp21 = (h1 // 2) % 2
                                j2 = g1 % 2
                                Af32 = scoA[g1 % 2][:, 0:1024]
                                tn.wait_ge(sNRM.h, b[id(sNRM)] + g1 + 1)
                                for grp in range(2):
                                    gg = 5 * g1 + grp
                                    if gg >= 2:
                                        tn.wait_ge(sGEV.h, b[id(sGEV)] + gg - 1)
                                    for bb in range(4):
                                        blkc = grp * 4 + bb
                                        ti = tn.transpose(
                                            psT[gg % 2][:, bb * 128:(bb + 1) * 128],
                                            Af32[:, blkc * 128:(blkc + 1) * 128],
                                            identF[:, :])
                                    ti.then_inc(sTRG.h, 1)
                                    sTRG.n += 1
                                tn.wait_ge(sDRDb[g1 % 2].h,
                                           b[id(sDRDb[g1 % 2])]
                                           + 16 * (g1 // 2 + 1))
                                AdgF = Adg[g1 % 2][:, :].bitcast(F32)[:, 0:1152]
                                for grp in range(3):
                                    gg = 5 * g1 + 2 + grp
                                    nb = 4 if grp < 2 else 1
                                    if gg >= 2:
                                        tn.wait_ge(sGEV.h, b[id(sGEV)] + gg - 1)
                                    for bb in range(nb):
                                        blkc = grp * 4 + bb
                                        ti = tn.transpose(
                                            psT[gg % 2][:, bb * 128:(bb + 1) * 128],
                                            AdgF[:, blkc * 128:(blkc + 1) * 128],
                                            identF[:, :])
                                    ti.then_inc(sTRG.h, 1)
                                    sTRG.n += 1
                                tn.wait_ge(sGEV.h, b[id(sGEV)] + 5 * g1 + 5)
                                if g1 >= 2:
                                    tn.wait_ge(sOEV.h, b[id(sOEV)] + g1 - 1)
                                ATb = ATsP[:, :].bitcast(BF16)
                                ATr = ATb.rearrange(
                                    "p (j c l s) -> p j c s l",
                                    j=2, c=8, l=128, s=2)
                                first = True
                                for c in range(8):
                                    for sx, vbuf in [(0, Vpe), (1, Vpo)]:
                                        rhs = ATr[:, j2, c, sx, :]
                                        tn.matmul(
                                            psO[g1 % 2][off1:off1 + 64, :],
                                            vbuf[hp21][:, c * 128 + off1:
                                                       c * 128 + off1 + 64],
                                            rhs, start=first, stop=False)
                                        first = False
                                DTb = DTsP[:, :].bitcast(BF16)
                                DTr = DTb.rearrange(
                                    "p (j c l s) -> p j c s l",
                                    j=2, c=9, l=128, s=2)
                                for cp in range(9):
                                    for sx, rvt in [(0, RVsbE), (1, RVsbO)]:
                                        gcp = aw0_of(t1) // 256 + cp
                                        rhs = DTr[:, j2, cp, sx, :]
                                        mi = tn.matmul(
                                            psO[g1 % 2][off1:off1 + 64, :],
                                            rvt[:, gcp * 64:(gcp + 1) * 64],
                                            rhs, start=False,
                                            stop=(cp == 8 and sx == 1))
                                mi.then_inc(sAV.h, 1)
                                sAV.n += 1
                            if g < NG:
                                h, t = divmod(g, NT)
                                off = 64 * (h % 2)
                                hp2 = (h // 2) % 2
                                tbl = tblA if t < 8 else tblB
                                lhs = Qp[hp2][off:off + 64,
                                              t * 128:(t + 1) * 128]
                                hp_ = h // 2
                                tn.wait_ge(sQPLb[hp_ % 2].h,
                                           b[id(sQPLb[hp_ % 2])]
                                           + 64 * (hp_ // 2 + 1))
                                for p, (o, w) in enumerate(PIECES):
                                    consumed_wait(tn, 8 * g + p)
                                    tn.matmul(PS[p % 2][:, 0:w], lhs,
                                              tbl[off:off + 64, o:o + w],
                                              start=True, stop=True
                                              ).then_inc(sSR.h, 1)
                                    sSR.n += 1
                                for p in range(4):
                                    consumed_wait(tn, 8 * g + 4 + p)
                                    tn.matmul(PS[p % 2][:, :], lhs,
                                              Kp[hp2][off:off + 64,
                                                      p * 512:(p + 1) * 512],
                                              start=True, stop=True
                                              ).then_inc(sQK.h, 1)
                                    sQK.n += 1
                    @blk.vector
                    def _(v):
                        for g in range(NG + 1):
                            if g >= 1:
                                g1 = g - 1
                                h1, t1 = divmod(g1, NT)
                                off1 = 64 * (h1 % 2)
                                j2 = g1 % 2
                                pend = [None]

                                def evac_inc(inst):
                                    if pend[0] is not None:
                                        inst.then_inc(sGEV.h, 1)
                                        sGEV.n += 1
                                    pend[0] = inst
                                for grp in range(2):
                                    gg = 5 * g1 + grp
                                    v.wait_ge(sTRG.h, b[id(sTRG)] + gg + 1)
                                    ei = v.tensor_copy(
                                        out=ATsP[:, j2 * 1024 + grp * 512:
                                                 j2 * 1024 + (grp + 1) * 512],
                                        in_=psT[gg % 2][:, :])
                                    evac_inc(ei)
                                for grp in range(3):
                                    gg = 5 * g1 + 2 + grp
                                    nb = 4 if grp < 2 else 1
                                    v.wait_ge(sTRG.h, b[id(sTRG)] + gg + 1)
                                    ei = v.tensor_copy(
                                        out=DTsP[:, j2 * 1152 + grp * 512:
                                                 j2 * 1152 + grp * 512 + nb * 128],
                                        in_=psT[gg % 2][:, 0:nb * 128])
                                    evac_inc(ei)
                                v.tensor_copy(out=czs[:, 3:4],
                                              in_=czs[:, 0:1]).then_inc(sGEV.h, 1)
                                sGEV.n += 1
                                hp1 = h1 // 2
                                v.wait_ge(sAV.h, b[id(sAV)] + g1 + 1)
                                if t1 <= 1 and hp1 >= 2:
                                    sh = sHODb[hp1 % 2]
                                    v.wait_ge(sh.h, b[id(sh)]
                                              + 16 * ((hp1 - 2) // 2 + 1))
                                v.tensor_copy(
                                    out=HOp[hp1 % 2][off1:off1 + 64,
                                                     t1 * 128:t1 * 128 + 128],
                                    in_=psO[g1 % 2][off1:off1 + 64, :]
                                    ).then_inc(sOEV.h, 1)
                                sOEV.n += 1
                            if g < NG:
                                h, t = divmod(g, NT)
                                prevE = [None]

                                def sh_inc(inst, sem, store):
                                    if store[0] is not None:
                                        inst.then_inc(sem.h, 1)
                                        sem.n += 1
                                    store[0] = inst
                                for p, (o, w) in enumerate(PIECES):
                                    v.wait_ge(sSR.h, b[id(sSR)] + 4 * g + p + 1)
                                    if p == 0 and g >= 2:
                                        v.wait_ge(sSSTb[g % 2].h,
                                                  b[id(sSSTb[g % 2])]
                                                  + 16 * ((g - 2) // 2 + 1))
                                    ei = v.tensor_copy(out=Ssb[g % 2][:, o:o + w],
                                                       in_=PS[p % 2][:, 0:w])
                                    sh_inc(ei, sSRE, prevE)
                                di0 = v.tensor_copy(out=czs[:, 3:4],
                                                    in_=czs[:, 0:1])
                                sh_inc(di0, sSRE, prevE)
                                prevE[0] = None
                                prevB = [None]
                                for p in range(4):
                                    v.wait_ge(sQK.h, b[id(sQK)] + 4 * g + p + 1)
                                    if p == 0:
                                        v.wait_ge(sBRDb[g % 2].h,
                                                  b[id(sBRDb[g % 2])]
                                                  + 16 * (g // 2 + 1))
                                        if g >= 2:
                                            v.wait_ge(sASTb[g % 2].h,
                                                      b[id(sASTb[g % 2])]
                                                      + 16 * ((g - 2) // 2 + 1))
                                            v.wait_ge(sTRG.h,
                                                      b[id(sTRG)] + 5 * (g - 2) + 2)
                                    bi = v.tensor_tensor(
                                        out=scoA[g % 2][:, p * 512:(p + 1) * 512],
                                        in0=PS[p % 2][:, :],
                                        in1=bndF[g % 2][:, p * 512:(p + 1) * 512],
                                        op=ALU.add)
                                    sh_inc(bi, sBAD, prevB)
                                di = v.tensor_copy(out=czs[:, 3:4],
                                                   in_=czs[:, 0:1])
                                sh_inc(di, sBAD, prevB)
                                v.wait_ge(sEXP.h, b[id(sEXP)] + g + 1)
                                v.reciprocal(out=rsc[:, 2 + g % 2:3 + g % 2],
                                             in_=rsc[:, g % 2:g % 2 + 1])
                                v.tensor_copy(out=rsc[:, 3 - g % 2:4 - g % 2],
                                              in_=czs[:, 0:1])
                                v.tensor_copy(out=czs[:, 3:4], in_=czs[:, 0:1])
                                v.tensor_scalar(
                                    out=scoA[g % 2][:, :].bitcast(BF16)[:, 0:T],
                                    in0=bndE[g % 2][:, :],
                                    scalar1=rsc[:, 2 + g % 2:3 + g % 2],
                                    scalar2=None, op0=ALU.mult)
                                v.tensor_copy(out=czs[:, 3:4],
                                              in_=czs[:, 0:1]).then_inc(sNRM.h, 1)
                                sNRM.n += 1
                    @blk.gpsimd
                    def _(g_):
                        g_.wait_ge(sOEV.h, b[id(sOEV)] + 16 * NH)

            with ExitStack() as scw:
                e3 = scw.enter_context
                HOsb = [e3(nc.sbuf_tensor(f"hosb{i}", [128, T], BF16))
                        for i in range(NDC)]
                wbf2 = [e3(nc.sbuf_tensor(f"wo{i}", [128, D], BF16))
                        for i in range(NDC)]
                wtm2 = [e3(nc.sbuf_tensor(f"wotm{i}", [128, D], F32))
                        for i in range(2)]
                psW = [e3(nc.psum_tensor(f"psw{i}", [128, 512], F32))
                       for i in range(2)]
                sWL, sWC, sWM, sWA = SS[53:57]
                sHL = SS[57]
                sWLb = [sWL, SS[71]]
                b = snapd(sWL, sWC, sWM, sWA, sHL, SS[71])
                with nc.Block() as blk:
                    @blk.sync
                    def _(s):
                        for c in range(NDC):
                            s.dma_start(out=HOsb[c][:, :],
                                        in_=bass.AP(hod, c * 128 * T,
                                                    [[T, 128], [1, T]])
                                        ).then_inc(sHL.h, 16)
                            sHL.n += 16
                        for c in range(NDC):
                            if c >= 2:
                                s.wait_ge(sWC.h, b[id(sWC)] + c - 1)
                            s.dma_start(out=wtm2[c % 2][:, :],
                                        in_=woT[c * 128:(c + 1) * 128, :]
                                        ).then_inc(sWLb[c % 2].h, 16)
                            sWLb[c % 2].n += 16
                        s.wait_ge(sHL.h, b[id(sHL)] + 16 * NDC)

                    @blk.vector
                    def _(v):
                        for c in range(NDC):
                            v.wait_ge(sWLb[c % 2].h,
                                      b[id(sWLb[c % 2])] + 16 * (c // 2 + 1))
                            v.tensor_copy(out=wbf2[c][:, :],
                                          in_=wtm2[c % 2][:, :]).then_inc(sWC.h, 1)
                            sWC.n += 1
                with nc.Block() as blk:
                    @blk.vector
                    def _(v):
                        for j in range(32):
                            oc, p = divmod(j, 4)
                            v.wait_ge(sWM.h, b[id(sWM)] + j + 1)
                            xsl = X[oc][:, p * 512:(p + 1) * 512]
                            v.scalar_tensor_tensor(out=xsl, in0=psW[j % 2][:, :],
                                                   scalar=bos[:, oc:oc + 1],
                                                   in1=xsl, op0=ALU.add,
                                                   op1=ALU.add).then_inc(sWA.h, 1)
                            sWA.n += 1

                    @blk.tensor
                    def _(tn):
                        for j in range(32):
                            oc, p = divmod(j, 4)
                            if j >= 2:
                                tn.wait_ge(sWA.h, b[id(sWA)] + j)
                            for cc in range(NDC):
                                mi = tn.matmul(
                                    psW[j % 2][:, :],
                                    wbf2[cc][:, oc * 128:(oc + 1) * 128],
                                    HOsb[cc][:, p * 512:(p + 1) * 512],
                                    start=(cc == 0), stop=(cc == NDC - 1))
                            mi.then_inc(sWM.h, 1)
                            sWM.n += 1

                    @blk.gpsimd
                    def _(g):
                        g.wait_ge(sWA.h, b[id(sWA)] + 32)
                        if dbg:
                            sst = SS[51]
                            for i in range(NDC):
                                g.dma_start(
                                    out=dbg_t["dbg_attn"][i * 128:(i + 1) * 128, :],
                                    in_=X[i][:, :]).then_inc(sst.h, 16)
                                sst.n += 16
                            for i in range(NDC):
                                g.dma_start(
                                    out=dbg_t["dbg_ho"][i * 128:(i + 1) * 128, :]
                                    .bitcast(BF16)[:, 0:T],
                                    in_=HOsb[i][:, :]).then_inc(sst.h, 16)
                                sst.n += 16
                            g.wait_ge(sst.h, sst.n)

        with ExitStack() as sc:
            e2 = sc.enter_context
            Xln = [e2(nc.sbuf_tensor(f"xln3{i}", [128, T], BF16))
                   for i in range(NDC)]
            psM = [e2(nc.psum_tensor(f"psm3{i}", [128, 512], F32))
                   for i in range(4)]
            psS = [e2(nc.psum_tensor(f"pss3{i}", [128, 512], F32))
                   for i in range(4)]
            with ExitStack() as scl:
                e3 = scl.enter_context
                rowsA = e3(nc.sbuf_tensor("rowsA3", [1, T], F32))
                rowsB = e3(nc.sbuf_tensor("rowsB3", [1, T], F32))
                bcA = e3(nc.sbuf_tensor("bcA3", [128, T], F32))
                bcB = e3(nc.sbuf_tensor("bcB3", [128, T], F32))
                xsq = [e3(nc.sbuf_tensor(f"xsq3{i}", [128, T], BF16))
                       for i in range(2)]
                vrow = e3(nc.sbuf_tensor("vrow3", [1, T], F32))
                tmpf = e3(nc.sbuf_tensor("tmpf3", [128, T], F32))
                with nc.Block() as blk:
                    ln_phase(blk, Xln, psM, psS, rowsA, rowsB, vrow, bcA, bcB,
                             xsq, tmpf)
            with ExitStack() as scf:
                e3 = scf.enter_context
                H = [e3(nc.sbuf_tensor(f"hb_{i}", [128, 1024], BF16))
                     for i in range(NFC)]
                w1s = [e3(nc.sbuf_tensor(f"w1sb_{i}", [128, 8, 128], BF16))
                       for i in range(2)]
                w2s = [e3(nc.sbuf_tensor(f"w2sb_{i}", [128, 32, 128], BF16))
                       for i in range(2)]
                with nc.Block() as blk:
                    ffn_phase(blk, Xln, H, w1s, w2s, psM, lambda g: None)

        with nc.Block() as blk:
            @blk.gpsimd
            def _(g):
                sst = SS[51]
                for i in range(NDC):
                    g.dma_start(out=yT[i * 128:(i + 1) * 128, :],
                                in_=X[i][:, :]).then_inc(sst.h, 16)
                    sst.n += 16
                g.wait_ge(sst.h, sst.n)

    return nc


def _prep_inputs(inputs):
    f = np.float32
    x = np.asarray(inputs["L0_attn_x"], f)

    def chunks(bv, n):
        return np.ascontiguousarray(np.asarray(bv, f).reshape(n, 128).T)

    relv = np.asarray(inputs["L0_attn_relative_v"], f)
    common = {
        "wqT": np.ascontiguousarray(np.asarray(inputs["L0_attn_Wq"], f).T),
        "wkT": np.ascontiguousarray(np.asarray(inputs["L0_attn_Wk"], f).T),
        "wvT": np.ascontiguousarray(np.asarray(inputs["L0_attn_Wv"], f).T),
        "woT": np.ascontiguousarray(np.asarray(inputs["L0_attn_Wo"], f).T),
        "w1T": np.ascontiguousarray(np.asarray(inputs["L0_ffn_W1"], f).T),
        "w2T": np.ascontiguousarray(np.asarray(inputs["L0_ffn_W2"], f).T),
        "bq_c": chunks(inputs["L0_attn_bq"], NDC),
        "bk_c": chunks(inputs["L0_attn_bk"], NDC),
        "bo_c": chunks(inputs["L0_attn_bo"], NDC),
        "b1_c": chunks(inputs["L0_ffn_b1"], NFC),
        "bv_r": np.asarray(inputs["L0_attn_bv"], f).reshape(1, D),
        "b2_r": np.asarray(inputs["L0_ffn_b2"], f).reshape(1, D),
        "relkT": np.ascontiguousarray(
            np.asarray(inputs["L0_attn_relative_k"], f).T),
        "rvpad": np.ascontiguousarray(
            relv[np.clip(np.arange(4096) - 1024, 0, 2046)]),
    }
    in_maps = []
    for c in range(8):
        m = dict(common)
        m["xT"] = np.ascontiguousarray(x[c].T)
        in_maps.append(m)
    return in_maps


def kernel(**inputs):
    in_maps = _prep_inputs(inputs)
    nc = build_nc(dbg=False)
    res = run_bass_kernel_spmd(nc, in_maps, core_ids=list(range(8)))
    out = np.stack([np.ascontiguousarray(r["yT"].T) for r in res.results])
    return out.astype(np.float32)


# version 08
# speedup vs baseline: 1.0350x; 1.0350x over previous
import numpy as np
from contextlib import ExitStack

import concourse.bass as bass
import concourse.mybir as mybir
from concourse.bass_utils import run_bass_kernel_spmd

F32 = mybir.dt.float32
BF16 = mybir.dt.bfloat16
FP16 = mybir.dt.float16
AF = mybir.ActivationFunctionType
ALU = mybir.AluOpType
AX = mybir.AxisListType

T = 2048
D = 1024
NH = 16
DK = 64
DF = 4096
NDC = D // 128
NFC = DF // 128
NT = T // 128
CSH = 32.0
EPS = 1e-5
WS = 4095
WA = 2560
NG = NH * NT
PIECES = [(0, 512), (512, 512), (1024, 512), (1536, 511)]


class Ctr:
    def __init__(self, h):
        self.h = h
        self.n = 0


def build_nc(dbg=False):
    nc = bass.Bass("TRN2", target_bir_lowering=False, detect_race_conditions=False)

    xT = nc.dram_tensor("xT", [D, T], F32, kind="ExternalInput").ap()
    wqT = nc.dram_tensor("wqT", [D, D], F32, kind="ExternalInput").ap()
    wkT = nc.dram_tensor("wkT", [D, D], F32, kind="ExternalInput").ap()
    wvT = nc.dram_tensor("wvT", [D, D], F32, kind="ExternalInput").ap()
    woT = nc.dram_tensor("woT", [D, D], F32, kind="ExternalInput").ap()
    w1T = nc.dram_tensor("w1T", [D, DF], F32, kind="ExternalInput").ap()
    w2T = nc.dram_tensor("w2T", [DF, D], F32, kind="ExternalInput").ap()
    bq_c = nc.dram_tensor("bq_c", [128, NDC], F32, kind="ExternalInput").ap()
    bk_c = nc.dram_tensor("bk_c", [128, NDC], F32, kind="ExternalInput").ap()
    bo_c = nc.dram_tensor("bo_c", [128, NDC], F32, kind="ExternalInput").ap()
    b1_c = nc.dram_tensor("b1_c", [128, NFC], F32, kind="ExternalInput").ap()
    bv_r = nc.dram_tensor("bv_r", [1, D], F32, kind="ExternalInput").ap()
    b2_r = nc.dram_tensor("b2_r", [1, D], F32, kind="ExternalInput").ap()
    relkT = nc.dram_tensor("relkT", [64, 2047], F32, kind="ExternalInput").ap()
    rvpad = nc.dram_tensor("rvpad", [4096, 64], F32, kind="ExternalInput").ap()
    yT = nc.dram_tensor("yT", [D, T], F32, kind="ExternalOutput").ap()
    dbg_t = {}
    if dbg:
        for name in ["dbg_ffn1", "dbg_ln2", "dbg_attn", "dbg_ho",
                     "dbg_x0", "dbg_xln1", "dbg_w1q"]:
            dbg_t[name] = nc.dram_tensor(name, [D, T], F32,
                                         kind="ExternalOutput").ap()
    qd = nc.dram_tensor("qd", [D, T], BF16, kind="Internal")
    kd = nc.dram_tensor("kd", [D, T], BF16, kind="Internal")
    vd = nc.dram_tensor("vd", [T, D], BF16, kind="Internal")
    w1q = nc.dram_tensor("w1q", [D, DF], BF16, kind="Internal")
    w2q = nc.dram_tensor("w2q", [DF, D], BF16, kind="Internal")
    spr = [nc.dram_tensor(f"spr{i}", [T * WS], FP16, kind="Internal")
           for i in range(2)]
    esc = [nc.dram_tensor(f"esc{i}", [T * WA], BF16, kind="Internal")
           for i in range(2)]
    rowscr = nc.dram_tensor("rowscr", [2 * T], F32, kind="Internal")
    hod = nc.dram_tensor("hod", [D, T], BF16, kind="Internal")
    dbg_rows = None
    if dbg:
        dbg_rows = nc.dram_tensor("dbg_rows", [2, T], F32,
                                  kind="ExternalOutput").ap()

    qdA = qd.ap()
    kdA = kd.ap()
    vdA = vd.ap()
    w1qA = w1q.ap()
    w2qA = w2q.ap()

    with ExitStack() as top:
        en = top.enter_context
        X = [en(nc.sbuf_tensor(f"x{i}", [128, T], F32)) for i in range(NDC)]
        tblA = en(nc.sbuf_tensor("tblA", [128, 2047], BF16))
        tblB = en(nc.sbuf_tensor("tblB", [128, 2047], BF16))
        RVsbE = en(nc.sbuf_tensor("rvsbe", [128, 16 * 64], BF16))
        RVsbO = en(nc.sbuf_tensor("rvsbo", [128, 16 * 64], BF16))
        ident = en(nc.sbuf_tensor("idnt", [128, 128], BF16))
        identF = en(nc.sbuf_tensor("idntf", [128, 128], F32))
        onec = en(nc.sbuf_tensor("onec", [128, 2], BF16))
        oner = en(nc.sbuf_tensor("oner", [1, 512], BF16))
        onerf = en(nc.sbuf_tensor("onerf", [1, 128], F32))
        bqs = en(nc.sbuf_tensor("bqs", [128, NDC], F32))
        bks = en(nc.sbuf_tensor("bks", [128, NDC], F32))
        bos = en(nc.sbuf_tensor("bos", [128, NDC], F32))
        b1s = en(nc.sbuf_tensor("b1s", [128, NFC], F32))
        bvr = en(nc.sbuf_tensor("bvr", [1, D], BF16))
        hb2r = en(nc.sbuf_tensor("hb2r", [1, D], BF16))
        thrs = en(nc.sbuf_tensor("thrs", [128, 4], F32))
        czs = en(nc.sbuf_tensor("czs", [128, 4], F32))
        SS = [Ctr(en(nc.semaphore(f"s{i}"))) for i in range(80)]

        def snap():
            return [s.n for s in SS]

        def snapd(*sems):
            return {id(s): s.n for s in sems}

        with ExitStack() as sc:
            e2 = sc.enter_context
            ztile = e2(nc.sbuf_tensor("ztile", [128, 1024], BF16))
            rtmp = e2(nc.sbuf_tensor("rtmp0", [128, 2048], F32))
            itmp = e2(nc.sbuf_tensor("itmp0", [128, 256], F32))
            rwb1 = e2(nc.sbuf_tensor("rwb1", [1, D], F32))
            rwb2 = e2(nc.sbuf_tensor("rwb2", [1, D], F32))
            s0, s1, s2, s3, s4 = SS[0], SS[1], SS[2], SS[3], SS[4]
            b = snap()
            with nc.Block() as blk:
                @blk.gpsimd
                def _(g):
                    for i in range(NDC):
                        g.dma_start(out=X[i][:, :],
                                    in_=xT[i * 128:(i + 1) * 128, :]
                                    ).then_inc(s0.h, 16)
                        s0.n += 16
                    for dst, srcd in [(bqs, bq_c), (bks, bk_c), (bos, bo_c),
                                      (b1s, b1_c)]:
                        g.dma_start(out=dst[:, :], in_=srcd).then_inc(s0.h, 16)
                        s0.n += 16
                    g.dma_start(out=rwb1[0:1, :], in_=bv_r).then_inc(s0.h, 16)
                    s0.n += 16
                    g.dma_start(out=rwb2[0:1, :], in_=b2_r).then_inc(s0.h, 16)
                    s0.n += 16
                    g.iota(itmp[:, 0:128], [[1, 128]], channel_multiplier=0,
                           allow_small_or_imprecise_dtypes=True)
                    g.iota(itmp[:, 128:256], [[0, 128]], channel_multiplier=1,
                           allow_small_or_imprecise_dtypes=True)
                    g.memset(ztile[:, :], 0)
                    g.memset(onec[:, :], 1.0)
                    g.memset(oner[:, :], 1.0)
                    g.memset(czs[:, 0:1], 0.0)
                    g.memset(czs[:, 1:2], -CSH)
                    g.memset(czs[:, 2:3], EPS)
                    g.memset(onerf[:, :], 1.0).then_inc(s1.h, 1)
                    s1.n += 1
                    g.wait_ge(s0.h, s0.n)

                @blk.sync
                def _(s):
                    s.dma_start(out=rtmp[0:64, 0:2047],
                                in_=relkT).then_inc(s2.h, 16)
                    s2.n += 16
                    s.dma_start(out=rtmp[64:128, 0:2047],
                                in_=relkT).then_inc(s2.h, 16)
                    s2.n += 16

                @blk.vector
                def _(v):
                    v.wait_ge(s1.h, b[1] + 1)
                    v.tensor_tensor(out=ident[:, :], in0=itmp[:, 0:128],
                                    in1=itmp[:, 128:256], op=ALU.is_equal)
                    v.tensor_tensor(out=identF[:, :], in0=itmp[:, 0:128],
                                    in1=itmp[:, 128:256], op=ALU.is_equal)
                    v.wait_ge(s0.h, b[0] + 16 * 14)
                    v.tensor_scalar(out=bks[:, :], in0=bks[:, :], scalar1=0.125,
                                    scalar2=None, op0=ALU.mult)
                    v.tensor_copy(out=bvr[:, :], in_=rwb1[0:1, :])
                    v.tensor_scalar(out=hb2r[:, :], in0=rwb2[0:1, :],
                                    scalar1=0.5, scalar2=None, op0=ALU.mult)
                    v.wait_ge(s2.h, b[2] + 32)
                    v.scalar_tensor_tensor(
                        out=tblA[:, :], in0=rtmp[:, 0:2047], scalar=1.0,
                        in1=rtmp[:, 2046:2047].to_broadcast([128, 2047]),
                        op0=ALU.mult, op1=ALU.subtract)
                    v.scalar_tensor_tensor(
                        out=tblB[:, :], in0=rtmp[:, 0:2047], scalar=1.0,
                        in1=rtmp[:, 0:1].to_broadcast([128, 2047]),
                        op0=ALU.mult, op1=ALU.subtract).then_inc(s3.h, 1)
                    s3.n += 1
                    v.wait_ge(s2.h, b[2] + 64)
                    v.tensor_copy(out=RVsbE[:, :], in_=rtmp[:, 0:1024])
                    v.tensor_copy(out=RVsbO[:, :], in_=rtmp[:, 1024:2048]
                                  ).then_inc(s3.h, 1)
                    s3.n += 1

                @blk.sync
                def _(s):
                    s.wait_ge(s3.h, b[3] + 1)
                    s.dma_start(
                        out=rtmp[:, 0:1024].rearrange("p (c k) -> p c k", k=64),
                        in_=bass.AP(rvpad.tensor, 0,
                                    [[128, 128], [128 * 128, 16], [1, 64]])
                    ).then_inc(s2.h, 16)
                    s2.n += 16
                    s.dma_start(
                        out=rtmp[:, 1024:2048].rearrange("p (c k) -> p c k", k=64),
                        in_=bass.AP(rvpad.tensor, 64,
                                    [[128, 128], [128 * 128, 16], [1, 64]])
                    ).then_inc(s2.h, 16)
                    s2.n += 16

                @blk.scalar
                def _(s):
                    pass
                    s.wait_ge(s1.h, b[1] + 1)
                    for bi in range(2):
                        for rt in range(NT):
                            for off_, wid_, tsr in [
                                    (rt * 128 * WS, 1024, spr[bi]),
                                    (rt * 128 * WS + 3071, 1024, spr[bi]),
                                    (rt * 128 * WA, 256, esc[bi]),
                                    (rt * 128 * WA + 2304, 256, esc[bi])]:
                                zsrc = (ztile[:, 0:wid_].bitcast(FP16)[:, 0:wid_]
                                        if wid_ == 1024 else ztile[:, 0:wid_])
                                s.dma_start(out=bass.AP(tsr, off_,
                                                        [[WS if wid_ == 1024 else WA, 128],
                                                         [1, wid_]]),
                                            in_=zsrc).then_inc(s4.h, 16)
                                s4.n += 16
                    s.wait_ge(s4.h, s4.n)
                    s.wait_ge(s3.h, b[3] + 2)

        with ExitStack() as sc:
            e2 = sc.enter_context
            wtmp = [e2(nc.sbuf_tensor(f"wt{i}", [128, DF], F32)) for i in range(2)]
            wabs = e2(nc.sbuf_tensor("wabs", [128, DF], FP16))
            wqo = [e2(nc.sbuf_tensor(f"wqo{i}", [128, DF], BF16)) for i in range(2)]
            qrows = e2(nc.sbuf_tensor("qrows", [1, 600], F32))
            psQ = e2(nc.psum_tensor("psq", [1, 512], F32))
            psB = e2(nc.psum_tensor("psb", [128, 2], F32))

            def quant_pass(blk, which):
                src, dstA = (w1T, w1qA) if which == 0 else (w2T, w2qA)
                nch = NDC if which == 0 else NFC
                wid = DF if which == 0 else D
                npc = wid // 512
                ntot = float(D * DF)
                thr_col = thrs[:, 2 * which:2 * which + 1]
                nthr_col = thrs[:, 2 * which + 1:2 * which + 2]
                sLD, sAB, sMM, sRW, sBC, sTH, sQO, sSO = SS[5:13]
                sLDb = [sLD, SS[67]]
                sSOb = [sSO, SS[68]]
                b = snapd(sLD, sAB, sMM, sRW, sBC, sTH, sQO, sSO,
                          SS[67], SS[68])

                @blk.sync
                def _(s):
                    for c in range(nch):
                        if c >= 2:
                            s.wait_ge(sAB.h, b[id(sAB)] + c - 1)
                        s.dma_start(out=wtmp[c % 2][:, 0:wid],
                                    in_=src[c * 128:(c + 1) * 128, :]
                                    ).then_inc(sLDb[c % 2].h, 16)
                        sLDb[c % 2].n += 16
                    for c in range(nch):
                        s.wait_ge(sTH.h, b[id(sTH)] + 1)
                        if c >= 2:
                            s.wait_ge(sQO.h, b[id(sQO)] + c - 1)
                        s.dma_start(out=wtmp[c % 2][:, 0:wid],
                                    in_=src[c * 128:(c + 1) * 128, :]
                                    ).then_inc(sLDb[c % 2].h, 16)
                        sLDb[c % 2].n += 16

                @blk.scalar
                def _(s):
                    for c in range(nch):
                        s.wait_ge(sLDb[c % 2].h,
                                  b[id(sLDb[c % 2])] + 16 * (c // 2 + 1))
                        if c >= 1:
                            s.wait_ge(sMM.h, b[id(sMM)] + c * npc)
                        s.activation(out=wabs[:, 0:wid], in_=wtmp[c % 2][:, 0:wid],
                                     func=AF.Abs, bias=czs[:, 0:1], scale=1.0)
                        s.activation(out=czs[:, 3:4], in_=czs[:, 0:1],
                                     func=AF.Copy, bias=0.0, scale=1.0
                                     ).then_inc(sAB.h, 1)
                        sAB.n += 1

                @blk.tensor
                def _(tn):
                    for c in range(nch):
                        tn.wait_ge(sAB.h, b[id(sAB)] + c + 1)
                        for p in range(npc):
                            tn.matmul(psQ[0:1, :],
                                      onec[:, 0:1], wabs[:, p * 512:(p + 1) * 512],
                                      start=(c == 0 and p == 0),
                                      stop=(c == nch - 1 and p == npc - 1)
                                      ).then_inc(sMM.h, 1)
                            sMM.n += 1
                    tn.wait_ge(sRW.h, b[id(sRW)] + 1)
                    tn.matmul(psB[:, 0:1], onerf[0:1, :], qrows[0:1, 577:578],
                              start=True, stop=True)
                    tn.matmul(psB[:, 1:2], onerf[0:1, :], qrows[0:1, 578:579],
                              start=True, stop=True).then_inc(sBC.h, 1)
                    sBC.n += 1

                @blk.vector
                def _(v):
                    def gap():
                        v.tensor_copy(out=czs[:, 3:4], in_=czs[:, 0:1])
                        v.tensor_copy(out=czs[:, 3:4], in_=czs[:, 0:1])
                    v.wait_ge(sMM.h, b[id(sMM)] + nch * npc)
                    v.reduce_sum(out=qrows[0:1, 0:64],
                                 in_=psQ[0:1, :].rearrange(
                                     "p (a c) -> p a c", c=8), axis=AX.X)
                    gap()
                    v.reduce_sum(out=qrows[0:1, 512:520],
                                 in_=qrows[0:1, 0:64].rearrange(
                                     "p (a c) -> p a c", c=8), axis=AX.X)
                    gap()
                    v.reduce_sum(out=qrows[0:1, 576:577],
                                 in_=qrows[0:1, 512:520], axis=AX.X)
                    gap()
                    v.tensor_scalar(out=qrows[0:1, 577:578],
                                    in0=qrows[0:1, 576:577],
                                    scalar1=0.5 / ntot, scalar2=0.5 * EPS,
                                    op0=ALU.mult, op1=ALU.add)
                    v.tensor_scalar(out=qrows[0:1, 578:579],
                                    in0=qrows[0:1, 576:577],
                                    scalar1=-0.5 / ntot, scalar2=-0.5 * EPS,
                                    op0=ALU.mult, op1=ALU.add)
                    v.tensor_copy(out=czs[:, 3:4],
                                  in_=czs[:, 0:1]).then_inc(sRW.h, 1)
                    sRW.n += 1
                    v.wait_ge(sBC.h, b[id(sBC)] + 1)
                    v.tensor_copy(out=thr_col, in_=psB[:, 0:1])
                    v.tensor_copy(out=nthr_col, in_=psB[:, 1:2]).then_inc(sTH.h, 1)
                    sTH.n += 1
                    sca = 0.5 if which == 1 else 1.0
                    for c in range(nch):
                        v.wait_ge(sLDb[c % 2].h,
                                  b[id(sLDb[c % 2])] + 16 * ((nch + c) // 2 + 1))
                        if c >= 2:
                            v.wait_ge(sSOb[c % 2].h,
                                      b[id(sSOb[c % 2])] + 16 * ((c - 2) // 2 + 1))
                        v.tensor_scalar(out=wabs[:, 0:wid],
                                        in0=wtmp[c % 2][:, 0:wid],
                                        scalar1=thr_col, scalar2=sca,
                                        op0=ALU.is_gt, op1=ALU.mult)
                        v.tensor_scalar(out=wqo[c % 2][:, 0:wid],
                                        in0=wtmp[c % 2][:, 0:wid],
                                        scalar1=nthr_col, scalar2=sca,
                                        op0=ALU.is_lt, op1=ALU.mult)
                        v.tensor_tensor(out=wqo[c % 2][:, 0:wid],
                                        in0=wabs[:, 0:wid],
                                        in1=wqo[c % 2][:, 0:wid],
                                        op=ALU.subtract)
                        v.tensor_copy(out=czs[:, 3:4],
                                      in_=czs[:, 0:1]).then_inc(sQO.h, 1)
                        sQO.n += 1

                @blk.gpsimd
                def _(g):
                    for c in range(nch):
                        g.wait_ge(sQO.h, b[id(sQO)] + c + 1)
                        g.dma_start(out=dstA[c * 128:(c + 1) * 128, :],
                                    in_=wqo[c % 2][:, 0:wid]
                                    ).then_inc(sSOb[c % 2].h, 16)
                        sSOb[c % 2].n += 16
                    g.wait_ge(sSOb[0].h, sSOb[0].n)
                    g.wait_ge(sSOb[1].h, sSOb[1].n)

            with nc.Block() as blk:
                quant_pass(blk, 0)
            with nc.Block() as blk:
                quant_pass(blk, 1)
            if dbg:
                with nc.Block() as blk:
                    @blk.gpsimd
                    def _(g):
                        sst = SS[51]
                        for i in range(NDC):
                            g.dma_start(out=dbg_t["dbg_x0"][i * 128:(i + 1) * 128, :],
                                        in_=X[i][:, :]).then_inc(sst.h, 16)
                            sst.n += 16
                        g.dma_start(out=dbg_t["dbg_w1q"][0:128, :].bitcast(BF16)[:, 0:T],
                                    in_=w1qA[0:128, 0:T]).then_inc(sst.h, 16)
                        sst.n += 16
                        g.wait_ge(sst.h, sst.n)

        def ln_phase(blk, Xln, psM, psS, rowsA, rowsB, vrow, bcA, bcB,
                     xsq, tmpf):
            sCS, sMM, sVAR, sRSD, sMR, sRST, sBCL, sAPP = SS[13:21]
            b = snapd(sCS, sMM, sVAR, sRSD, sMR, sRST, sBCL, sAPP)

            @blk.scalar
            def _(s):
                for dc in range(NDC):
                    if dc >= 2:
                        s.wait_ge(sMM.h, b[id(sMM)] + 4 * (dc - 1))
                    s.copy(out=Xln[dc][:, :], in_=X[dc][:, :])
                    s.activation(out=xsq[dc % 2][:, :], in_=X[dc][:, :],
                                 func=AF.Square, bias=czs[:, 0:1], scale=1.0)
                    s.activation(out=czs[:, 3:4], in_=czs[:, 0:1],
                                 func=AF.Copy, bias=0.0, scale=1.0
                                 ).then_inc(sCS.h, 1)
                    sCS.n += 1
                s.wait_ge(sVAR.h, b[id(sVAR)] + 1)
                s.activation(out=rowsB[0:1, :], in_=vrow[0:1, :], func=AF.Ln,
                             bias=czs[0:1, 2:3], scale=1.0)
                s.activation(out=rowsB[0:1, :], in_=rowsB[0:1, :], func=AF.Exp,
                             bias=czs[0:1, 0:1], scale=-0.5)
                s.activation(out=czs[:, 3:4], in_=czs[:, 0:1],
                             func=AF.Copy, bias=0.0, scale=1.0
                             ).then_inc(sRSD.h, 1)
                sRSD.n += 1
                s.dma_start(out=bass.AP(rowscr, 0, [[1, 1], [1, T]]),
                            in_=rowsB[0:1, :]).then_inc(sRST.h, 16)
                sRST.n += 16
                s.dma_start(out=bass.AP(rowscr, T, [[1, 1], [1, T]]),
                            in_=vrow[0:1, :]).then_inc(sRST.h, 16)
                sRST.n += 16
                s.wait_ge(sRST.h, sRST.n)

            @blk.tensor
            def _(tn):
                for dc in range(NDC):
                    tn.wait_ge(sCS.h, b[id(sCS)] + dc + 1)
                    for p in range(4):
                        tn.matmul(psM[p][0:1, :], onec[:, 0:1],
                                  Xln[dc][:, p * 512:(p + 1) * 512],
                                  start=(dc == 0), stop=(dc == NDC - 1))
                        tn.matmul(psS[p][0:1, :], onec[:, 0:1],
                                  xsq[dc % 2][:, p * 512:(p + 1) * 512],
                                  start=(dc == 0), stop=(dc == NDC - 1)
                                  ).then_inc(sMM.h, 1)
                        sMM.n += 1
                tn.wait_ge(sMR.h, b[id(sMR)] + 1)
                for p in range(4):
                    tn.matmul(psM[p][:, :], onerf[0:1, :],
                              rowsB[0:1, p * 512:(p + 1) * 512],
                              start=True, stop=True)
                    tn.matmul(psS[p][:, :], onerf[0:1, :],
                              rowsA[0:1, p * 512:(p + 1) * 512],
                              start=True, stop=True).then_inc(sBCL.h, 1)
                    sBCL.n += 1

            @blk.vector
            def _(v):
                def gap():
                    v.tensor_copy(out=czs[:, 3:4], in_=czs[:, 0:1])
                    v.tensor_copy(out=czs[:, 3:4], in_=czs[:, 0:1])
                v.wait_ge(sMM.h, b[id(sMM)] + 4 * NDC)
                for p in range(4):
                    v.tensor_scalar(out=rowsA[0:1, p * 512:(p + 1) * 512],
                                    in0=psM[p][0:1, :], scalar1=1.0 / D,
                                    scalar2=None, op0=ALU.mult)
                gap()
                v.tensor_tensor(out=rowsB[0:1, :], in0=rowsA[0:1, :],
                                in1=rowsA[0:1, :], op=ALU.mult)
                gap()
                for p in range(4):
                    inst = v.scalar_tensor_tensor(
                        out=vrow[0:1, p * 512:(p + 1) * 512],
                        in0=psS[p][0:1, :], scalar=1.0 / D,
                        in1=rowsB[0:1, p * 512:(p + 1) * 512],
                        op0=ALU.mult, op1=ALU.subtract)
                gap()
                v.tensor_copy(out=czs[:, 3:4],
                              in_=czs[:, 0:1]).then_inc(sVAR.h, 1)
                sVAR.n += 1
                inst = inst
                v.wait_ge(sRSD.h, b[id(sRSD)] + 1)
                v.tensor_tensor(out=rowsA[0:1, :], in0=rowsA[0:1, :],
                                in1=rowsB[0:1, :], op=ALU.mult)
                gap()
                nc.vector.sem_inc(sMR.h, 1)
                sMR.n += 1
                v.wait_ge(sBCL.h, b[id(sBCL)] + 4)
                for p in range(4):
                    v.tensor_copy(out=bcA[:, p * 512:(p + 1) * 512],
                                  in_=psM[p][:, :])
                    v.tensor_copy(out=bcB[:, p * 512:(p + 1) * 512],
                                  in_=psS[p][:, :])
                for dc in range(NDC):
                    v.tensor_tensor(out=tmpf[:, :], in0=X[dc][:, :],
                                    in1=bcA[:, :], op=ALU.mult)
                    v.tensor_tensor(out=Xln[dc][:, :], in0=tmpf[:, :],
                                    in1=bcB[:, :], op=ALU.subtract
                                    ).then_inc(sAPP.h, 1)
                    sAPP.n += 1

            @blk.gpsimd
            def _(g):
                g.wait_ge(sAPP.h, b[id(sAPP)] + NDC)

        def ffn_phase(blk, Xln, H, w1s, w2s, psF, evt):
            sL1, sL2, sM1, sRL, sM2, sXA = SS[21:27]
            sL1b = [sL1, SS[65]]
            sL2b = [sL2, SS[66]]
            b = snapd(sL1, sL2, sM1, sRL, sM2, sXA, SS[65], SS[66])

            @blk.sync
            def _(s):
                for th in range(2):
                    for fc in range(NFC):
                        i = th * NFC + fc
                        if i >= 2:
                            s.wait_ge(sM1.h, b[id(sM1)] + 2 * (i - 1))
                        s.dma_start(out=w1s[i % 2][:, :, :],
                                    in_=bass.AP(w1q, 128 * fc,
                                                [[DF, 128], [128 * DF, 8],
                                                 [1, 128]])
                                    ).then_inc(sL1b[i % 2].h, 16)
                        sL1b[i % 2].n += 16
                    for dc2 in range(NDC):
                        i = th * NDC + dc2
                        if i >= 2:
                            s.wait_ge(sM2.h, b[id(sM2)] + 2 * (i - 1))
                        s.dma_start(out=w2s[i % 2][:, :, :],
                                    in_=bass.AP(w2q, 128 * dc2,
                                                [[D, 128], [128 * D, 32],
                                                 [1, 128]])
                                    ).then_inc(sL2b[i % 2].h, 16)
                        sL2b[i % 2].n += 16

            @blk.tensor
            def _(tn):
                for th in range(2):
                    for fc in range(NFC):
                        i = th * NFC + fc
                        tn.wait_ge(sL1b[i % 2].h,
                                   b[id(sL1b[i % 2])] + 16 * (i // 2 + 1))
                        for p in range(2):
                            if i >= 1:
                                tn.wait_ge(sRL.h, b[id(sRL)] + 2 * i - 1 + p)
                            for dc in range(NDC):
                                mi = tn.matmul(
                                    psF[p][:, :], w1s[i % 2][:, dc, :],
                                    Xln[dc][:, th * 1024 + p * 512:
                                            th * 1024 + (p + 1) * 512],
                                    start=(dc == 0), stop=(dc == NDC - 1))
                            mi.then_inc(sM1.h, 1)
                            sM1.n += 1
                    for dc2 in range(NDC):
                        i = th * NDC + dc2
                        tn.wait_ge(sL2b[i % 2].h,
                                   b[id(sL2b[i % 2])] + 16 * (i // 2 + 1))
                        tn.wait_ge(sRL.h, b[id(sRL)] + 64 * (th + 1))
                        for p in range(2):
                            if i >= 1:
                                tn.wait_ge(sXA.h, b[id(sXA)] + 2 * i - 1 + p)
                            for fc in range(NFC):
                                tn.matmul(psF[2 + p][:, :], w2s[i % 2][:, fc, :],
                                          H[fc][:, p * 512:(p + 1) * 512],
                                          start=(fc == 0), stop=False)
                            tn.matmul(psF[2 + p][:, :],
                                      hb2r[0:1, dc2 * 128:(dc2 + 1) * 128],
                                      oner[0:1, :], start=False, stop=True
                                      ).then_inc(sM2.h, 1)
                            sM2.n += 1

            @blk.scalar
            def _(s):
                prevR = [None]
                for th in range(2):
                    if th == 1:
                        s.wait_ge(sM2.h, b[id(sM2)] + 16)
                    for fc in range(NFC):
                        i = th * NFC + fc
                        for p in range(2):
                            s.wait_ge(sM1.h, b[id(sM1)] + 2 * i + p + 1)
                            ri = s.activation(out=H[fc][:, p * 512:(p + 1) * 512],
                                              in_=psF[p][:, :], func=AF.Relu,
                                              bias=b1s[:, fc:fc + 1], scale=1.0)
                            if prevR[0] is not None:
                                ri.then_inc(sRL.h, 1)
                                sRL.n += 1
                            prevR[0] = ri
                    s.activation(out=czs[:, 3:4], in_=czs[:, 0:1],
                                 func=AF.Copy, bias=0.0, scale=1.0
                                 ).then_inc(sRL.h, 1)
                    sRL.n += 1
                    prevR[0] = None

            @blk.vector
            def _(v):
                for th in range(2):
                    for dc2 in range(NDC):
                        i = th * NDC + dc2
                        for p in range(2):
                            v.wait_ge(sM2.h, b[id(sM2)] + 2 * i + p + 1)
                            xsl = X[dc2][:, th * 1024 + p * 512:
                                         th * 1024 + (p + 1) * 512]
                            v.tensor_tensor(out=xsl, in0=psF[2 + p][:, :],
                                            in1=xsl, op=ALU.add).then_inc(sXA.h, 1)
                            sXA.n += 1

            @blk.gpsimd
            def _(g):
                g.wait_ge(sXA.h, b[id(sXA)] + 32)
                evt(g)

        with ExitStack() as sc:
            e2 = sc.enter_context
            Xln = [e2(nc.sbuf_tensor(f"xln{i}", [128, T], BF16))
                   for i in range(NDC)]
            psM = [e2(nc.psum_tensor(f"psm{i}", [128, 512], F32))
                   for i in range(4)]
            psS = [e2(nc.psum_tensor(f"pss{i}", [128, 512], F32))
                   for i in range(4)]

            lnctr = [0]

            def run_ln():
                lnctr[0] += 1
                u = lnctr[0]
                with ExitStack() as scl:
                    e3 = scl.enter_context
                    rowsA = e3(nc.sbuf_tensor(f"rowsA_{u}", [1, T], F32))
                    rowsB = e3(nc.sbuf_tensor(f"rowsB_{u}", [1, T], F32))
                    bcA = e3(nc.sbuf_tensor(f"bcA_{u}", [128, T], F32))
                    bcB = e3(nc.sbuf_tensor(f"bcB_{u}", [128, T], F32))
                    xsq = [e3(nc.sbuf_tensor(f"xsq_{u}_{i}", [128, T], BF16))
                           for i in range(2)]
                    vrow = e3(nc.sbuf_tensor(f"vrow_{u}", [1, T], F32))
                    tmpf = e3(nc.sbuf_tensor(f"tmpfl_{u}", [128, T], F32))
                    with nc.Block() as blk:
                        ln_phase(blk, Xln, psM, psS, rowsA, rowsB, vrow,
                                 bcA, bcB, xsq, tmpf)

            run_ln()
            if dbg:
                with nc.Block() as blk:
                    @blk.gpsimd
                    def _(g):
                        sst = SS[51]
                        for i in range(NDC):
                            g.dma_start(out=dbg_t["dbg_xln1"][i * 128:(i + 1) * 128, :]
                                        .bitcast(BF16)[:, 0:T],
                                        in_=Xln[i][:, :]).then_inc(sst.h, 16)
                            sst.n += 16
                        g.dma_start(out=dbg_rows,
                                    in_=bass.AP(rowscr, 0, [[T, 2], [1, T]])
                                    ).then_inc(sst.h, 16)
                        sst.n += 16
                        g.wait_ge(sst.h, sst.n)
            with ExitStack() as scf:
                e3 = scf.enter_context
                H = [e3(nc.sbuf_tensor(f"ha_{i}", [128, 1024], BF16))
                     for i in range(NFC)]
                w1s = [e3(nc.sbuf_tensor(f"w1s{i}", [128, 8, 128], BF16))
                       for i in range(2)]
                w2s = [e3(nc.sbuf_tensor(f"w2s{i}", [128, 32, 128], BF16))
                       for i in range(2)]

                def evt1(g):
                    if dbg:
                        sst = SS[51]
                        for i in range(NDC):
                            g.dma_start(out=dbg_t["dbg_ffn1"][i * 128:(i + 1) * 128, :],
                                        in_=X[i][:, :]).then_inc(sst.h, 16)
                            sst.n += 16
                        g.wait_ge(sst.h, sst.n)
                with nc.Block() as blk:
                    ffn_phase(blk, Xln, H, w1s, w2s, psM, evt1)

            run_ln()
            if dbg:
                with nc.Block() as blk:
                    @blk.gpsimd
                    def _(g):
                        sst = SS[51]
                        for i in range(NDC):
                            g.dma_start(out=dbg_t["dbg_ln2"][i * 128:(i + 1) * 128, :]
                                        .bitcast(BF16)[:, 0:T],
                                        in_=Xln[i][:, :]).then_inc(sst.h, 16)
                            sst.n += 16
                        g.wait_ge(sst.h, sst.n)

            with ExitStack() as scp:
                e3 = scp.enter_context
                wbf = [e3(nc.sbuf_tensor(f"wbf{i}", [128, D], BF16))
                       for i in range(NDC)]
                wtm = [e3(nc.sbuf_tensor(f"wtm{i}", [128, D], F32))
                       for i in range(2)]
                otile = [e3(nc.sbuf_tensor(f"ot{i}", [128, 512], BF16))
                         for i in range(2)]
                psP = [psM[0], psM[1]]
                sWL, sWC, sPM, sPE, sPS = SS[27:32]
                sWCF = SS[58]
                sWLb = [sWL, SS[69]]
                sPSb = [sPS, SS[70]]
                b = snapd(sWL, sWC, sPM, sPE, sPS, sWCF, SS[69], SS[70])
                with nc.Block() as blk:
                    @blk.sync
                    def _(s):
                        for wi, wsrc in enumerate([wqT, wkT, wvT]):
                            for c in range(NDC):
                                i = wi * NDC + c
                                if i >= 2:
                                    s.wait_ge(sWC.h, b[id(sWC)] + i - 1)
                                s.dma_start(out=wtm[c % 2][:, :],
                                            in_=wsrc[c * 128:(c + 1) * 128, :]
                                            ).then_inc(sWLb[i % 2].h, 16)
                                sWLb[i % 2].n += 16

                    @blk.vector
                    def _(v):
                        for wi in range(3):
                            for c in range(NDC):
                                i = wi * NDC + c
                                v.wait_ge(sWLb[i % 2].h,
                                           b[id(sWLb[i % 2])] + 16 * (i // 2 + 1))
                                if wi > 0:
                                    v.wait_ge(sPM.h, b[id(sPM)] + wi * 32)
                                v.tensor_copy(out=wbf[c][:, :],
                                              in_=wtm[c % 2][:, :]
                                              ).then_inc(sWC.h, 1)
                                sWC.n += 1
                            v.tensor_copy(out=czs[:, 3:4],
                                          in_=czs[:, 0:1]).then_inc(sWCF.h, 1)
                            sWCF.n += 1
                            for j in range(32):
                                jj = wi * 32 + j
                                v.wait_ge(sPM.h, b[id(sPM)] + jj + 1)
                                if jj >= 2:
                                    v.wait_ge(sPSb[jj % 2].h,
                                              b[id(sPSb[jj % 2])]
                                              + 16 * ((jj - 2) // 2 + 1))
                                if wi == 0:
                                    oc = j // 4
                                    ei = v.tensor_scalar(out=otile[j % 2][:, :],
                                                         in0=psP[j % 2][:, :],
                                                         scalar1=bqs[:, oc:oc + 1],
                                                         scalar2=None, op0=ALU.add)
                                elif wi == 1:
                                    oc = j // 4
                                    ei = v.tensor_scalar(out=otile[j % 2][:, :],
                                                         in0=psP[j % 2][:, :],
                                                         scalar1=0.125,
                                                         scalar2=bks[:, oc:oc + 1],
                                                         op0=ALU.mult, op1=ALU.add)
                                else:
                                    ei = v.tensor_copy(out=otile[j % 2][:, :],
                                                       in_=psP[j % 2][:, :])
                                v.tensor_copy(out=czs[:, 3:4],
                                              in_=czs[:, 0:1]).then_inc(sPE.h, 1)
                                sPE.n += 1
                                ei = ei

                    @blk.tensor
                    def _(tn):
                        tn.wait_ge(sWCF.h, b[id(sWCF)] + 1)
                        for wi in range(3):
                            if wi > 0:
                                tn.wait_ge(sWCF.h, b[id(sWCF)] + wi + 1)
                            for j in range(32):
                                jj = wi * 32 + j
                                if j >= 2:
                                    tn.wait_ge(sPE.h, b[id(sPE)] + jj - 1)
                                if wi < 2:
                                    oc, p = divmod(j, 4)
                                    for ic in range(NDC):
                                        mi = tn.matmul(
                                            psP[j % 2][:, :],
                                            wbf[ic][:, oc * 128:(oc + 1) * 128],
                                            Xln[ic][:, p * 512:(p + 1) * 512],
                                            start=(ic == 0), stop=(ic == NDC - 1))
                                else:
                                    tc, p = divmod(j, 2)
                                    for dc in range(NDC):
                                        tn.matmul(
                                            psP[j % 2][:, :],
                                            Xln[dc][:, tc * 128:(tc + 1) * 128],
                                            wbf[dc][:, p * 512:(p + 1) * 512],
                                            start=(dc == 0), stop=False)
                                    mi = tn.matmul(psP[j % 2][:, :],
                                                   oner[0:1, 0:128],
                                                   bvr[0:1, p * 512:(p + 1) * 512],
                                                   start=False, stop=True)
                                mi.then_inc(sPM.h, 1)
                                sPM.n += 1

                    @blk.scalar
                    def _(s):
                        for wi in range(3):
                            for j in range(32):
                                jj = wi * 32 + j
                                s.wait_ge(sPE.h, b[id(sPE)] + jj + 1)
                                if wi == 0:
                                    oc, p = divmod(j, 4)
                                    dst = qdA[oc * 128:(oc + 1) * 128,
                                              p * 512:(p + 1) * 512]
                                elif wi == 1:
                                    oc, p = divmod(j, 4)
                                    dst = kdA[oc * 128:(oc + 1) * 128,
                                              p * 512:(p + 1) * 512]
                                else:
                                    tc, p = divmod(j, 2)
                                    dst = vdA[tc * 128:(tc + 1) * 128,
                                              p * 512:(p + 1) * 512]
                                s.dma_start(out=dst, in_=otile[j % 2][:, :]
                                            ).then_inc(sPSb[jj % 2].h, 16)
                                sPSb[jj % 2].n += 16
                        s.wait_ge(sPSb[0].h, sPSb[0].n)
                        s.wait_ge(sPSb[1].h, sPSb[1].n)

        with ExitStack() as sc:
            e2 = sc.enter_context
            with ExitStack() as sca:
                e3 = sca.enter_context
                Qp = [e3(nc.sbuf_tensor(f"qp{i}", [128, T], BF16))
                      for i in range(2)]
                Kp = [e3(nc.sbuf_tensor(f"kp{i}", [128, T], BF16))
                      for i in range(2)]
                Vpe = [e3(nc.sbuf_tensor(f"vpe{i}", [128, 8 * 128], BF16))
                       for i in range(2)]
                Vpo = [e3(nc.sbuf_tensor(f"vpo{i}", [128, 8 * 128], BF16))
                       for i in range(2)]
                scoA = [e3(nc.sbuf_tensor(f"sco{i}", [128, T], F32))
                        for i in range(2)]
                bndE = [e3(nc.sbuf_tensor(f"bnd{i}", [128, T], BF16))
                        for i in range(2)]
                bndF = [e3(nc.sbuf_tensor(f"bnf{i}", [128, T], FP16))
                        for i in range(2)]
                Ssb = [e3(nc.sbuf_tensor(f"ssb{i}", [128, 2047], FP16))
                       for i in range(2)]
                ATsP = e3(nc.sbuf_tensor("atsp", [128, 2048], F32))
                DTsP = e3(nc.sbuf_tensor("dtsp", [128, 2304], F32))
                Adg = [e3(nc.sbuf_tensor(f"adg{i}", [128, 2304], BF16))
                       for i in range(2)]
                HOp = [e3(nc.sbuf_tensor(f"hop{i}", [128, T], BF16))
                       for i in range(2)]
                rsc = e3(nc.sbuf_tensor("rsc", [128, 4], F32))
                PS = [e3(nc.psum_tensor(f"aps{i}", [128, 512], F32))
                      for i in range(2)]
                psT = [e3(nc.psum_tensor(f"apt{i}", [128, 512], F32))
                       for i in range(4)]
                psO = [e3(nc.psum_tensor(f"apo{i}", [128, 128], F32))
                       for i in range(2)]
                (sQPL, sSR, sSRE, sSST, sBRD, sQK, sBAD, sEXP, sNRM, sAST,
                 sDRD, sTRG, sGEV, sAV, sOEV) = SS[32:47]
                sHOD = SS[47]
                sSSTb = [sSST, SS[59]]
                sBRDb = [sBRD, SS[60]]
                sASTb = [sAST, SS[61]]
                sDRDb = [sDRD, SS[62]]
                sQPLb = [sQPL, SS[63]]
                sHODb = [sHOD, SS[64]]
                b = snapd(sQPL, sSR, sSRE, sSST, sBRD, sQK, sBAD, sEXP,
                          sNRM, sAST, sDRD, sTRG, sGEV, sAV, sOEV, sHOD,
                          SS[59], SS[60], SS[61], SS[62], SS[63], SS[64])

                def consumed_wait(eng, q):
                    prev = q - 2
                    if prev < 0:
                        return
                    gp, jp = divmod(prev, 8)
                    if jp < 4:
                        eng.wait_ge(sSRE.h, b[id(sSRE)] + 4 * gp + jp + 1)
                    else:
                        eng.wait_ge(sBAD.h, b[id(sBAD)] + 4 * gp + (jp - 4) + 1)

                def aw0_of(t):
                    w0 = 128 * (15 - t)
                    return w0 - 128 * ((15 - t) % 2)

                with nc.Block() as blk:
                    @blk.sync
                    def _(s):
                        for g in range(NG):
                            h, t = divmod(g, NT)
                            if t == 0 and h % 2 == 0:
                                hp = h // 2
                                if hp >= 1:
                                    s.wait_ge(sOEV.h, b[id(sOEV)] + 32 * hp)
                                    sh = sHODb[(hp - 1) % 2]
                                    s.dma_start(
                                        out=bass.AP(hod, (hp - 1) * 128 * T,
                                                    [[T, 128], [1, T]]),
                                        in_=HOp[(hp - 1) % 2][:, :]
                                    ).then_inc(sh.h, 16)
                                    sh.n += 16
                                if h >= 4:
                                    s.wait_ge(sQK.h, b[id(sQK)] + 64 * (h - 2))
                                    s.wait_ge(sAV.h, b[id(sAV)] + 16 * (h - 2))
                                sq = sQPLb[hp % 2]
                                s.dma_start(out=Qp[hp % 2][:, :],
                                            in_=qdA[h * 64:(h + 2) * 64, :]
                                            ).then_inc(sq.h, 16)
                                s.dma_start(out=Kp[hp % 2][:, :],
                                            in_=kdA[h * 64:(h + 2) * 64, :]
                                            ).then_inc(sq.h, 16)
                                for so, vbuf in [(0, Vpe), (1, Vpo)]:
                                    s.dma_start(
                                        out=vbuf[hp % 2][:, :].rearrange(
                                            "p (c k) -> p c k", k=128),
                                        in_=bass.AP(vd, so * D + h * 64,
                                                    [[2 * D, 128],
                                                     [256 * D, 8], [1, 128]])
                                    ).then_inc(sq.h, 16)
                                sq.n += 64
                            s.wait_ge(sSSTb[g % 2].h,
                                      b[id(sSSTb[g % 2])] + 16 * (g // 2 + 1))
                            if g >= 2:
                                s.wait_ge(sNRM.h, b[id(sNRM)] + g - 1)
                            s.dma_start(out=bndF[g % 2][:, :],
                                        in_=bass.AP(spr[h % 2],
                                                    t * 128 * (WS - 1) + 2047,
                                                    [[WS - 1, 128], [1, 2048]])
                                        ).then_inc(sBRDb[g % 2].h, 16)
                            sBRDb[g % 2].n += 16
                            s.wait_ge(sASTb[g % 2].h,
                                      b[id(sASTb[g % 2])] + 16 * (g // 2 + 1))
                            if g >= 2:
                                s.wait_ge(sTRG.h, b[id(sTRG)] + 5 * (g - 1))
                            s.dma_start(out=Adg[g % 2][:, :],
                                        in_=bass.AP(esc[h % 2],
                                                    t * 128 * (WA + 1)
                                                    + aw0_of(t) - 1791,
                                                    [[WA + 1, 128], [1, 2304]])
                                        ).then_inc(sDRDb[g % 2].h, 16)
                            sDRDb[g % 2].n += 16
                        s.wait_ge(sOEV.h, b[id(sOEV)] + 32 * 8)
                        s.dma_start(out=bass.AP(hod, 7 * 128 * T,
                                                [[T, 128], [1, T]]),
                                    in_=HOp[7 % 2][:, :]).then_inc(sHODb[1].h, 16)
                        sHODb[1].n += 16
                        s.wait_ge(sHODb[0].h, sHODb[0].n)
                        s.wait_ge(sHODb[1].h, sHODb[1].n)

                    @blk.scalar
                    def _(s):
                        for g in range(NG):
                            h, t = divmod(g, NT)
                            s.wait_ge(sSRE.h, b[id(sSRE)] + 4 * g + 4)
                            if h >= 2:
                                gp = (h - 2) * NT + t
                                s.wait_ge(sBRDb[gp % 2].h,
                                          b[id(sBRDb[gp % 2])] + 16 * (gp // 2 + 1))
                            s.dma_start(out=bass.AP(spr[h % 2],
                                                    t * 128 * WS + 1024,
                                                    [[WS, 128], [1, 2047]]),
                                        in_=Ssb[g % 2][:, :]
                                        ).then_inc(sSSTb[g % 2].h, 16)
                            sSSTb[g % 2].n += 16
                            s.wait_ge(sBAD.h, b[id(sBAD)] + 4 * g + 4)
                            s.activation(out=bndE[g % 2][:, :],
                                         in_=scoA[g % 2][:, :], func=AF.Exp,
                                         bias=czs[:, 1:2], scale=1.0,
                                         accum_out=rsc[:, g % 2:g % 2 + 1])
                            s.activation(out=czs[:, 3:4], in_=czs[:, 0:1],
                                         func=AF.Copy, bias=0.0, scale=1.0
                                         ).then_inc(sEXP.h, 1)
                            sEXP.n += 1
                            s.wait_ge(sNRM.h, b[id(sNRM)] + g + 1)
                            if h >= 2:
                                gp = (h - 2) * NT + t
                                s.wait_ge(sDRDb[gp % 2].h,
                                          b[id(sDRDb[gp % 2])] + 16 * (gp // 2 + 1))
                            Av = scoA[g % 2][:, :].bitcast(BF16)[:, 0:T]
                            s.dma_start(out=bass.AP(esc[h % 2],
                                                    t * 128 * WA + 256,
                                                    [[WA, 128], [1, 2048]]),
                                        in_=Av).then_inc(sASTb[g % 2].h, 16)
                            sASTb[g % 2].n += 16

                    @blk.tensor
                    def _(tn):
                        for g in range(NG + 1):
                            if g >= 1:
                                g1 = g - 1
                                h1, t1 = divmod(g1, NT)
                                off1 = 64 * (h1 % 2)
                                hp21 = (h1 // 2) % 2
                                j2 = g1 % 2
                                Af32 = scoA[g1 % 2][:, 0:1024]
                                tn.wait_ge(sNRM.h, b[id(sNRM)] + g1 + 1)
                                for grp in range(2):
                                    gg = 5 * g1 + grp
                                    if gg >= 4:
                                        tn.wait_ge(sGEV.h, b[id(sGEV)] + gg - 3)
                                    for bb in range(4):
                                        blkc = grp * 4 + bb
                                        ti = tn.transpose(
                                            psT[gg % 4][:, bb * 128:(bb + 1) * 128],
                                            Af32[:, blkc * 128:(blkc + 1) * 128],
                                            identF[:, :])
                                    ti.then_inc(sTRG.h, 1)
                                    sTRG.n += 1
                                tn.wait_ge(sDRDb[g1 % 2].h,
                                           b[id(sDRDb[g1 % 2])]
                                           + 16 * (g1 // 2 + 1))
                                AdgF = Adg[g1 % 2][:, :].bitcast(F32)[:, 0:1152]
                                for grp in range(3):
                                    gg = 5 * g1 + 2 + grp
                                    nb = 4 if grp < 2 else 1
                                    if gg >= 4:
                                        tn.wait_ge(sGEV.h, b[id(sGEV)] + gg - 3)
                                    for bb in range(nb):
                                        blkc = grp * 4 + bb
                                        ti = tn.transpose(
                                            psT[gg % 4][:, bb * 128:(bb + 1) * 128],
                                            AdgF[:, blkc * 128:(blkc + 1) * 128],
                                            identF[:, :])
                                    ti.then_inc(sTRG.h, 1)
                                    sTRG.n += 1
                                tn.wait_ge(sGEV.h, b[id(sGEV)] + 5 * g1 + 5)
                                if g1 >= 2:
                                    tn.wait_ge(sOEV.h, b[id(sOEV)] + g1 - 1)
                                ATb = ATsP[:, :].bitcast(BF16)
                                ATr = ATb.rearrange(
                                    "p (j c l s) -> p j c s l",
                                    j=2, c=8, l=128, s=2)
                                first = True
                                for c in range(8):
                                    for sx, vbuf in [(0, Vpe), (1, Vpo)]:
                                        rhs = ATr[:, j2, c, sx, :]
                                        tn.matmul(
                                            psO[g1 % 2][off1:off1 + 64, :],
                                            vbuf[hp21][:, c * 128 + off1:
                                                       c * 128 + off1 + 64],
                                            rhs, start=first, stop=False)
                                        first = False
                                DTb = DTsP[:, :].bitcast(BF16)
                                DTr = DTb.rearrange(
                                    "p (j c l s) -> p j c s l",
                                    j=2, c=9, l=128, s=2)
                                for cp in range(9):
                                    for sx, rvt in [(0, RVsbE), (1, RVsbO)]:
                                        gcp = aw0_of(t1) // 256 + cp
                                        rhs = DTr[:, j2, cp, sx, :]
                                        mi = tn.matmul(
                                            psO[g1 % 2][off1:off1 + 64, :],
                                            rvt[:, gcp * 64:(gcp + 1) * 64],
                                            rhs, start=False,
                                            stop=(cp == 8 and sx == 1))
                                mi.then_inc(sAV.h, 1)
                                sAV.n += 1
                            if g < NG:
                                h, t = divmod(g, NT)
                                off = 64 * (h % 2)
                                hp2 = (h // 2) % 2
                                tbl = tblA if t < 8 else tblB
                                lhs = Qp[hp2][off:off + 64,
                                              t * 128:(t + 1) * 128]
                                hp_ = h // 2
                                tn.wait_ge(sQPLb[hp_ % 2].h,
                                           b[id(sQPLb[hp_ % 2])]
                                           + 64 * (hp_ // 2 + 1))
                                for p, (o, w) in enumerate(PIECES):
                                    consumed_wait(tn, 8 * g + p)
                                    tn.matmul(PS[p % 2][:, 0:w], lhs,
                                              tbl[off:off + 64, o:o + w],
                                              start=True, stop=True
                                              ).then_inc(sSR.h, 1)
                                    sSR.n += 1
                                for p in range(4):
                                    consumed_wait(tn, 8 * g + 4 + p)
                                    tn.matmul(PS[p % 2][:, :], lhs,
                                              Kp[hp2][off:off + 64,
                                                      p * 512:(p + 1) * 512],
                                              start=True, stop=True
                                              ).then_inc(sQK.h, 1)
                                    sQK.n += 1
                    @blk.vector
                    def _(v):
                        for g in range(NG + 1):
                            if g >= 1:
                                g1 = g - 1
                                h1, t1 = divmod(g1, NT)
                                off1 = 64 * (h1 % 2)
                                j2 = g1 % 2
                                pend = [None]

                                def evac_inc(inst):
                                    if pend[0] is not None:
                                        inst.then_inc(sGEV.h, 1)
                                        sGEV.n += 1
                                    pend[0] = inst
                                for grp in range(2):
                                    gg = 5 * g1 + grp
                                    v.wait_ge(sTRG.h, b[id(sTRG)] + gg + 1)
                                    ei = v.tensor_copy(
                                        out=ATsP[:, j2 * 1024 + grp * 512:
                                                 j2 * 1024 + (grp + 1) * 512],
                                        in_=psT[gg % 4][:, :])
                                    evac_inc(ei)
                                for grp in range(3):
                                    gg = 5 * g1 + 2 + grp
                                    nb = 4 if grp < 2 else 1
                                    v.wait_ge(sTRG.h, b[id(sTRG)] + gg + 1)
                                    ei = v.tensor_copy(
                                        out=DTsP[:, j2 * 1152 + grp * 512:
                                                 j2 * 1152 + grp * 512 + nb * 128],
                                        in_=psT[gg % 4][:, 0:nb * 128])
                                    evac_inc(ei)
                                v.tensor_copy(out=czs[:, 3:4],
                                              in_=czs[:, 0:1]).then_inc(sGEV.h, 1)
                                sGEV.n += 1
                                hp1 = h1 // 2
                                v.wait_ge(sAV.h, b[id(sAV)] + g1 + 1)
                                if t1 <= 1 and hp1 >= 2:
                                    sh = sHODb[hp1 % 2]
                                    v.wait_ge(sh.h, b[id(sh)]
                                              + 16 * ((hp1 - 2) // 2 + 1))
                                v.tensor_copy(
                                    out=HOp[hp1 % 2][off1:off1 + 64,
                                                     t1 * 128:t1 * 128 + 128],
                                    in_=psO[g1 % 2][off1:off1 + 64, :]
                                    ).then_inc(sOEV.h, 1)
                                sOEV.n += 1
                            if g < NG:
                                h, t = divmod(g, NT)
                                prevE = [None]

                                def sh_inc(inst, sem, store):
                                    if store[0] is not None:
                                        inst.then_inc(sem.h, 1)
                                        sem.n += 1
                                    store[0] = inst
                                for p, (o, w) in enumerate(PIECES):
                                    v.wait_ge(sSR.h, b[id(sSR)] + 4 * g + p + 1)
                                    if p == 0 and g >= 2:
                                        v.wait_ge(sSSTb[g % 2].h,
                                                  b[id(sSSTb[g % 2])]
                                                  + 16 * ((g - 2) // 2 + 1))
                                    ei = v.tensor_copy(out=Ssb[g % 2][:, o:o + w],
                                                       in_=PS[p % 2][:, 0:w])
                                    sh_inc(ei, sSRE, prevE)
                                di0 = v.tensor_copy(out=czs[:, 3:4],
                                                    in_=czs[:, 0:1])
                                sh_inc(di0, sSRE, prevE)
                                prevE[0] = None
                                prevB = [None]
                                for p in range(4):
                                    v.wait_ge(sQK.h, b[id(sQK)] + 4 * g + p + 1)
                                    if p == 0:
                                        v.wait_ge(sBRDb[g % 2].h,
                                                  b[id(sBRDb[g % 2])]
                                                  + 16 * (g // 2 + 1))
                                        if g >= 2:
                                            v.wait_ge(sASTb[g % 2].h,
                                                      b[id(sASTb[g % 2])]
                                                      + 16 * ((g - 2) // 2 + 1))
                                            v.wait_ge(sTRG.h,
                                                      b[id(sTRG)] + 5 * (g - 2) + 2)
                                    bi = v.tensor_tensor(
                                        out=scoA[g % 2][:, p * 512:(p + 1) * 512],
                                        in0=PS[p % 2][:, :],
                                        in1=bndF[g % 2][:, p * 512:(p + 1) * 512],
                                        op=ALU.add)
                                    sh_inc(bi, sBAD, prevB)
                                di = v.tensor_copy(out=czs[:, 3:4],
                                                   in_=czs[:, 0:1])
                                sh_inc(di, sBAD, prevB)
                                v.wait_ge(sEXP.h, b[id(sEXP)] + g + 1)
                                v.reciprocal(out=rsc[:, 2 + g % 2:3 + g % 2],
                                             in_=rsc[:, g % 2:g % 2 + 1])
                                v.tensor_copy(out=rsc[:, 3 - g % 2:4 - g % 2],
                                              in_=czs[:, 0:1])
                                v.tensor_copy(out=czs[:, 3:4], in_=czs[:, 0:1])
                                v.tensor_scalar(
                                    out=scoA[g % 2][:, :].bitcast(BF16)[:, 0:T],
                                    in0=bndE[g % 2][:, :],
                                    scalar1=rsc[:, 2 + g % 2:3 + g % 2],
                                    scalar2=None, op0=ALU.mult)
                                v.tensor_copy(out=czs[:, 3:4],
                                              in_=czs[:, 0:1]).then_inc(sNRM.h, 1)
                                sNRM.n += 1
                    @blk.gpsimd
                    def _(g_):
                        g_.wait_ge(sOEV.h, b[id(sOEV)] + 16 * NH)

            with ExitStack() as scw:
                e3 = scw.enter_context
                HOsb = [e3(nc.sbuf_tensor(f"hosb{i}", [128, T], BF16))
                        for i in range(NDC)]
                wbf2 = [e3(nc.sbuf_tensor(f"wo{i}", [128, D], BF16))
                        for i in range(NDC)]
                wtm2 = [e3(nc.sbuf_tensor(f"wotm{i}", [128, D], F32))
                        for i in range(2)]
                psW = [e3(nc.psum_tensor(f"psw{i}", [128, 512], F32))
                       for i in range(2)]
                sWL, sWC, sWM, sWA = SS[53:57]
                sHL = SS[57]
                sWLb = [sWL, SS[71]]
                b = snapd(sWL, sWC, sWM, sWA, sHL, SS[71])
                with nc.Block() as blk:
                    @blk.sync
                    def _(s):
                        for c in range(NDC):
                            s.dma_start(out=HOsb[c][:, :],
                                        in_=bass.AP(hod, c * 128 * T,
                                                    [[T, 128], [1, T]])
                                        ).then_inc(sHL.h, 16)
                            sHL.n += 16
                        for c in range(NDC):
                            if c >= 2:
                                s.wait_ge(sWC.h, b[id(sWC)] + c - 1)
                            s.dma_start(out=wtm2[c % 2][:, :],
                                        in_=woT[c * 128:(c + 1) * 128, :]
                                        ).then_inc(sWLb[c % 2].h, 16)
                            sWLb[c % 2].n += 16
                        s.wait_ge(sHL.h, b[id(sHL)] + 16 * NDC)

                    @blk.vector
                    def _(v):
                        for c in range(NDC):
                            v.wait_ge(sWLb[c % 2].h,
                                      b[id(sWLb[c % 2])] + 16 * (c // 2 + 1))
                            v.tensor_copy(out=wbf2[c][:, :],
                                          in_=wtm2[c % 2][:, :]).then_inc(sWC.h, 1)
                            sWC.n += 1
                with nc.Block() as blk:
                    @blk.vector
                    def _(v):
                        for j in range(32):
                            oc, p = divmod(j, 4)
                            v.wait_ge(sWM.h, b[id(sWM)] + j + 1)
                            xsl = X[oc][:, p * 512:(p + 1) * 512]
                            v.scalar_tensor_tensor(out=xsl, in0=psW[j % 2][:, :],
                                                   scalar=bos[:, oc:oc + 1],
                                                   in1=xsl, op0=ALU.add,
                                                   op1=ALU.add).then_inc(sWA.h, 1)
                            sWA.n += 1

                    @blk.tensor
                    def _(tn):
                        for j in range(32):
                            oc, p = divmod(j, 4)
                            if j >= 2:
                                tn.wait_ge(sWA.h, b[id(sWA)] + j)
                            for cc in range(NDC):
                                mi = tn.matmul(
                                    psW[j % 2][:, :],
                                    wbf2[cc][:, oc * 128:(oc + 1) * 128],
                                    HOsb[cc][:, p * 512:(p + 1) * 512],
                                    start=(cc == 0), stop=(cc == NDC - 1))
                            mi.then_inc(sWM.h, 1)
                            sWM.n += 1

                    @blk.gpsimd
                    def _(g):
                        g.wait_ge(sWA.h, b[id(sWA)] + 32)
                        if dbg:
                            sst = SS[51]
                            for i in range(NDC):
                                g.dma_start(
                                    out=dbg_t["dbg_attn"][i * 128:(i + 1) * 128, :],
                                    in_=X[i][:, :]).then_inc(sst.h, 16)
                                sst.n += 16
                            for i in range(NDC):
                                g.dma_start(
                                    out=dbg_t["dbg_ho"][i * 128:(i + 1) * 128, :]
                                    .bitcast(BF16)[:, 0:T],
                                    in_=HOsb[i][:, :]).then_inc(sst.h, 16)
                                sst.n += 16
                            g.wait_ge(sst.h, sst.n)

        with ExitStack() as sc:
            e2 = sc.enter_context
            Xln = [e2(nc.sbuf_tensor(f"xln3{i}", [128, T], BF16))
                   for i in range(NDC)]
            psM = [e2(nc.psum_tensor(f"psm3{i}", [128, 512], F32))
                   for i in range(4)]
            psS = [e2(nc.psum_tensor(f"pss3{i}", [128, 512], F32))
                   for i in range(4)]
            with ExitStack() as scl:
                e3 = scl.enter_context
                rowsA = e3(nc.sbuf_tensor("rowsA3", [1, T], F32))
                rowsB = e3(nc.sbuf_tensor("rowsB3", [1, T], F32))
                bcA = e3(nc.sbuf_tensor("bcA3", [128, T], F32))
                bcB = e3(nc.sbuf_tensor("bcB3", [128, T], F32))
                xsq = [e3(nc.sbuf_tensor(f"xsq3{i}", [128, T], BF16))
                       for i in range(2)]
                vrow = e3(nc.sbuf_tensor("vrow3", [1, T], F32))
                tmpf = e3(nc.sbuf_tensor("tmpf3", [128, T], F32))
                with nc.Block() as blk:
                    ln_phase(blk, Xln, psM, psS, rowsA, rowsB, vrow, bcA, bcB,
                             xsq, tmpf)
            with ExitStack() as scf:
                e3 = scf.enter_context
                H = [e3(nc.sbuf_tensor(f"hb_{i}", [128, 1024], BF16))
                     for i in range(NFC)]
                w1s = [e3(nc.sbuf_tensor(f"w1sb_{i}", [128, 8, 128], BF16))
                       for i in range(2)]
                w2s = [e3(nc.sbuf_tensor(f"w2sb_{i}", [128, 32, 128], BF16))
                       for i in range(2)]
                with nc.Block() as blk:
                    ffn_phase(blk, Xln, H, w1s, w2s, psM, lambda g: None)

        with nc.Block() as blk:
            @blk.gpsimd
            def _(g):
                sst = SS[51]
                for i in range(NDC):
                    g.dma_start(out=yT[i * 128:(i + 1) * 128, :],
                                in_=X[i][:, :]).then_inc(sst.h, 16)
                    sst.n += 16
                g.wait_ge(sst.h, sst.n)

    return nc


def _prep_inputs(inputs):
    f = np.float32
    x = np.asarray(inputs["L0_attn_x"], f)

    def chunks(bv, n):
        return np.ascontiguousarray(np.asarray(bv, f).reshape(n, 128).T)

    relv = np.asarray(inputs["L0_attn_relative_v"], f)
    common = {
        "wqT": np.ascontiguousarray(np.asarray(inputs["L0_attn_Wq"], f).T),
        "wkT": np.ascontiguousarray(np.asarray(inputs["L0_attn_Wk"], f).T),
        "wvT": np.ascontiguousarray(np.asarray(inputs["L0_attn_Wv"], f).T),
        "woT": np.ascontiguousarray(np.asarray(inputs["L0_attn_Wo"], f).T),
        "w1T": np.ascontiguousarray(np.asarray(inputs["L0_ffn_W1"], f).T),
        "w2T": np.ascontiguousarray(np.asarray(inputs["L0_ffn_W2"], f).T),
        "bq_c": chunks(inputs["L0_attn_bq"], NDC),
        "bk_c": chunks(inputs["L0_attn_bk"], NDC),
        "bo_c": chunks(inputs["L0_attn_bo"], NDC),
        "b1_c": chunks(inputs["L0_ffn_b1"], NFC),
        "bv_r": np.asarray(inputs["L0_attn_bv"], f).reshape(1, D),
        "b2_r": np.asarray(inputs["L0_ffn_b2"], f).reshape(1, D),
        "relkT": np.ascontiguousarray(
            np.asarray(inputs["L0_attn_relative_k"], f).T),
        "rvpad": np.ascontiguousarray(
            relv[np.clip(np.arange(4096) - 1024, 0, 2046)]),
    }
    in_maps = []
    for c in range(8):
        m = dict(common)
        m["xT"] = np.ascontiguousarray(x[c].T)
        in_maps.append(m)
    return in_maps


def kernel(**inputs):
    in_maps = _prep_inputs(inputs)
    nc = build_nc(dbg=False)
    res = run_bass_kernel_spmd(nc, in_maps, core_ids=list(range(8)))
    out = np.stack([np.ascontiguousarray(r["yT"].T) for r in res.results])
    return out.astype(np.float32)


# version 09
# speedup vs baseline: 1.4795x; 1.4295x over previous
import numpy as np
from contextlib import ExitStack

import concourse.bass as bass
import concourse.mybir as mybir
from concourse.bass_utils import run_bass_kernel_spmd

F32 = mybir.dt.float32
BF16 = mybir.dt.bfloat16
FP16 = mybir.dt.float16
AF = mybir.ActivationFunctionType
ALU = mybir.AluOpType
AX = mybir.AxisListType

T = 2048
D = 1024
NH = 16
DK = 64
DF = 4096
NDC = D // 128
NFC = DF // 128
NT = T // 128
CSH = 32.0
EPS = 1e-5
WS = 4095
WA = 2560
NG = NH * NT
PIECES = [(0, 512), (512, 512), (1024, 512), (1536, 511)]


class Ctr:
    def __init__(self, h):
        self.h = h
        self.n = 0


def build_nc(dbg=False):
    nc = bass.Bass("TRN2", target_bir_lowering=False, detect_race_conditions=False)

    xT = nc.dram_tensor("xT", [D, T], F32, kind="ExternalInput").ap()
    wqT = nc.dram_tensor("wqT", [D, D], F32, kind="ExternalInput").ap()
    wkT = nc.dram_tensor("wkT", [D, D], F32, kind="ExternalInput").ap()
    wvT = nc.dram_tensor("wvT", [D, D], F32, kind="ExternalInput").ap()
    woT = nc.dram_tensor("woT", [D, D], F32, kind="ExternalInput").ap()
    w1T = nc.dram_tensor("w1T", [D, DF], F32, kind="ExternalInput").ap()
    w2T = nc.dram_tensor("w2T", [DF, D], F32, kind="ExternalInput").ap()
    bq_c = nc.dram_tensor("bq_c", [128, NDC], F32, kind="ExternalInput").ap()
    bk_c = nc.dram_tensor("bk_c", [128, NDC], F32, kind="ExternalInput").ap()
    bo_c = nc.dram_tensor("bo_c", [128, NDC], F32, kind="ExternalInput").ap()
    b1_c = nc.dram_tensor("b1_c", [128, NFC], F32, kind="ExternalInput").ap()
    bv_r = nc.dram_tensor("bv_r", [1, D], F32, kind="ExternalInput").ap()
    b2_r = nc.dram_tensor("b2_r", [1, D], F32, kind="ExternalInput").ap()
    relkT = nc.dram_tensor("relkT", [64, 2047], F32, kind="ExternalInput").ap()
    rvpad = nc.dram_tensor("rvpad", [4096, 64], F32, kind="ExternalInput").ap()
    yT = nc.dram_tensor("yT", [D, T], F32, kind="ExternalOutput").ap()
    dbg_t = {}
    if dbg:
        for name in ["dbg_ffn1", "dbg_ln2", "dbg_attn", "dbg_ho",
                     "dbg_x0", "dbg_xln1", "dbg_w1q"]:
            dbg_t[name] = nc.dram_tensor(name, [D, T], F32,
                                         kind="ExternalOutput").ap()
    qd = nc.dram_tensor("qd", [D, T], BF16, kind="Internal")
    kd = nc.dram_tensor("kd", [D, T], BF16, kind="Internal")
    vd = nc.dram_tensor("vd", [T, D], BF16, kind="Internal")
    w1q = nc.dram_tensor("w1q", [D, DF], BF16, kind="Internal")
    w2q = nc.dram_tensor("w2q", [DF, D], BF16, kind="Internal")
    spr = [nc.dram_tensor(f"spr{i}", [T * WS], FP16, kind="Internal")
           for i in range(2)]
    esc = [nc.dram_tensor(f"esc{i}", [T * WA], BF16, kind="Internal")
           for i in range(2)]
    rowscr = nc.dram_tensor("rowscr", [2 * T], F32, kind="Internal")
    dbg_rows = None
    if dbg:
        dbg_rows = nc.dram_tensor("dbg_rows", [2, T], F32,
                                  kind="ExternalOutput").ap()

    qdA = qd.ap()
    kdA = kd.ap()
    vdA = vd.ap()
    w1qA = w1q.ap()
    w2qA = w2q.ap()

    with ExitStack() as top:
        en = top.enter_context
        X = [en(nc.sbuf_tensor(f"x{i}", [128, T], F32)) for i in range(NDC)]
        tblA = en(nc.sbuf_tensor("tblA", [128, 2047], BF16))
        tblB = en(nc.sbuf_tensor("tblB", [128, 2047], BF16))
        RVsb = en(nc.sbuf_tensor("rvsb", [128, 32 * 64], BF16))
        ident = en(nc.sbuf_tensor("idnt", [128, 128], BF16))
        onec = en(nc.sbuf_tensor("onec", [128, 2], BF16))
        oner = en(nc.sbuf_tensor("oner", [1, 512], BF16))
        onerf = en(nc.sbuf_tensor("onerf", [1, 128], F32))
        bqs = en(nc.sbuf_tensor("bqs", [128, NDC], F32))
        bks = en(nc.sbuf_tensor("bks", [128, NDC], F32))
        bos = en(nc.sbuf_tensor("bos", [128, NDC], F32))
        b1s = en(nc.sbuf_tensor("b1s", [128, NFC], F32))
        bvr = en(nc.sbuf_tensor("bvr", [1, D], BF16))
        hb2r = en(nc.sbuf_tensor("hb2r", [1, D], BF16))
        thrs = en(nc.sbuf_tensor("thrs", [128, 4], F32))
        czs = en(nc.sbuf_tensor("czs", [128, 4], F32))
        SS = [Ctr(en(nc.semaphore(f"s{i}"))) for i in range(64)]

        def snap():
            return [s.n for s in SS]

        def snapd(*sems):
            return {id(s): s.n for s in sems}

        with ExitStack() as sc:
            e2 = sc.enter_context
            ztile = e2(nc.sbuf_tensor("ztile", [128, 1024], BF16))
            rtmp = e2(nc.sbuf_tensor("rtmp0", [128, 2048], F32))
            itmp = e2(nc.sbuf_tensor("itmp0", [128, 256], F32))
            rwb1 = e2(nc.sbuf_tensor("rwb1", [1, D], F32))
            rwb2 = e2(nc.sbuf_tensor("rwb2", [1, D], F32))
            s0, s1, s2, s3, s4 = SS[0], SS[1], SS[2], SS[3], SS[4]
            b = snap()
            with nc.Block() as blk:
                @blk.gpsimd
                def _(g):
                    for i in range(NDC):
                        g.dma_start(out=X[i][:, :],
                                    in_=xT[i * 128:(i + 1) * 128, :]
                                    ).then_inc(s0.h, 16)
                        s0.n += 16
                    for dst, srcd in [(bqs, bq_c), (bks, bk_c), (bos, bo_c),
                                      (b1s, b1_c)]:
                        g.dma_start(out=dst[:, :], in_=srcd).then_inc(s0.h, 16)
                        s0.n += 16
                    g.dma_start(out=rwb1[0:1, :], in_=bv_r).then_inc(s0.h, 16)
                    s0.n += 16
                    g.dma_start(out=rwb2[0:1, :], in_=b2_r).then_inc(s0.h, 16)
                    s0.n += 16
                    g.iota(itmp[:, 0:128], [[1, 128]], channel_multiplier=0,
                           allow_small_or_imprecise_dtypes=True)
                    g.iota(itmp[:, 128:256], [[0, 128]], channel_multiplier=1,
                           allow_small_or_imprecise_dtypes=True)
                    g.memset(ztile[:, :], 0)
                    g.memset(onec[:, :], 1.0)
                    g.memset(oner[:, :], 1.0)
                    g.memset(czs[:, 0:1], 0.0)
                    g.memset(czs[:, 1:2], -CSH)
                    g.memset(czs[:, 2:3], EPS)
                    g.memset(onerf[:, :], 1.0).then_inc(s1.h, 1)
                    s1.n += 1
                    g.wait_ge(s0.h, s0.n)

                @blk.sync
                def _(s):
                    s.dma_start(out=rtmp[0:64, 0:2047],
                                in_=relkT).then_inc(s2.h, 16)
                    s2.n += 16
                    s.dma_start(out=rtmp[64:128, 0:2047],
                                in_=relkT).then_inc(s2.h, 16)
                    s2.n += 16

                @blk.vector
                def _(v):
                    v.wait_ge(s1.h, b[1] + 1)
                    v.tensor_tensor(out=ident[:, :], in0=itmp[:, 0:128],
                                    in1=itmp[:, 128:256], op=ALU.is_equal)
                    v.wait_ge(s0.h, b[0] + 16 * 14)
                    v.tensor_scalar(out=bks[:, :], in0=bks[:, :], scalar1=0.125,
                                    scalar2=None, op0=ALU.mult)
                    v.tensor_copy(out=bvr[:, :], in_=rwb1[0:1, :])
                    v.tensor_scalar(out=hb2r[:, :], in0=rwb2[0:1, :],
                                    scalar1=0.5, scalar2=None, op0=ALU.mult)
                    v.wait_ge(s2.h, b[2] + 32)
                    v.scalar_tensor_tensor(
                        out=tblA[:, :], in0=rtmp[:, 0:2047], scalar=1.0,
                        in1=rtmp[:, 2046:2047].to_broadcast([128, 2047]),
                        op0=ALU.mult, op1=ALU.subtract)
                    v.scalar_tensor_tensor(
                        out=tblB[:, :], in0=rtmp[:, 0:2047], scalar=1.0,
                        in1=rtmp[:, 0:1].to_broadcast([128, 2047]),
                        op0=ALU.mult, op1=ALU.subtract).then_inc(s3.h, 1)
                    s3.n += 1
                    v.wait_ge(s2.h, b[2] + 48)
                    v.tensor_copy(out=RVsb[:, :], in_=rtmp[:, 0:2048]).then_inc(
                        s3.h, 1)
                    s3.n += 1

                @blk.sync
                def _(s):
                    s.wait_ge(s3.h, b[3] + 1)
                    s.dma_start(
                        out=rtmp[:, 0:2048].rearrange("p (c k) -> p c k", k=64),
                        in_=bass.AP(rvpad.tensor, 0, [[64, 128], [128 * 64, 32], [1, 64]])
                    ).then_inc(s2.h, 16)
                    s2.n += 16

                @blk.scalar
                def _(s):
                    pass
                    s.wait_ge(s1.h, b[1] + 1)
                    for bi in range(2):
                        for rt in range(NT):
                            for off_, wid_, tsr in [
                                    (rt * 128 * WS, 1024, spr[bi]),
                                    (rt * 128 * WS + 3071, 1024, spr[bi]),
                                    (rt * 128 * WA, 256, esc[bi]),
                                    (rt * 128 * WA + 2304, 256, esc[bi])]:
                                zsrc = (ztile[:, 0:wid_].bitcast(FP16)[:, 0:wid_]
                                        if wid_ == 1024 else ztile[:, 0:wid_])
                                s.dma_start(out=bass.AP(tsr, off_,
                                                        [[WS if wid_ == 1024 else WA, 128],
                                                         [1, wid_]]),
                                            in_=zsrc).then_inc(s4.h, 16)
                                s4.n += 16
                    s.wait_ge(s4.h, s4.n)
                    s.wait_ge(s3.h, b[3] + 2)

        with ExitStack() as sc:
            e2 = sc.enter_context
            wtmp = [e2(nc.sbuf_tensor(f"wt{i}", [128, DF], F32)) for i in range(2)]
            wabs = e2(nc.sbuf_tensor("wabs", [128, DF], FP16))
            wqo = [e2(nc.sbuf_tensor(f"wqo{i}", [128, DF], BF16)) for i in range(2)]
            qrows = e2(nc.sbuf_tensor("qrows", [1, 600], F32))
            psQ = e2(nc.psum_tensor("psq", [1, 512], F32))
            psB = e2(nc.psum_tensor("psb", [128, 2], F32))

            def quant_pass(blk, which):
                src, dstA = (w1T, w1qA) if which == 0 else (w2T, w2qA)
                nch = NDC if which == 0 else NFC
                wid = DF if which == 0 else D
                npc = wid // 512
                ntot = float(D * DF)
                thr_col = thrs[:, 2 * which:2 * which + 1]
                nthr_col = thrs[:, 2 * which + 1:2 * which + 2]
                sLD, sAB, sMM, sRW, sBC, sTH, sQO, sSO = SS[5:13]
                sLDb = [sLD, SS[52]]
                sSOb = [sSO, SS[53]]
                b = snapd(sLD, sAB, sMM, sRW, sBC, sTH, sQO, sSO,
                          SS[52], SS[53])

                @blk.sync
                def _(s):
                    for c in range(nch):
                        if c >= 2:
                            s.wait_ge(sAB.h, b[id(sAB)] + c - 1)
                        sl = sLDb[c % 2]
                        s.dma_start(out=wtmp[c % 2][:, 0:wid],
                                    in_=src[c * 128:(c + 1) * 128, :]
                                    ).then_inc(sl.h, 16)
                        sl.n += 16
                    for c in range(nch):
                        s.wait_ge(sTH.h, b[id(sTH)] + 1)
                        if c >= 2:
                            s.wait_ge(sQO.h, b[id(sQO)] + c - 1)
                        sl = sLDb[c % 2]
                        s.dma_start(out=wtmp[c % 2][:, 0:wid],
                                    in_=src[c * 128:(c + 1) * 128, :]
                                    ).then_inc(sl.h, 16)
                        sl.n += 16

                @blk.scalar
                def _(s):
                    for c in range(nch):
                        s.wait_ge(sLDb[c % 2].h,
                                  b[id(sLDb[c % 2])] + 16 * (c // 2 + 1))
                        if c >= 1:
                            s.wait_ge(sMM.h, b[id(sMM)] + c * npc)
                        s.activation(out=wabs[:, 0:wid], in_=wtmp[c % 2][:, 0:wid],
                                     func=AF.Abs, bias=czs[:, 0:1], scale=1.0
                                     ).then_inc(sAB.h, 1)
                        sAB.n += 1

                @blk.tensor
                def _(tn):
                    for c in range(nch):
                        tn.wait_ge(sAB.h, b[id(sAB)] + c + 1)
                        for p in range(npc):
                            tn.matmul(psQ[0:1, :],
                                      onec[:, 0:1], wabs[:, p * 512:(p + 1) * 512],
                                      start=(c == 0 and p == 0),
                                      stop=(c == nch - 1 and p == npc - 1)
                                      ).then_inc(sMM.h, 1)
                            sMM.n += 1
                    tn.wait_ge(sRW.h, b[id(sRW)] + 1)
                    tn.matmul(psB[:, 0:1], onerf[0:1, :], qrows[0:1, 577:578],
                              start=True, stop=True)
                    tn.matmul(psB[:, 1:2], onerf[0:1, :], qrows[0:1, 578:579],
                              start=True, stop=True).then_inc(sBC.h, 1)
                    sBC.n += 1

                @blk.vector
                def _(v):
                    def gap():
                        v.tensor_copy(out=czs[:, 3:4], in_=czs[:, 0:1])
                        v.tensor_copy(out=czs[:, 3:4], in_=czs[:, 0:1])
                    v.wait_ge(sMM.h, b[id(sMM)] + nch * npc)
                    v.reduce_sum(out=qrows[0:1, 0:64],
                                 in_=psQ[0:1, :].rearrange(
                                     "p (a c) -> p a c", c=8), axis=AX.X)
                    gap()
                    v.reduce_sum(out=qrows[0:1, 512:520],
                                 in_=qrows[0:1, 0:64].rearrange(
                                     "p (a c) -> p a c", c=8), axis=AX.X)
                    gap()
                    v.reduce_sum(out=qrows[0:1, 576:577],
                                 in_=qrows[0:1, 512:520], axis=AX.X)
                    gap()
                    v.tensor_scalar(out=qrows[0:1, 577:578],
                                    in0=qrows[0:1, 576:577],
                                    scalar1=0.5 / ntot, scalar2=0.5 * EPS,
                                    op0=ALU.mult, op1=ALU.add)
                    v.tensor_scalar(out=qrows[0:1, 578:579],
                                    in0=qrows[0:1, 576:577],
                                    scalar1=-0.5 / ntot, scalar2=-0.5 * EPS,
                                    op0=ALU.mult, op1=ALU.add).then_inc(sRW.h, 1)
                    sRW.n += 1
                    v.wait_ge(sBC.h, b[id(sBC)] + 1)
                    v.tensor_copy(out=thr_col, in_=psB[:, 0:1])
                    v.tensor_copy(out=nthr_col, in_=psB[:, 1:2]).then_inc(sTH.h, 1)
                    sTH.n += 1
                    sca = 0.5 if which == 1 else 1.0
                    for c in range(nch):
                        v.wait_ge(sLDb[c % 2].h,
                                  b[id(sLDb[c % 2])]
                                  + 16 * (nch // 2 + c // 2 + 1))
                        if c >= 2:
                            v.wait_ge(sSOb[c % 2].h,
                                      b[id(sSOb[c % 2])] + 16 * (c // 2))
                        v.tensor_scalar(out=wabs[:, 0:wid],
                                        in0=wtmp[c % 2][:, 0:wid],
                                        scalar1=thr_col, scalar2=sca,
                                        op0=ALU.is_gt, op1=ALU.mult)
                        v.tensor_scalar(out=wqo[c % 2][:, 0:wid],
                                        in0=wtmp[c % 2][:, 0:wid],
                                        scalar1=nthr_col, scalar2=sca,
                                        op0=ALU.is_lt, op1=ALU.mult)
                        v.tensor_tensor(out=wqo[c % 2][:, 0:wid],
                                        in0=wabs[:, 0:wid],
                                        in1=wqo[c % 2][:, 0:wid],
                                        op=ALU.subtract).then_inc(sQO.h, 1)
                        sQO.n += 1

                @blk.gpsimd
                def _(g):
                    for c in range(nch):
                        g.wait_ge(sQO.h, b[id(sQO)] + c + 1)
                        so_ = sSOb[c % 2]
                        g.dma_start(out=dstA[c * 128:(c + 1) * 128, :],
                                    in_=wqo[c % 2][:, 0:wid]).then_inc(so_.h, 16)
                        so_.n += 16
                    g.wait_ge(sSOb[0].h, sSOb[0].n)
                    g.wait_ge(sSOb[1].h, sSOb[1].n)

            with nc.Block() as blk:
                quant_pass(blk, 0)
            with nc.Block() as blk:
                quant_pass(blk, 1)
            if dbg:
                with nc.Block() as blk:
                    @blk.gpsimd
                    def _(g):
                        sst = SS[51]
                        for i in range(NDC):
                            g.dma_start(out=dbg_t["dbg_x0"][i * 128:(i + 1) * 128, :],
                                        in_=X[i][:, :]).then_inc(sst.h, 16)
                            sst.n += 16
                        g.dma_start(out=dbg_t["dbg_w1q"][0:128, :].bitcast(BF16)[:, 0:T],
                                    in_=w1qA[0:128, 0:T]).then_inc(sst.h, 16)
                        sst.n += 16
                        g.wait_ge(sst.h, sst.n)

        def ln_phase(blk, Xln, psM, psS, rowsA, rowsB, vrow, bcA, bcB,
                     xsq, tmpf):
            sCS, sMM, sVAR, sRSD, sMR, sRST, sBCL, sAPP = SS[13:21]
            b = snapd(sCS, sMM, sVAR, sRSD, sMR, sRST, sBCL, sAPP)

            @blk.scalar
            def _(s):
                for dc in range(NDC):
                    if dc >= 2:
                        s.wait_ge(sMM.h, b[id(sMM)] + 4 * (dc - 1))
                    s.copy(out=Xln[dc][:, :], in_=X[dc][:, :])
                    s.activation(out=xsq[dc % 2][:, :], in_=X[dc][:, :],
                                 func=AF.Square, bias=czs[:, 0:1], scale=1.0
                                 ).then_inc(sCS.h, 1)
                    sCS.n += 1
                s.wait_ge(sVAR.h, b[id(sVAR)] + 1)
                s.activation(out=rowsB[0:1, :], in_=vrow[0:1, :], func=AF.Ln,
                             bias=czs[0:1, 2:3], scale=1.0)
                s.activation(out=rowsB[0:1, :], in_=rowsB[0:1, :], func=AF.Exp,
                             bias=czs[0:1, 0:1], scale=-0.5).then_inc(sRSD.h, 1)
                sRSD.n += 1
                s.dma_start(out=bass.AP(rowscr, 0, [[1, 1], [1, T]]),
                            in_=rowsB[0:1, :]).then_inc(sRST.h, 16)
                sRST.n += 16
                s.dma_start(out=bass.AP(rowscr, T, [[1, 1], [1, T]]),
                            in_=vrow[0:1, :]).then_inc(sRST.h, 16)
                sRST.n += 16
                s.wait_ge(sRST.h, sRST.n)

            @blk.tensor
            def _(tn):
                for dc in range(NDC):
                    tn.wait_ge(sCS.h, b[id(sCS)] + dc + 1)
                    for p in range(4):
                        tn.matmul(psM[p][0:1, :], onec[:, 0:1],
                                  Xln[dc][:, p * 512:(p + 1) * 512],
                                  start=(dc == 0), stop=(dc == NDC - 1))
                        tn.matmul(psS[p][0:1, :], onec[:, 0:1],
                                  xsq[dc % 2][:, p * 512:(p + 1) * 512],
                                  start=(dc == 0), stop=(dc == NDC - 1)
                                  ).then_inc(sMM.h, 1)
                        sMM.n += 1
                tn.wait_ge(sMR.h, b[id(sMR)] + 1)
                for p in range(4):
                    tn.matmul(psM[p][:, :], onerf[0:1, :],
                              rowsB[0:1, p * 512:(p + 1) * 512],
                              start=True, stop=True)
                    tn.matmul(psS[p][:, :], onerf[0:1, :],
                              rowsA[0:1, p * 512:(p + 1) * 512],
                              start=True, stop=True).then_inc(sBCL.h, 1)
                    sBCL.n += 1

            @blk.vector
            def _(v):
                def gap():
                    v.tensor_copy(out=czs[:, 3:4], in_=czs[:, 0:1])
                    v.tensor_copy(out=czs[:, 3:4], in_=czs[:, 0:1])
                v.wait_ge(sMM.h, b[id(sMM)] + 4 * NDC)
                for p in range(4):
                    v.tensor_scalar(out=rowsA[0:1, p * 512:(p + 1) * 512],
                                    in0=psM[p][0:1, :], scalar1=1.0 / D,
                                    scalar2=None, op0=ALU.mult)
                gap()
                v.tensor_tensor(out=rowsB[0:1, :], in0=rowsA[0:1, :],
                                in1=rowsA[0:1, :], op=ALU.mult)
                gap()
                for p in range(4):
                    inst = v.scalar_tensor_tensor(
                        out=vrow[0:1, p * 512:(p + 1) * 512],
                        in0=psS[p][0:1, :], scalar=1.0 / D,
                        in1=rowsB[0:1, p * 512:(p + 1) * 512],
                        op0=ALU.mult, op1=ALU.subtract)
                gap()
                inst.then_inc(sVAR.h, 1)
                sVAR.n += 1
                v.wait_ge(sRSD.h, b[id(sRSD)] + 1)
                v.tensor_tensor(out=rowsA[0:1, :], in0=rowsA[0:1, :],
                                in1=rowsB[0:1, :], op=ALU.mult)
                gap()
                nc.vector.sem_inc(sMR.h, 1)
                sMR.n += 1
                v.wait_ge(sBCL.h, b[id(sBCL)] + 4)
                for p in range(4):
                    v.tensor_copy(out=bcA[:, p * 512:(p + 1) * 512],
                                  in_=psM[p][:, :])
                    v.tensor_copy(out=bcB[:, p * 512:(p + 1) * 512],
                                  in_=psS[p][:, :])
                for dc in range(NDC):
                    v.tensor_tensor(out=tmpf[:, :], in0=X[dc][:, :],
                                    in1=bcA[:, :], op=ALU.mult)
                    v.tensor_tensor(out=Xln[dc][:, :], in0=tmpf[:, :],
                                    in1=bcB[:, :], op=ALU.subtract
                                    ).then_inc(sAPP.h, 1)
                    sAPP.n += 1

            @blk.gpsimd
            def _(g):
                g.wait_ge(sAPP.h, b[id(sAPP)] + NDC)

        def ffn_phase(blk, Xln, H, w1s, w2s, psF, evt):
            sL1, sL2, sM1, sRL, sM2, sXA = SS[21:27]
            sL1b = [sL1, SS[54]]
            sL2b = [sL2, SS[55]]
            b = snapd(sL1, sL2, sM1, sRL, sM2, sXA, SS[54], SS[55])

            @blk.sync
            def _(s):
                for th in range(2):
                    for fc in range(NFC):
                        i = th * NFC + fc
                        if i >= 2:
                            s.wait_ge(sM1.h, b[id(sM1)] + 2 * (i - 1))
                        s.dma_start(out=w1s[i % 2][:, :, :],
                                    in_=bass.AP(w1q, 128 * fc,
                                                [[DF, 128], [128 * DF, 8],
                                                 [1, 128]])).then_inc(
                                                     sL1b[i % 2].h, 16)
                        sL1b[i % 2].n += 16
                    for dc2 in range(NDC):
                        i = th * NDC + dc2
                        if i >= 2:
                            s.wait_ge(sM2.h, b[id(sM2)] + 2 * (i - 1))
                        s.dma_start(out=w2s[i % 2][:, :, :],
                                    in_=bass.AP(w2q, 128 * dc2,
                                                [[D, 128], [128 * D, 32],
                                                 [1, 128]])).then_inc(
                                                     sL2b[i % 2].h, 16)
                        sL2b[i % 2].n += 16

            @blk.tensor
            def _(tn):
                for th in range(2):
                    for fc in range(NFC):
                        i = th * NFC + fc
                        tn.wait_ge(sL1b[i % 2].h,
                                   b[id(sL1b[i % 2])] + 16 * (i // 2 + 1))
                        for p in range(2):
                            if i >= 1:
                                tn.wait_ge(sRL.h, b[id(sRL)] + 2 * i - 1 + p)
                            for dc in range(NDC):
                                mi = tn.matmul(
                                    psF[p][:, :], w1s[i % 2][:, dc, :],
                                    Xln[dc][:, th * 1024 + p * 512:
                                            th * 1024 + (p + 1) * 512],
                                    start=(dc == 0), stop=(dc == NDC - 1))
                            mi.then_inc(sM1.h, 1)
                            sM1.n += 1
                    for dc2 in range(NDC):
                        i = th * NDC + dc2
                        tn.wait_ge(sL2b[i % 2].h,
                                   b[id(sL2b[i % 2])] + 16 * (i // 2 + 1))
                        tn.wait_ge(sRL.h, b[id(sRL)] + 64 * (th + 1))
                        for p in range(2):
                            if i >= 1:
                                tn.wait_ge(sXA.h, b[id(sXA)] + 2 * i - 1 + p)
                            for fc in range(NFC):
                                tn.matmul(psF[2 + p][:, :], w2s[i % 2][:, fc, :],
                                          H[fc][:, p * 512:(p + 1) * 512],
                                          start=(fc == 0), stop=False)
                            tn.matmul(psF[2 + p][:, :],
                                      hb2r[0:1, dc2 * 128:(dc2 + 1) * 128],
                                      oner[0:1, :], start=False, stop=True
                                      ).then_inc(sM2.h, 1)
                            sM2.n += 1

            @blk.scalar
            def _(s):
                for th in range(2):
                    if th == 1:
                        s.wait_ge(sM2.h, b[id(sM2)] + 16)
                    for fc in range(NFC):
                        i = th * NFC + fc
                        for p in range(2):
                            s.wait_ge(sM1.h, b[id(sM1)] + 2 * i + p + 1)
                            s.activation(out=H[fc][:, p * 512:(p + 1) * 512],
                                         in_=psF[p][:, :], func=AF.Relu,
                                         bias=b1s[:, fc:fc + 1], scale=1.0
                                         ).then_inc(sRL.h, 1)
                            sRL.n += 1

            @blk.vector
            def _(v):
                for th in range(2):
                    for dc2 in range(NDC):
                        i = th * NDC + dc2
                        for p in range(2):
                            v.wait_ge(sM2.h, b[id(sM2)] + 2 * i + p + 1)
                            xsl = X[dc2][:, th * 1024 + p * 512:
                                         th * 1024 + (p + 1) * 512]
                            v.tensor_tensor(out=xsl, in0=psF[2 + p][:, :],
                                            in1=xsl, op=ALU.add).then_inc(sXA.h, 1)
                            sXA.n += 1

            @blk.gpsimd
            def _(g):
                g.wait_ge(sXA.h, b[id(sXA)] + 32)
                evt(g)

        with ExitStack() as sc:
            e2 = sc.enter_context
            Xln = [e2(nc.sbuf_tensor(f"xln{i}", [128, T], BF16))
                   for i in range(NDC)]
            psM = [e2(nc.psum_tensor(f"psm{i}", [128, 512], F32))
                   for i in range(4)]
            psS = [e2(nc.psum_tensor(f"pss{i}", [128, 512], F32))
                   for i in range(4)]

            lnctr = [0]

            def run_ln():
                lnctr[0] += 1
                u = lnctr[0]
                with ExitStack() as scl:
                    e3 = scl.enter_context
                    rowsA = e3(nc.sbuf_tensor(f"rowsA_{u}", [1, T], F32))
                    rowsB = e3(nc.sbuf_tensor(f"rowsB_{u}", [1, T], F32))
                    bcA = e3(nc.sbuf_tensor(f"bcA_{u}", [128, T], F32))
                    bcB = e3(nc.sbuf_tensor(f"bcB_{u}", [128, T], F32))
                    xsq = [e3(nc.sbuf_tensor(f"xsq_{u}_{i}", [128, T], BF16))
                           for i in range(2)]
                    vrow = e3(nc.sbuf_tensor(f"vrow_{u}", [1, T], F32))
                    tmpf = e3(nc.sbuf_tensor(f"tmpfl_{u}", [128, T], F32))
                    with nc.Block() as blk:
                        ln_phase(blk, Xln, psM, psS, rowsA, rowsB, vrow,
                                 bcA, bcB, xsq, tmpf)

            run_ln()
            if dbg:
                with nc.Block() as blk:
                    @blk.gpsimd
                    def _(g):
                        sst = SS[51]
                        for i in range(NDC):
                            g.dma_start(out=dbg_t["dbg_xln1"][i * 128:(i + 1) * 128, :]
                                        .bitcast(BF16)[:, 0:T],
                                        in_=Xln[i][:, :]).then_inc(sst.h, 16)
                            sst.n += 16
                        g.dma_start(out=dbg_rows,
                                    in_=bass.AP(rowscr, 0, [[T, 2], [1, T]])
                                    ).then_inc(sst.h, 16)
                        sst.n += 16
                        g.wait_ge(sst.h, sst.n)
            with ExitStack() as scf:
                e3 = scf.enter_context
                H = [e3(nc.sbuf_tensor(f"ha_{i}", [128, 1024], BF16))
                     for i in range(NFC)]
                w1s = [e3(nc.sbuf_tensor(f"w1s{i}", [128, 8, 128], BF16))
                       for i in range(2)]
                w2s = [e3(nc.sbuf_tensor(f"w2s{i}", [128, 32, 128], BF16))
                       for i in range(2)]

                def evt1(g):
                    if dbg:
                        sst = SS[51]
                        for i in range(NDC):
                            g.dma_start(out=dbg_t["dbg_ffn1"][i * 128:(i + 1) * 128, :],
                                        in_=X[i][:, :]).then_inc(sst.h, 16)
                            sst.n += 16
                        g.wait_ge(sst.h, sst.n)
                with nc.Block() as blk:
                    ffn_phase(blk, Xln, H, w1s, w2s, psM, evt1)

            run_ln()
            if dbg:
                with nc.Block() as blk:
                    @blk.gpsimd
                    def _(g):
                        sst = SS[51]
                        for i in range(NDC):
                            g.dma_start(out=dbg_t["dbg_ln2"][i * 128:(i + 1) * 128, :]
                                        .bitcast(BF16)[:, 0:T],
                                        in_=Xln[i][:, :]).then_inc(sst.h, 16)
                            sst.n += 16
                        g.wait_ge(sst.h, sst.n)

            with ExitStack() as scp:
                e3 = scp.enter_context
                wbf = [e3(nc.sbuf_tensor(f"wbf{i}", [128, D], BF16))
                       for i in range(NDC)]
                wtm = [e3(nc.sbuf_tensor(f"wtm{i}", [128, D], F32))
                       for i in range(2)]
                otile = [e3(nc.sbuf_tensor(f"ot{i}", [128, 512], BF16))
                         for i in range(2)]
                psP = [psM[0], psM[1]]
                sWL, sWC, sPM, sPE, sPS = SS[27:32]
                sWLb = [sWL, SS[56]]
                sPSb = [sPS, SS[57]]
                b = snapd(sWL, sWC, sPM, sPE, sPS, SS[56], SS[57])
                with nc.Block() as blk:
                    @blk.sync
                    def _(s):
                        for wi, wsrc in enumerate([wqT, wkT, wvT]):
                            for c in range(NDC):
                                i = wi * NDC + c
                                if i >= 2:
                                    s.wait_ge(sWC.h, b[id(sWC)] + i - 1)
                                s.dma_start(out=wtm[c % 2][:, :],
                                            in_=wsrc[c * 128:(c + 1) * 128, :]
                                            ).then_inc(sWLb[i % 2].h, 16)
                                sWLb[i % 2].n += 16

                    @blk.vector
                    def _(v):
                        for wi in range(3):
                            for c in range(NDC):
                                i = wi * NDC + c
                                v.wait_ge(sWLb[i % 2].h,
                                          b[id(sWLb[i % 2])] + 16 * (i // 2 + 1))
                                if wi > 0:
                                    v.wait_ge(sPM.h, b[id(sPM)] + wi * 32)
                                v.tensor_copy(out=wbf[c][:, :],
                                              in_=wtm[c % 2][:, :]
                                              ).then_inc(sWC.h, 1)
                                sWC.n += 1
                            for j in range(32):
                                jj = wi * 32 + j
                                v.wait_ge(sPM.h, b[id(sPM)] + jj + 1)
                                if jj >= 2:
                                    v.wait_ge(sPSb[jj % 2].h,
                                              b[id(sPSb[jj % 2])] + 16 * (jj // 2))
                                if wi == 0:
                                    oc = j // 4
                                    ei = v.tensor_scalar(out=otile[j % 2][:, :],
                                                         in0=psP[j % 2][:, :],
                                                         scalar1=bqs[:, oc:oc + 1],
                                                         scalar2=None, op0=ALU.add)
                                elif wi == 1:
                                    oc = j // 4
                                    ei = v.tensor_scalar(out=otile[j % 2][:, :],
                                                         in0=psP[j % 2][:, :],
                                                         scalar1=0.125,
                                                         scalar2=bks[:, oc:oc + 1],
                                                         op0=ALU.mult, op1=ALU.add)
                                else:
                                    ei = v.tensor_copy(out=otile[j % 2][:, :],
                                                       in_=psP[j % 2][:, :])
                                ei.then_inc(sPE.h, 1)
                                sPE.n += 1

                    @blk.tensor
                    def _(tn):
                        tn.wait_ge(sWC.h, b[id(sWC)] + NDC)
                        for wi in range(3):
                            if wi > 0:
                                tn.wait_ge(sWC.h, b[id(sWC)] + (wi + 1) * NDC)
                            for j in range(32):
                                jj = wi * 32 + j
                                if j >= 2:
                                    tn.wait_ge(sPE.h, b[id(sPE)] + jj - 1)
                                if wi < 2:
                                    oc, p = divmod(j, 4)
                                    for ic in range(NDC):
                                        mi = tn.matmul(
                                            psP[j % 2][:, :],
                                            wbf[ic][:, oc * 128:(oc + 1) * 128],
                                            Xln[ic][:, p * 512:(p + 1) * 512],
                                            start=(ic == 0), stop=(ic == NDC - 1))
                                else:
                                    tc, p = divmod(j, 2)
                                    for dc in range(NDC):
                                        tn.matmul(
                                            psP[j % 2][:, :],
                                            Xln[dc][:, tc * 128:(tc + 1) * 128],
                                            wbf[dc][:, p * 512:(p + 1) * 512],
                                            start=(dc == 0), stop=False)
                                    mi = tn.matmul(psP[j % 2][:, :],
                                                   oner[0:1, 0:128],
                                                   bvr[0:1, p * 512:(p + 1) * 512],
                                                   start=False, stop=True)
                                mi.then_inc(sPM.h, 1)
                                sPM.n += 1

                    @blk.scalar
                    def _(s):
                        for wi in range(3):
                            for j in range(32):
                                jj = wi * 32 + j
                                s.wait_ge(sPE.h, b[id(sPE)] + jj + 1)
                                if wi == 0:
                                    oc, p = divmod(j, 4)
                                    dst = qdA[oc * 128:(oc + 1) * 128,
                                              p * 512:(p + 1) * 512]
                                elif wi == 1:
                                    oc, p = divmod(j, 4)
                                    dst = kdA[oc * 128:(oc + 1) * 128,
                                              p * 512:(p + 1) * 512]
                                else:
                                    tc, p = divmod(j, 2)
                                    dst = vdA[tc * 128:(tc + 1) * 128,
                                              p * 512:(p + 1) * 512]
                                s.dma_start(out=dst, in_=otile[j % 2][:, :]
                                            ).then_inc(sPSb[jj % 2].h, 16)
                                sPSb[jj % 2].n += 16
                        s.wait_ge(sPSb[0].h, sPSb[0].n)
                        s.wait_ge(sPSb[1].h, sPSb[1].n)

        with ExitStack() as sc:
            e2 = sc.enter_context
            HO = [e2(nc.sbuf_tensor(f"ho{i}", [128, T], BF16))
                  for i in range(NDC)]
            with ExitStack() as sca:
                e3 = sca.enter_context
                Qp = [e3(nc.sbuf_tensor(f"qp{i}", [128, T], BF16))
                      for i in range(2)]
                Kp = [e3(nc.sbuf_tensor(f"kp{i}", [128, T], BF16))
                      for i in range(2)]
                Vp = [e3(nc.sbuf_tensor(f"vp{i}", [128, 16 * 128], BF16))
                      for i in range(2)]
                scoA = [e3(nc.sbuf_tensor(f"sco{i}", [128, T], F32))
                        for i in range(2)]
                bndE = [e3(nc.sbuf_tensor(f"bnd{i}", [128, T], BF16))
                        for i in range(2)]
                bndF = [e3(nc.sbuf_tensor(f"bnf{i}", [128, T], FP16))
                        for i in range(2)]
                Ssb = [e3(nc.sbuf_tensor(f"ssb{i}", [128, 2047], FP16))
                       for i in range(2)]
                ATs = e3(nc.sbuf_tensor("ats", [128, T], BF16))
                DTs = e3(nc.sbuf_tensor("dts", [128, 2176], BF16))
                Adg = [e3(nc.sbuf_tensor(f"adg{i}", [128, 2176], BF16))
                       for i in range(2)]
                rsc = e3(nc.sbuf_tensor("rsc", [128, 4], F32))
                PS = [e3(nc.psum_tensor(f"aps{i}", [128, 512], F32))
                      for i in range(2)]
                psT = [e3(nc.psum_tensor(f"apt{i}", [128, 1024], BF16))
                       for i in range(2)]
                psO = [e3(nc.psum_tensor(f"apo{i}", [128, 128], F32))
                       for i in range(2)]
                (sQPL, sSR, sSRE, sSST, sBRD, sQK, sBAD, sEXP, sNRM, sAST,
                 sDRD, sTRG, sGEV, sAV, sOEV) = SS[32:47]
                sQPLb = [sQPL, SS[58]]
                sSSTb = [sSST, SS[59]]
                sBRDb = [sBRD, SS[60]]
                sASTb = [sAST, SS[61]]
                sDRDb = [sDRD, SS[62]]
                b = snapd(sQPL, sSR, sSRE, sSST, sBRD, sQK, sBAD, sEXP,
                          sNRM, sAST, sDRD, sTRG, sGEV, sAV, sOEV,
                          SS[58], SS[59], SS[60], SS[61], SS[62])

                def consumed_wait(eng, q):
                    prev = q - 2
                    if prev < 0:
                        return
                    gp, jp = divmod(prev, 8)
                    if jp < 4:
                        eng.wait_ge(sSRE.h, b[id(sSRE)] + 4 * gp + jp + 1)
                    else:
                        eng.wait_ge(sBAD.h, b[id(sBAD)] + 4 * gp + (jp - 4) + 1)

                with nc.Block() as blk:
                    @blk.sync
                    def _(s):
                        for g in range(NG):
                            h, t = divmod(g, NT)
                            if t == 0 and h % 2 == 0:
                                hp = h // 2
                                if h >= 4:
                                    s.wait_ge(sQK.h, b[id(sQK)] + 64 * (h - 2))
                                    s.wait_ge(sAV.h, b[id(sAV)] + NT * (h - 2))
                                sq = sQPLb[hp % 2]
                                s.dma_start(out=Qp[hp % 2][:, :],
                                            in_=qdA[h * 64:(h + 2) * 64, :]
                                            ).then_inc(sq.h, 16)
                                s.dma_start(out=Kp[hp % 2][:, :],
                                            in_=kdA[h * 64:(h + 2) * 64, :]
                                            ).then_inc(sq.h, 16)
                                s.dma_start(
                                    out=Vp[hp % 2][:, :].rearrange(
                                        "p (c k) -> p c k", k=128),
                                    in_=bass.AP(vd, h * 64,
                                                [[D, 128], [128 * D, 16],
                                                 [1, 128]])).then_inc(sq.h, 16)
                                sq.n += 48
                            s.wait_ge(sSSTb[g % 2].h,
                                      b[id(sSSTb[g % 2])] + 16 * (g // 2 + 1))
                            if g >= 2:
                                s.wait_ge(sNRM.h, b[id(sNRM)] + g - 1)
                            s.dma_start(out=bndF[g % 2][:, :],
                                        in_=bass.AP(spr[h % 2],
                                                    t * 128 * (WS - 1) + 2047,
                                                    [[WS - 1, 128], [1, 2048]])
                                        ).then_inc(sBRDb[g % 2].h, 16)
                            sBRDb[g % 2].n += 16
                            s.wait_ge(sASTb[g % 2].h,
                                      b[id(sASTb[g % 2])] + 16 * (g // 2 + 1))
                            if g >= 2:
                                s.wait_ge(sTRG.h, b[id(sTRG)] + 5 * (g - 1))
                            s.dma_start(out=Adg[g % 2][:, :],
                                        in_=bass.AP(esc[h % 2],
                                                    t * 128 * WA + 129,
                                                    [[WA + 1, 128], [1, 2176]])
                                        ).then_inc(sDRDb[g % 2].h, 16)
                            sDRDb[g % 2].n += 16

                    @blk.scalar
                    def _(s):
                        for g in range(NG):
                            h, t = divmod(g, NT)
                            s.wait_ge(sSRE.h, b[id(sSRE)] + 4 * g + 4)
                            if h >= 2:
                                gp = (h - 2) * NT + t
                                s.wait_ge(sBRDb[gp % 2].h,
                                          b[id(sBRDb[gp % 2])] + 16 * (gp // 2 + 1))
                            s.dma_start(out=bass.AP(spr[h % 2],
                                                    t * 128 * WS + 1024,
                                                    [[WS, 128], [1, 2047]]),
                                        in_=Ssb[g % 2][:, :]
                                        ).then_inc(sSSTb[g % 2].h, 16)
                            sSSTb[g % 2].n += 16
                            s.wait_ge(sBAD.h, b[id(sBAD)] + 4 * g + 4)
                            s.activation(out=bndE[g % 2][:, :],
                                         in_=scoA[g % 2][:, :], func=AF.Exp,
                                         bias=czs[:, 1:2], scale=1.0,
                                         accum_out=rsc[:, g % 2:g % 2 + 1]
                                         ).then_inc(sEXP.h, 1)
                            sEXP.n += 1
                            s.wait_ge(sNRM.h, b[id(sNRM)] + g + 1)
                            if h >= 2:
                                gp = (h - 2) * NT + t
                                s.wait_ge(sDRDb[gp % 2].h,
                                          b[id(sDRDb[gp % 2])] + 16 * (gp // 2 + 1))
                            Av = scoA[g % 2][:, :].bitcast(BF16)[:, 0:T]
                            s.dma_start(out=bass.AP(esc[h % 2],
                                                    t * 128 * WA + 256,
                                                    [[WA, 128], [1, 2048]]),
                                        in_=Av).then_inc(sASTb[g % 2].h, 16)
                            sASTb[g % 2].n += 16

                    @blk.tensor
                    def _(tn):
                        for g in range(NG + 1):
                            if g < NG:
                                h, t = divmod(g, NT)
                                off = 64 * (h % 2)
                                hp2 = (h // 2) % 2
                                tbl = tblA if t < 8 else tblB
                                lhs = Qp[hp2][off:off + 64,
                                              t * 128:(t + 1) * 128]
                                hp_ = h // 2
                                tn.wait_ge(sQPLb[hp_ % 2].h,
                                           b[id(sQPLb[hp_ % 2])]
                                           + 48 * (hp_ // 2 + 1))
                                for p, (o, w) in enumerate(PIECES):
                                    consumed_wait(tn, 8 * g + p)
                                    tn.matmul(PS[p % 2][:, 0:w], lhs,
                                              tbl[off:off + 64, o:o + w],
                                              start=True, stop=True
                                              ).then_inc(sSR.h, 1)
                                    sSR.n += 1
                                for p in range(4):
                                    consumed_wait(tn, 8 * g + 4 + p)
                                    tn.matmul(PS[p % 2][:, :], lhs,
                                              Kp[hp2][off:off + 64,
                                                      p * 512:(p + 1) * 512],
                                              start=True, stop=True
                                              ).then_inc(sQK.h, 1)
                                    sQK.n += 1
                            if g >= 1:
                                g1 = g - 1
                                h1, t1 = divmod(g1, NT)
                                off1 = 64 * (h1 % 2)
                                hp21 = (h1 // 2) % 2
                                Av1 = scoA[g1 % 2][:, :].bitcast(BF16)[:, 0:T]
                                tn.wait_ge(sNRM.h, b[id(sNRM)] + g1 + 1)
                                for grp in range(2):
                                    gg = 5 * g1 + grp
                                    if gg >= 2:
                                        tn.wait_ge(sGEV.h, b[id(sGEV)] + gg - 1)
                                    for bb in range(8):
                                        blkc = grp * 8 + bb
                                        ti = tn.transpose(
                                            psT[gg % 2][:, bb * 128:(bb + 1) * 128],
                                            Av1[:, blkc * 128:(blkc + 1) * 128],
                                            ident[:, :])
                                    ti.then_inc(sTRG.h, 1)
                                    sTRG.n += 1
                                tn.wait_ge(sDRDb[g1 % 2].h,
                                           b[id(sDRDb[g1 % 2])]
                                           + 16 * (g1 // 2 + 1))
                                for grp in range(3):
                                    gg = 5 * g1 + 2 + grp
                                    nb = 8 if grp < 2 else 1
                                    if gg >= 2:
                                        tn.wait_ge(sGEV.h, b[id(sGEV)] + gg - 1)
                                    for bb in range(nb):
                                        blkc = grp * 8 + bb
                                        ti = tn.transpose(
                                            psT[gg % 2][:, bb * 128:(bb + 1) * 128],
                                            Adg[g1 % 2][:, blkc * 128:(blkc + 1) * 128],
                                            ident[:, :])
                                    ti.then_inc(sTRG.h, 1)
                                    sTRG.n += 1
                                tn.wait_ge(sGEV.h, b[id(sGEV)] + 5 * g1 + 2)
                                if g1 >= 2:
                                    tn.wait_ge(sOEV.h, b[id(sOEV)] + g1 - 1)
                                for rc in range(16):
                                    tn.matmul(psO[g1 % 2][off1:off1 + 64, :],
                                              Vp[hp21][:, rc * 128 + off1:rc * 128 + off1 + 64],
                                              ATs[:, rc * 128:(rc + 1) * 128],
                                              start=(rc == 0), stop=False)
                                tn.wait_ge(sGEV.h, b[id(sGEV)] + 5 * g1 + 5)
                                for cc in range(17):
                                    gc = 15 - t1 + cc
                                    mi = tn.matmul(
                                        psO[g1 % 2][off1:off1 + 64, :],
                                        RVsb[:, gc * 64:(gc + 1) * 64],
                                        DTs[:, cc * 128:(cc + 1) * 128],
                                        start=False, stop=(cc == 16))
                                mi.then_inc(sAV.h, 1)
                                sAV.n += 1

                    @blk.vector
                    def _(v):
                        for g in range(NG + 1):
                            if g < NG:
                                for p, (o, w) in enumerate(PIECES):
                                    v.wait_ge(sSR.h, b[id(sSR)] + 4 * g + p + 1)
                                    if p == 0 and g >= 2:
                                        v.wait_ge(sSSTb[g % 2].h,
                                                  b[id(sSSTb[g % 2])]
                                                  + 16 * (g // 2))
                                    v.tensor_copy(out=Ssb[g % 2][:, o:o + w],
                                                  in_=PS[p % 2][:, 0:w]
                                                  ).then_inc(sSRE.h, 1)
                                    sSRE.n += 1
                                for p in range(4):
                                    v.wait_ge(sQK.h, b[id(sQK)] + 4 * g + p + 1)
                                    if p == 0:
                                        v.wait_ge(sBRDb[g % 2].h,
                                                  b[id(sBRDb[g % 2])]
                                                  + 16 * (g // 2 + 1))
                                        if g >= 2:
                                            v.wait_ge(sASTb[g % 2].h,
                                                      b[id(sASTb[g % 2])]
                                                      + 16 * (g // 2))
                                            v.wait_ge(sTRG.h,
                                                      b[id(sTRG)] + 5 * (g - 2) + 2)
                                    v.tensor_tensor(
                                        out=scoA[g % 2][:, p * 512:(p + 1) * 512],
                                        in0=PS[p % 2][:, :],
                                        in1=bndF[g % 2][:, p * 512:(p + 1) * 512],
                                        op=ALU.add).then_inc(sBAD.h, 1)
                                    sBAD.n += 1
                                v.wait_ge(sEXP.h, b[id(sEXP)] + g + 1)
                                v.reciprocal(out=rsc[:, 2 + g % 2:3 + g % 2],
                                             in_=rsc[:, g % 2:g % 2 + 1])
                            if g >= 1:
                                g1 = g - 1
                                for grp in range(2):
                                    gg = 5 * g1 + grp
                                    v.wait_ge(sTRG.h, b[id(sTRG)] + gg + 1)
                                    v.tensor_copy(
                                        out=ATs[:, grp * 1024:(grp + 1) * 1024],
                                        in_=psT[gg % 2][:, :]).then_inc(sGEV.h, 1)
                                    sGEV.n += 1
                            if g < NG:
                                if g == 0:
                                    v.tensor_copy(out=rsc[:, 3:4],
                                                  in_=czs[:, 0:1])
                                v.tensor_scalar(
                                    out=scoA[g % 2][:, :].bitcast(BF16)[:, 0:T],
                                    in0=bndE[g % 2][:, :],
                                    scalar1=rsc[:, 2 + g % 2:3 + g % 2],
                                    scalar2=None, op0=ALU.mult
                                    ).then_inc(sNRM.h, 1)
                                sNRM.n += 1
                            if g >= 1:
                                g1 = g - 1
                                h1, t1 = divmod(g1, NT)
                                off1 = 64 * (h1 % 2)
                                for grp in range(3):
                                    gg = 5 * g1 + 2 + grp
                                    nb = 8 if grp < 2 else 1
                                    v.wait_ge(sTRG.h, b[id(sTRG)] + gg + 1)
                                    v.tensor_copy(
                                        out=DTs[:, grp * 1024:grp * 1024 + nb * 128],
                                        in_=psT[gg % 2][:, 0:nb * 128]
                                        ).then_inc(sGEV.h, 1)
                                    sGEV.n += 1
                                v.wait_ge(sAV.h, b[id(sAV)] + g1 + 1)
                                v.tensor_copy(
                                    out=HO[h1 // 2][off1:off1 + 64,
                                                    t1 * 128:(t1 + 1) * 128],
                                    in_=psO[g1 % 2][off1:off1 + 64, :]
                                    ).then_inc(sOEV.h, 1)
                                sOEV.n += 1

                    @blk.gpsimd
                    def _(g_):
                        g_.wait_ge(sOEV.h, b[id(sOEV)] + NG)

            with ExitStack() as scw:
                e3 = scw.enter_context
                wbf2 = [e3(nc.sbuf_tensor(f"wo{i}", [128, D], BF16))
                        for i in range(NDC)]
                wtm2 = [e3(nc.sbuf_tensor(f"wotm{i}", [128, D], F32))
                        for i in range(2)]
                psW = [e3(nc.psum_tensor(f"psw{i}", [128, 512], F32))
                       for i in range(2)]
                sWL, sWC, sWM, sWA = SS[47:51]
                sWLb = [sWL, SS[63]]
                b = snapd(sWL, sWC, sWM, sWA, SS[63])
                with nc.Block() as blk:
                    @blk.sync
                    def _(s):
                        for c in range(NDC):
                            if c >= 2:
                                s.wait_ge(sWC.h, b[id(sWC)] + c - 1)
                            s.dma_start(out=wtm2[c % 2][:, :],
                                        in_=woT[c * 128:(c + 1) * 128, :]
                                        ).then_inc(sWLb[c % 2].h, 16)
                            sWLb[c % 2].n += 16

                    @blk.vector
                    def _(v):
                        for c in range(NDC):
                            v.wait_ge(sWLb[c % 2].h,
                                      b[id(sWLb[c % 2])] + 16 * (c // 2 + 1))
                            v.tensor_copy(out=wbf2[c][:, :],
                                          in_=wtm2[c % 2][:, :]).then_inc(sWC.h, 1)
                            sWC.n += 1
                        for j in range(32):
                            oc, p = divmod(j, 4)
                            v.wait_ge(sWM.h, b[id(sWM)] + j + 1)
                            xsl = X[oc][:, p * 512:(p + 1) * 512]
                            v.scalar_tensor_tensor(out=xsl, in0=psW[j % 2][:, :],
                                                   scalar=bos[:, oc:oc + 1],
                                                   in1=xsl, op0=ALU.add,
                                                   op1=ALU.add).then_inc(sWA.h, 1)
                            sWA.n += 1

                    @blk.tensor
                    def _(tn):
                        tn.wait_ge(sWC.h, b[id(sWC)] + NDC)
                        for j in range(32):
                            oc, p = divmod(j, 4)
                            if j >= 2:
                                tn.wait_ge(sWA.h, b[id(sWA)] + j - 1)
                            for cc in range(NDC):
                                mi = tn.matmul(
                                    psW[j % 2][:, :],
                                    wbf2[cc][:, oc * 128:(oc + 1) * 128],
                                    HO[cc][:, p * 512:(p + 1) * 512],
                                    start=(cc == 0), stop=(cc == NDC - 1))
                            mi.then_inc(sWM.h, 1)
                            sWM.n += 1

                    @blk.gpsimd
                    def _(g):
                        g.wait_ge(sWA.h, b[id(sWA)] + 32)
                        if dbg:
                            sst = SS[51]
                            for i in range(NDC):
                                g.dma_start(
                                    out=dbg_t["dbg_attn"][i * 128:(i + 1) * 128, :],
                                    in_=X[i][:, :]).then_inc(sst.h, 16)
                                sst.n += 16
                            for i in range(NDC):
                                g.dma_start(
                                    out=dbg_t["dbg_ho"][i * 128:(i + 1) * 128, :]
                                    .bitcast(BF16)[:, 0:T],
                                    in_=HO[i][:, :]).then_inc(sst.h, 16)
                                sst.n += 16
                            g.wait_ge(sst.h, sst.n)

        with ExitStack() as sc:
            e2 = sc.enter_context
            Xln = [e2(nc.sbuf_tensor(f"xln3{i}", [128, T], BF16))
                   for i in range(NDC)]
            psM = [e2(nc.psum_tensor(f"psm3{i}", [128, 512], F32))
                   for i in range(4)]
            psS = [e2(nc.psum_tensor(f"pss3{i}", [128, 512], F32))
                   for i in range(4)]
            with ExitStack() as scl:
                e3 = scl.enter_context
                rowsA = e3(nc.sbuf_tensor("rowsA3", [1, T], F32))
                rowsB = e3(nc.sbuf_tensor("rowsB3", [1, T], F32))
                bcA = e3(nc.sbuf_tensor("bcA3", [128, T], F32))
                bcB = e3(nc.sbuf_tensor("bcB3", [128, T], F32))
                xsq = [e3(nc.sbuf_tensor(f"xsq3{i}", [128, T], BF16))
                       for i in range(2)]
                vrow = e3(nc.sbuf_tensor("vrow3", [1, T], F32))
                tmpf = e3(nc.sbuf_tensor("tmpf3", [128, T], F32))
                with nc.Block() as blk:
                    ln_phase(blk, Xln, psM, psS, rowsA, rowsB, vrow, bcA, bcB,
                             xsq, tmpf)
            with ExitStack() as scf:
                e3 = scf.enter_context
                H = [e3(nc.sbuf_tensor(f"hb_{i}", [128, 1024], BF16))
                     for i in range(NFC)]
                w1s = [e3(nc.sbuf_tensor(f"w1sb_{i}", [128, 8, 128], BF16))
                       for i in range(2)]
                w2s = [e3(nc.sbuf_tensor(f"w2sb_{i}", [128, 32, 128], BF16))
                       for i in range(2)]
                with nc.Block() as blk:
                    ffn_phase(blk, Xln, H, w1s, w2s, psM, lambda g: None)

        with nc.Block() as blk:
            @blk.gpsimd
            def _(g):
                sst = SS[51]
                for i in range(NDC):
                    g.dma_start(out=yT[i * 128:(i + 1) * 128, :],
                                in_=X[i][:, :]).then_inc(sst.h, 16)
                    sst.n += 16
                g.wait_ge(sst.h, sst.n)

    return nc


def _prep_inputs(inputs):
    f = np.float32
    x = np.asarray(inputs["L0_attn_x"], f)

    def chunks(bv, n):
        return np.ascontiguousarray(np.asarray(bv, f).reshape(n, 128).T)

    relv = np.asarray(inputs["L0_attn_relative_v"], f)
    common = {
        "wqT": np.ascontiguousarray(np.asarray(inputs["L0_attn_Wq"], f).T),
        "wkT": np.ascontiguousarray(np.asarray(inputs["L0_attn_Wk"], f).T),
        "wvT": np.ascontiguousarray(np.asarray(inputs["L0_attn_Wv"], f).T),
        "woT": np.ascontiguousarray(np.asarray(inputs["L0_attn_Wo"], f).T),
        "w1T": np.ascontiguousarray(np.asarray(inputs["L0_ffn_W1"], f).T),
        "w2T": np.ascontiguousarray(np.asarray(inputs["L0_ffn_W2"], f).T),
        "bq_c": chunks(inputs["L0_attn_bq"], NDC),
        "bk_c": chunks(inputs["L0_attn_bk"], NDC),
        "bo_c": chunks(inputs["L0_attn_bo"], NDC),
        "b1_c": chunks(inputs["L0_ffn_b1"], NFC),
        "bv_r": np.asarray(inputs["L0_attn_bv"], f).reshape(1, D),
        "b2_r": np.asarray(inputs["L0_ffn_b2"], f).reshape(1, D),
        "relkT": np.ascontiguousarray(
            np.asarray(inputs["L0_attn_relative_k"], f).T),
        "rvpad": np.ascontiguousarray(
            relv[np.clip(np.arange(4096) - 1024, 0, 2046)]),
    }
    in_maps = []
    for c in range(8):
        m = dict(common)
        m["xT"] = np.ascontiguousarray(x[c].T)
        in_maps.append(m)
    return in_maps


def kernel(**inputs):
    in_maps = _prep_inputs(inputs)
    nc = build_nc(dbg=False)
    res = run_bass_kernel_spmd(nc, in_maps, core_ids=list(range(8)))
    out = np.stack([np.ascontiguousarray(r["yT"].T) for r in res.results])
    return out.astype(np.float32)


# version 10 (final)
# speedup vs baseline: 1.5469x; 1.0455x over previous
import numpy as np
from contextlib import ExitStack

import concourse.bass as bass
import concourse.mybir as mybir
from concourse.bass_utils import run_bass_kernel_spmd

F32 = mybir.dt.float32
BF16 = mybir.dt.bfloat16
FP16 = mybir.dt.float16
AF = mybir.ActivationFunctionType
ALU = mybir.AluOpType
AX = mybir.AxisListType

T = 2048
D = 1024
NH = 16
DK = 64
DF = 4096
NDC = D // 128
NFC = DF // 128
NT = T // 128
CSH = 32.0
EPS = 1e-5
WS = 4095
WA = 2560
NG = NH * NT
PIECES = [(0, 512), (512, 512), (1024, 512), (1536, 511)]


class Ctr:
    def __init__(self, h):
        self.h = h
        self.n = 0


def build_nc(dbg=False):
    nc = bass.Bass("TRN2", target_bir_lowering=False, detect_race_conditions=False)

    xT = nc.dram_tensor("xT", [D, T], F32, kind="ExternalInput").ap()
    wqT = nc.dram_tensor("wqT", [D, D], F32, kind="ExternalInput").ap()
    wkT = nc.dram_tensor("wkT", [D, D], F32, kind="ExternalInput").ap()
    wvT = nc.dram_tensor("wvT", [D, D], F32, kind="ExternalInput").ap()
    woT = nc.dram_tensor("woT", [D, D], F32, kind="ExternalInput").ap()
    w1T = nc.dram_tensor("w1T", [D, DF], F32, kind="ExternalInput").ap()
    w2T = nc.dram_tensor("w2T", [DF, D], F32, kind="ExternalInput").ap()
    bq_c = nc.dram_tensor("bq_c", [128, NDC], F32, kind="ExternalInput").ap()
    bk_c = nc.dram_tensor("bk_c", [128, NDC], F32, kind="ExternalInput").ap()
    bo_c = nc.dram_tensor("bo_c", [128, NDC], F32, kind="ExternalInput").ap()
    b1_c = nc.dram_tensor("b1_c", [128, NFC], F32, kind="ExternalInput").ap()
    bv_r = nc.dram_tensor("bv_r", [1, D], F32, kind="ExternalInput").ap()
    b2_r = nc.dram_tensor("b2_r", [1, D], F32, kind="ExternalInput").ap()
    relkT = nc.dram_tensor("relkT", [64, 2047], F32, kind="ExternalInput").ap()
    rvpad = nc.dram_tensor("rvpad", [4096, 64], F32, kind="ExternalInput").ap()
    yT = nc.dram_tensor("yT", [D, T], F32, kind="ExternalOutput").ap()
    dbg_t = {}
    if dbg:
        for name in ["dbg_ffn1", "dbg_ln2", "dbg_attn", "dbg_ho",
                     "dbg_x0", "dbg_xln1", "dbg_w1q"]:
            dbg_t[name] = nc.dram_tensor(name, [D, T], F32,
                                         kind="ExternalOutput").ap()
    qd = nc.dram_tensor("qd", [D, T], BF16, kind="Internal")
    kd = nc.dram_tensor("kd", [D, T], BF16, kind="Internal")
    vd = nc.dram_tensor("vd", [T, D], BF16, kind="Internal")
    w1q = nc.dram_tensor("w1q", [D, DF], BF16, kind="Internal")
    w2q = nc.dram_tensor("w2q", [DF, D], BF16, kind="Internal")
    spr = [nc.dram_tensor(f"spr{i}", [T * WS], FP16, kind="Internal")
           for i in range(2)]
    esc = [nc.dram_tensor(f"esc{i}", [T * WA], BF16, kind="Internal")
           for i in range(2)]
    rowscr = nc.dram_tensor("rowscr", [2 * T], F32, kind="Internal")
    dbg_rows = None
    if dbg:
        dbg_rows = nc.dram_tensor("dbg_rows", [2, T], F32,
                                  kind="ExternalOutput").ap()

    qdA = qd.ap()
    kdA = kd.ap()
    vdA = vd.ap()
    w1qA = w1q.ap()
    w2qA = w2q.ap()

    with ExitStack() as top:
        en = top.enter_context
        X = [en(nc.sbuf_tensor(f"x{i}", [128, T], F32)) for i in range(NDC)]
        tblA = en(nc.sbuf_tensor("tblA", [128, 2047], BF16))
        tblB = en(nc.sbuf_tensor("tblB", [128, 2047], BF16))
        RVsb = en(nc.sbuf_tensor("rvsb", [128, 32 * 64], BF16))
        ident = en(nc.sbuf_tensor("idnt", [128, 128], BF16))
        onec = en(nc.sbuf_tensor("onec", [128, 2], BF16))
        oner = en(nc.sbuf_tensor("oner", [1, 512], BF16))
        onerf = en(nc.sbuf_tensor("onerf", [1, 128], F32))
        bqs = en(nc.sbuf_tensor("bqs", [128, NDC], F32))
        bks = en(nc.sbuf_tensor("bks", [128, NDC], F32))
        bos = en(nc.sbuf_tensor("bos", [128, NDC], F32))
        b1s = en(nc.sbuf_tensor("b1s", [128, NFC], F32))
        bvr = en(nc.sbuf_tensor("bvr", [1, D], BF16))
        hb2r = en(nc.sbuf_tensor("hb2r", [1, D], BF16))
        thrs = en(nc.sbuf_tensor("thrs", [128, 4], F32))
        czs = en(nc.sbuf_tensor("czs", [128, 4], F32))
        SS = [Ctr(en(nc.semaphore(f"s{i}"))) for i in range(64)]

        def snap():
            return [s.n for s in SS]

        def snapd(*sems):
            return {id(s): s.n for s in sems}

        with ExitStack() as sc:
            e2 = sc.enter_context
            ztile = e2(nc.sbuf_tensor("ztile", [128, 1024], BF16))
            rtmp = e2(nc.sbuf_tensor("rtmp0", [128, 2048], F32))
            itmp = e2(nc.sbuf_tensor("itmp0", [128, 256], F32))
            rwb1 = e2(nc.sbuf_tensor("rwb1", [1, D], F32))
            rwb2 = e2(nc.sbuf_tensor("rwb2", [1, D], F32))
            s0, s1, s2, s3, s4 = SS[0], SS[1], SS[2], SS[3], SS[4]
            b = snap()
            with nc.Block() as blk:
                @blk.gpsimd
                def _(g):
                    for i in range(NDC):
                        g.dma_start(out=X[i][:, :],
                                    in_=xT[i * 128:(i + 1) * 128, :]
                                    ).then_inc(s0.h, 16)
                        s0.n += 16
                    for dst, srcd in [(bqs, bq_c), (bks, bk_c), (bos, bo_c),
                                      (b1s, b1_c)]:
                        g.dma_start(out=dst[:, :], in_=srcd).then_inc(s0.h, 16)
                        s0.n += 16
                    g.dma_start(out=rwb1[0:1, :], in_=bv_r).then_inc(s0.h, 16)
                    s0.n += 16
                    g.dma_start(out=rwb2[0:1, :], in_=b2_r).then_inc(s0.h, 16)
                    s0.n += 16
                    g.iota(itmp[:, 0:128], [[1, 128]], channel_multiplier=0,
                           allow_small_or_imprecise_dtypes=True)
                    g.iota(itmp[:, 128:256], [[0, 128]], channel_multiplier=1,
                           allow_small_or_imprecise_dtypes=True)
                    g.memset(ztile[:, :], 0)
                    g.memset(onec[:, :], 1.0)
                    g.memset(oner[:, :], 1.0)
                    g.memset(czs[:, 0:1], 0.0)
                    g.memset(czs[:, 1:2], -CSH)
                    g.memset(czs[:, 2:3], EPS)
                    g.memset(onerf[:, :], 1.0).then_inc(s1.h, 1)
                    s1.n += 1
                    g.wait_ge(s0.h, s0.n)

                @blk.sync
                def _(s):
                    s.dma_start(out=rtmp[0:64, 0:2047],
                                in_=relkT).then_inc(s2.h, 16)
                    s2.n += 16
                    s.dma_start(out=rtmp[64:128, 0:2047],
                                in_=relkT).then_inc(s2.h, 16)
                    s2.n += 16

                @blk.vector
                def _(v):
                    v.wait_ge(s1.h, b[1] + 1)
                    v.tensor_tensor(out=ident[:, :], in0=itmp[:, 0:128],
                                    in1=itmp[:, 128:256], op=ALU.is_equal)
                    v.wait_ge(s0.h, b[0] + 16 * 14)
                    v.tensor_scalar(out=bks[:, :], in0=bks[:, :], scalar1=0.125,
                                    scalar2=None, op0=ALU.mult)
                    v.tensor_copy(out=bvr[:, :], in_=rwb1[0:1, :])
                    v.tensor_scalar(out=hb2r[:, :], in0=rwb2[0:1, :],
                                    scalar1=0.5, scalar2=None, op0=ALU.mult)
                    v.wait_ge(s2.h, b[2] + 32)
                    v.scalar_tensor_tensor(
                        out=tblA[:, :], in0=rtmp[:, 0:2047], scalar=1.0,
                        in1=rtmp[:, 2046:2047].to_broadcast([128, 2047]),
                        op0=ALU.mult, op1=ALU.subtract)
                    v.scalar_tensor_tensor(
                        out=tblB[:, :], in0=rtmp[:, 0:2047], scalar=1.0,
                        in1=rtmp[:, 0:1].to_broadcast([128, 2047]),
                        op0=ALU.mult, op1=ALU.subtract).then_inc(s3.h, 1)
                    s3.n += 1
                    v.wait_ge(s2.h, b[2] + 48)
                    v.tensor_copy(out=RVsb[:, :], in_=rtmp[:, 0:2048]).then_inc(
                        s3.h, 1)
                    s3.n += 1

                @blk.sync
                def _(s):
                    s.wait_ge(s3.h, b[3] + 1)
                    s.dma_start(
                        out=rtmp[:, 0:2048].rearrange("p (c k) -> p c k", k=64),
                        in_=bass.AP(rvpad.tensor, 0, [[64, 128], [128 * 64, 32], [1, 64]])
                    ).then_inc(s2.h, 16)
                    s2.n += 16

                @blk.scalar
                def _(s):
                    pass
                    s.wait_ge(s1.h, b[1] + 1)
                    for bi in range(2):
                        for rt in range(NT):
                            for off_, wid_, tsr in [
                                    (rt * 128 * WS, 1024, spr[bi]),
                                    (rt * 128 * WS + 3071, 1024, spr[bi]),
                                    (rt * 128 * WA, 256, esc[bi]),
                                    (rt * 128 * WA + 2304, 256, esc[bi])]:
                                zsrc = (ztile[:, 0:wid_].bitcast(FP16)[:, 0:wid_]
                                        if wid_ == 1024 else ztile[:, 0:wid_])
                                s.dma_start(out=bass.AP(tsr, off_,
                                                        [[WS if wid_ == 1024 else WA, 128],
                                                         [1, wid_]]),
                                            in_=zsrc).then_inc(s4.h, 16)
                                s4.n += 16
                    s.wait_ge(s4.h, s4.n)
                    s.wait_ge(s3.h, b[3] + 2)

        with ExitStack() as sc:
            e2 = sc.enter_context
            wtmp = [e2(nc.sbuf_tensor(f"wt{i}", [128, DF], F32)) for i in range(2)]
            wabs = e2(nc.sbuf_tensor("wabs", [128, DF], FP16))
            wqo = [e2(nc.sbuf_tensor(f"wqo{i}", [128, DF], BF16)) for i in range(2)]
            qrows = e2(nc.sbuf_tensor("qrows", [1, 600], F32))
            psQ = e2(nc.psum_tensor("psq", [1, 512], F32))
            psB = e2(nc.psum_tensor("psb", [128, 2], F32))

            def quant_pass(blk, which):
                src, dstA = (w1T, w1qA) if which == 0 else (w2T, w2qA)
                nch = NDC if which == 0 else NFC
                wid = DF if which == 0 else D
                npc = wid // 512
                ntot = float(D * DF)
                thr_col = thrs[:, 2 * which:2 * which + 1]
                nthr_col = thrs[:, 2 * which + 1:2 * which + 2]
                sLD, sAB, sMM, sRW, sBC, sTH, sQO, sSO = SS[5:13]
                sLDb = [sLD, SS[52]]
                sSOb = [sSO, SS[53]]
                b = snapd(sLD, sAB, sMM, sRW, sBC, sTH, sQO, sSO,
                          SS[52], SS[53])

                @blk.sync
                def _(s):
                    for c in range(nch):
                        if c >= 2:
                            s.wait_ge(sAB.h, b[id(sAB)] + c - 1)
                        sl = sLDb[c % 2]
                        s.dma_start(out=wtmp[c % 2][:, 0:wid],
                                    in_=src[c * 128:(c + 1) * 128, :]
                                    ).then_inc(sl.h, 16)
                        sl.n += 16
                    for c in range(nch):
                        s.wait_ge(sTH.h, b[id(sTH)] + 1)
                        if c >= 2:
                            s.wait_ge(sQO.h, b[id(sQO)] + c - 1)
                        sl = sLDb[c % 2]
                        s.dma_start(out=wtmp[c % 2][:, 0:wid],
                                    in_=src[c * 128:(c + 1) * 128, :]
                                    ).then_inc(sl.h, 16)
                        sl.n += 16

                @blk.scalar
                def _(s):
                    for c in range(nch):
                        s.wait_ge(sLDb[c % 2].h,
                                  b[id(sLDb[c % 2])] + 16 * (c // 2 + 1))
                        if c >= 1:
                            s.wait_ge(sMM.h, b[id(sMM)] + c * npc)
                        s.activation(out=wabs[:, 0:wid], in_=wtmp[c % 2][:, 0:wid],
                                     func=AF.Abs, bias=czs[:, 0:1], scale=1.0
                                     ).then_inc(sAB.h, 1)
                        sAB.n += 1

                @blk.tensor
                def _(tn):
                    for c in range(nch):
                        tn.wait_ge(sAB.h, b[id(sAB)] + c + 1)
                        for p in range(npc):
                            tn.matmul(psQ[0:1, :],
                                      onec[:, 0:1], wabs[:, p * 512:(p + 1) * 512],
                                      start=(c == 0 and p == 0),
                                      stop=(c == nch - 1 and p == npc - 1)
                                      ).then_inc(sMM.h, 1)
                            sMM.n += 1
                    tn.wait_ge(sRW.h, b[id(sRW)] + 1)
                    tn.matmul(psB[:, 0:1], onerf[0:1, :], qrows[0:1, 577:578],
                              start=True, stop=True)
                    tn.matmul(psB[:, 1:2], onerf[0:1, :], qrows[0:1, 578:579],
                              start=True, stop=True).then_inc(sBC.h, 1)
                    sBC.n += 1

                @blk.vector
                def _(v):
                    def gap():
                        v.tensor_copy(out=czs[:, 3:4], in_=czs[:, 0:1])
                        v.tensor_copy(out=czs[:, 3:4], in_=czs[:, 0:1])
                    v.wait_ge(sMM.h, b[id(sMM)] + nch * npc)
                    v.reduce_sum(out=qrows[0:1, 0:64],
                                 in_=psQ[0:1, :].rearrange(
                                     "p (a c) -> p a c", c=8), axis=AX.X)
                    gap()
                    v.reduce_sum(out=qrows[0:1, 512:520],
                                 in_=qrows[0:1, 0:64].rearrange(
                                     "p (a c) -> p a c", c=8), axis=AX.X)
                    gap()
                    v.reduce_sum(out=qrows[0:1, 576:577],
                                 in_=qrows[0:1, 512:520], axis=AX.X)
                    gap()
                    v.tensor_scalar(out=qrows[0:1, 577:578],
                                    in0=qrows[0:1, 576:577],
                                    scalar1=0.5 / ntot, scalar2=0.5 * EPS,
                                    op0=ALU.mult, op1=ALU.add)
                    v.tensor_scalar(out=qrows[0:1, 578:579],
                                    in0=qrows[0:1, 576:577],
                                    scalar1=-0.5 / ntot, scalar2=-0.5 * EPS,
                                    op0=ALU.mult, op1=ALU.add).then_inc(sRW.h, 1)
                    sRW.n += 1
                    v.wait_ge(sBC.h, b[id(sBC)] + 1)
                    v.tensor_copy(out=thr_col, in_=psB[:, 0:1])
                    v.tensor_copy(out=nthr_col, in_=psB[:, 1:2]).then_inc(sTH.h, 1)
                    sTH.n += 1
                    sca = 0.5 if which == 1 else 1.0
                    for c in range(nch):
                        v.wait_ge(sLDb[c % 2].h,
                                  b[id(sLDb[c % 2])]
                                  + 16 * (nch // 2 + c // 2 + 1))
                        if c >= 2:
                            v.wait_ge(sSOb[c % 2].h,
                                      b[id(sSOb[c % 2])] + 16 * (c // 2))
                        v.tensor_scalar(out=wabs[:, 0:wid],
                                        in0=wtmp[c % 2][:, 0:wid],
                                        scalar1=thr_col, scalar2=sca,
                                        op0=ALU.is_gt, op1=ALU.mult)
                        v.tensor_scalar(out=wqo[c % 2][:, 0:wid],
                                        in0=wtmp[c % 2][:, 0:wid],
                                        scalar1=nthr_col, scalar2=sca,
                                        op0=ALU.is_lt, op1=ALU.mult)
                        v.tensor_tensor(out=wqo[c % 2][:, 0:wid],
                                        in0=wabs[:, 0:wid],
                                        in1=wqo[c % 2][:, 0:wid],
                                        op=ALU.subtract).then_inc(sQO.h, 1)
                        sQO.n += 1

                @blk.gpsimd
                def _(g):
                    for c in range(nch):
                        g.wait_ge(sQO.h, b[id(sQO)] + c + 1)
                        so_ = sSOb[c % 2]
                        g.dma_start(out=dstA[c * 128:(c + 1) * 128, :],
                                    in_=wqo[c % 2][:, 0:wid]).then_inc(so_.h, 16)
                        so_.n += 16
                    g.wait_ge(sSOb[0].h, sSOb[0].n)
                    g.wait_ge(sSOb[1].h, sSOb[1].n)

            with nc.Block() as blk:
                quant_pass(blk, 0)
            with nc.Block() as blk:
                quant_pass(blk, 1)
            if dbg:
                with nc.Block() as blk:
                    @blk.gpsimd
                    def _(g):
                        sst = SS[51]
                        for i in range(NDC):
                            g.dma_start(out=dbg_t["dbg_x0"][i * 128:(i + 1) * 128, :],
                                        in_=X[i][:, :]).then_inc(sst.h, 16)
                            sst.n += 16
                        g.dma_start(out=dbg_t["dbg_w1q"][0:128, :].bitcast(BF16)[:, 0:T],
                                    in_=w1qA[0:128, 0:T]).then_inc(sst.h, 16)
                        sst.n += 16
                        g.wait_ge(sst.h, sst.n)

        def ln_phase(blk, Xln, psM, psS, rowsA, rowsB, vrow, bcA, bcB,
                     xsq, tmpf):
            sCS, sMM, sVAR, sRSD, sMR, sRST, sBCL, sAPP = SS[13:21]
            b = snapd(sCS, sMM, sVAR, sRSD, sMR, sRST, sBCL, sAPP)

            @blk.scalar
            def _(s):
                for dc in range(NDC):
                    if dc >= 2:
                        s.wait_ge(sMM.h, b[id(sMM)] + 4 * (dc - 1))
                    s.copy(out=Xln[dc][:, :], in_=X[dc][:, :])
                    s.activation(out=xsq[dc % 2][:, :], in_=X[dc][:, :],
                                 func=AF.Square, bias=czs[:, 0:1], scale=1.0
                                 ).then_inc(sCS.h, 1)
                    sCS.n += 1
                s.wait_ge(sVAR.h, b[id(sVAR)] + 1)
                s.activation(out=rowsB[0:1, :], in_=vrow[0:1, :], func=AF.Ln,
                             bias=czs[0:1, 2:3], scale=1.0)
                s.activation(out=rowsB[0:1, :], in_=rowsB[0:1, :], func=AF.Exp,
                             bias=czs[0:1, 0:1], scale=-0.5).then_inc(sRSD.h, 1)
                sRSD.n += 1
                s.dma_start(out=bass.AP(rowscr, 0, [[1, 1], [1, T]]),
                            in_=rowsB[0:1, :]).then_inc(sRST.h, 16)
                sRST.n += 16
                s.dma_start(out=bass.AP(rowscr, T, [[1, 1], [1, T]]),
                            in_=vrow[0:1, :]).then_inc(sRST.h, 16)
                sRST.n += 16
                s.wait_ge(sRST.h, sRST.n)

            @blk.tensor
            def _(tn):
                for dc in range(NDC):
                    tn.wait_ge(sCS.h, b[id(sCS)] + dc + 1)
                    for p in range(4):
                        tn.matmul(psM[p][0:1, :], onec[:, 0:1],
                                  Xln[dc][:, p * 512:(p + 1) * 512],
                                  start=(dc == 0), stop=(dc == NDC - 1))
                        tn.matmul(psS[p][0:1, :], onec[:, 0:1],
                                  xsq[dc % 2][:, p * 512:(p + 1) * 512],
                                  start=(dc == 0), stop=(dc == NDC - 1)
                                  ).then_inc(sMM.h, 1)
                        sMM.n += 1
                tn.wait_ge(sMR.h, b[id(sMR)] + 1)
                for p in range(4):
                    tn.matmul(psM[p][:, :], onerf[0:1, :],
                              rowsB[0:1, p * 512:(p + 1) * 512],
                              start=True, stop=True)
                    tn.matmul(psS[p][:, :], onerf[0:1, :],
                              rowsA[0:1, p * 512:(p + 1) * 512],
                              start=True, stop=True).then_inc(sBCL.h, 1)
                    sBCL.n += 1

            @blk.vector
            def _(v):
                def gap():
                    v.tensor_copy(out=czs[:, 3:4], in_=czs[:, 0:1])
                    v.tensor_copy(out=czs[:, 3:4], in_=czs[:, 0:1])
                v.wait_ge(sMM.h, b[id(sMM)] + 4 * NDC)
                for p in range(4):
                    v.tensor_scalar(out=rowsA[0:1, p * 512:(p + 1) * 512],
                                    in0=psM[p][0:1, :], scalar1=1.0 / D,
                                    scalar2=None, op0=ALU.mult)
                gap()
                v.tensor_tensor(out=rowsB[0:1, :], in0=rowsA[0:1, :],
                                in1=rowsA[0:1, :], op=ALU.mult)
                gap()
                for p in range(4):
                    inst = v.scalar_tensor_tensor(
                        out=vrow[0:1, p * 512:(p + 1) * 512],
                        in0=psS[p][0:1, :], scalar=1.0 / D,
                        in1=rowsB[0:1, p * 512:(p + 1) * 512],
                        op0=ALU.mult, op1=ALU.subtract)
                gap()
                inst.then_inc(sVAR.h, 1)
                sVAR.n += 1
                v.wait_ge(sRSD.h, b[id(sRSD)] + 1)
                v.tensor_tensor(out=rowsA[0:1, :], in0=rowsA[0:1, :],
                                in1=rowsB[0:1, :], op=ALU.mult)
                gap()
                nc.vector.sem_inc(sMR.h, 1)
                sMR.n += 1
                v.wait_ge(sBCL.h, b[id(sBCL)] + 4)
                for p in range(4):
                    v.tensor_copy(out=bcA[:, p * 512:(p + 1) * 512],
                                  in_=psM[p][:, :])
                    v.tensor_copy(out=bcB[:, p * 512:(p + 1) * 512],
                                  in_=psS[p][:, :])
                for dc in range(NDC):
                    v.tensor_tensor(out=tmpf[:, :], in0=X[dc][:, :],
                                    in1=bcA[:, :], op=ALU.mult)
                    v.tensor_tensor(out=Xln[dc][:, :], in0=tmpf[:, :],
                                    in1=bcB[:, :], op=ALU.subtract
                                    ).then_inc(sAPP.h, 1)
                    sAPP.n += 1

            @blk.gpsimd
            def _(g):
                g.wait_ge(sAPP.h, b[id(sAPP)] + NDC)

        def ffn_phase(blk, Xln, H, w1s, w2s, psF, evt):
            sL1, sL2, sM1, sRL, sM2, sXA = SS[21:27]
            sL1b = [sL1, SS[54]]
            sL2b = [sL2, SS[55]]
            b = snapd(sL1, sL2, sM1, sRL, sM2, sXA, SS[54], SS[55])

            @blk.sync
            def _(s):
                for th in range(2):
                    for fc in range(NFC):
                        i = th * NFC + fc
                        if i >= 2:
                            s.wait_ge(sM1.h, b[id(sM1)] + 2 * (i - 1))
                        s.dma_start(out=w1s[i % 2][:, :, :],
                                    in_=bass.AP(w1q, 128 * fc,
                                                [[DF, 128], [128 * DF, 8],
                                                 [1, 128]])).then_inc(
                                                     sL1b[i % 2].h, 16)
                        sL1b[i % 2].n += 16
                    for dc2 in range(NDC):
                        i = th * NDC + dc2
                        if i >= 2:
                            s.wait_ge(sM2.h, b[id(sM2)] + 2 * (i - 1))
                        s.dma_start(out=w2s[i % 2][:, :, :],
                                    in_=bass.AP(w2q, 128 * dc2,
                                                [[D, 128], [128 * D, 32],
                                                 [1, 128]])).then_inc(
                                                     sL2b[i % 2].h, 16)
                        sL2b[i % 2].n += 16

            @blk.tensor
            def _(tn):
                for th in range(2):
                    for fc in range(NFC):
                        i = th * NFC + fc
                        tn.wait_ge(sL1b[i % 2].h,
                                   b[id(sL1b[i % 2])] + 16 * (i // 2 + 1))
                        for p in range(2):
                            if i >= 1:
                                tn.wait_ge(sRL.h, b[id(sRL)] + 2 * i - 1 + p)
                            for dc in range(NDC):
                                mi = tn.matmul(
                                    psF[p][:, :], w1s[i % 2][:, dc, :],
                                    Xln[dc][:, th * 1024 + p * 512:
                                            th * 1024 + (p + 1) * 512],
                                    start=(dc == 0), stop=(dc == NDC - 1))
                            mi.then_inc(sM1.h, 1)
                            sM1.n += 1
                    for dc2 in range(NDC):
                        i = th * NDC + dc2
                        tn.wait_ge(sL2b[i % 2].h,
                                   b[id(sL2b[i % 2])] + 16 * (i // 2 + 1))
                        tn.wait_ge(sRL.h, b[id(sRL)] + 64 * (th + 1))
                        for p in range(2):
                            if i >= 1:
                                tn.wait_ge(sXA.h, b[id(sXA)] + 2 * i - 1 + p)
                            for fc in range(NFC):
                                tn.matmul(psF[2 + p][:, :], w2s[i % 2][:, fc, :],
                                          H[fc][:, p * 512:(p + 1) * 512],
                                          start=(fc == 0), stop=False)
                            tn.matmul(psF[2 + p][:, :],
                                      hb2r[0:1, dc2 * 128:(dc2 + 1) * 128],
                                      oner[0:1, :], start=False, stop=True
                                      ).then_inc(sM2.h, 1)
                            sM2.n += 1

            @blk.scalar
            def _(s):
                for th in range(2):
                    if th == 1:
                        s.wait_ge(sM2.h, b[id(sM2)] + 16)
                    for fc in range(NFC):
                        i = th * NFC + fc
                        for p in range(2):
                            s.wait_ge(sM1.h, b[id(sM1)] + 2 * i + p + 1)
                            s.activation(out=H[fc][:, p * 512:(p + 1) * 512],
                                         in_=psF[p][:, :], func=AF.Relu,
                                         bias=b1s[:, fc:fc + 1], scale=1.0
                                         ).then_inc(sRL.h, 1)
                            sRL.n += 1

            @blk.vector
            def _(v):
                for th in range(2):
                    for dc2 in range(NDC):
                        i = th * NDC + dc2
                        for p in range(2):
                            v.wait_ge(sM2.h, b[id(sM2)] + 2 * i + p + 1)
                            xsl = X[dc2][:, th * 1024 + p * 512:
                                         th * 1024 + (p + 1) * 512]
                            v.tensor_tensor(out=xsl, in0=psF[2 + p][:, :],
                                            in1=xsl, op=ALU.add).then_inc(sXA.h, 1)
                            sXA.n += 1

            @blk.gpsimd
            def _(g):
                g.wait_ge(sXA.h, b[id(sXA)] + 32)
                evt(g)

        with ExitStack() as sc:
            e2 = sc.enter_context
            Xln = [e2(nc.sbuf_tensor(f"xln{i}", [128, T], BF16))
                   for i in range(NDC)]
            psM = [e2(nc.psum_tensor(f"psm{i}", [128, 512], F32))
                   for i in range(4)]
            psS = [e2(nc.psum_tensor(f"pss{i}", [128, 512], F32))
                   for i in range(4)]

            lnctr = [0]

            def run_ln():
                lnctr[0] += 1
                u = lnctr[0]
                with ExitStack() as scl:
                    e3 = scl.enter_context
                    rowsA = e3(nc.sbuf_tensor(f"rowsA_{u}", [1, T], F32))
                    rowsB = e3(nc.sbuf_tensor(f"rowsB_{u}", [1, T], F32))
                    bcA = e3(nc.sbuf_tensor(f"bcA_{u}", [128, T], F32))
                    bcB = e3(nc.sbuf_tensor(f"bcB_{u}", [128, T], F32))
                    xsq = [e3(nc.sbuf_tensor(f"xsq_{u}_{i}", [128, T], BF16))
                           for i in range(2)]
                    vrow = e3(nc.sbuf_tensor(f"vrow_{u}", [1, T], F32))
                    tmpf = e3(nc.sbuf_tensor(f"tmpfl_{u}", [128, T], F32))
                    with nc.Block() as blk:
                        ln_phase(blk, Xln, psM, psS, rowsA, rowsB, vrow,
                                 bcA, bcB, xsq, tmpf)

            run_ln()
            if dbg:
                with nc.Block() as blk:
                    @blk.gpsimd
                    def _(g):
                        sst = SS[51]
                        for i in range(NDC):
                            g.dma_start(out=dbg_t["dbg_xln1"][i * 128:(i + 1) * 128, :]
                                        .bitcast(BF16)[:, 0:T],
                                        in_=Xln[i][:, :]).then_inc(sst.h, 16)
                            sst.n += 16
                        g.dma_start(out=dbg_rows,
                                    in_=bass.AP(rowscr, 0, [[T, 2], [1, T]])
                                    ).then_inc(sst.h, 16)
                        sst.n += 16
                        g.wait_ge(sst.h, sst.n)
            with ExitStack() as scf:
                e3 = scf.enter_context
                H = [e3(nc.sbuf_tensor(f"ha_{i}", [128, 1024], BF16))
                     for i in range(NFC)]
                w1s = [e3(nc.sbuf_tensor(f"w1s{i}", [128, 8, 128], BF16))
                       for i in range(2)]
                w2s = [e3(nc.sbuf_tensor(f"w2s{i}", [128, 32, 128], BF16))
                       for i in range(2)]

                def evt1(g):
                    if dbg:
                        sst = SS[51]
                        for i in range(NDC):
                            g.dma_start(out=dbg_t["dbg_ffn1"][i * 128:(i + 1) * 128, :],
                                        in_=X[i][:, :]).then_inc(sst.h, 16)
                            sst.n += 16
                        g.wait_ge(sst.h, sst.n)
                with nc.Block() as blk:
                    ffn_phase(blk, Xln, H, w1s, w2s, psM, evt1)

            run_ln()
            if dbg:
                with nc.Block() as blk:
                    @blk.gpsimd
                    def _(g):
                        sst = SS[51]
                        for i in range(NDC):
                            g.dma_start(out=dbg_t["dbg_ln2"][i * 128:(i + 1) * 128, :]
                                        .bitcast(BF16)[:, 0:T],
                                        in_=Xln[i][:, :]).then_inc(sst.h, 16)
                            sst.n += 16
                        g.wait_ge(sst.h, sst.n)

            with ExitStack() as scp:
                e3 = scp.enter_context
                wbf = [e3(nc.sbuf_tensor(f"wbf{i}", [128, D], BF16))
                       for i in range(NDC)]
                wtm = [e3(nc.sbuf_tensor(f"wtm{i}", [128, D], F32))
                       for i in range(2)]
                otile = [e3(nc.sbuf_tensor(f"ot{i}", [128, 512], BF16))
                         for i in range(2)]
                psP = [psM[0], psM[1]]
                sWL, sWC, sPM, sPE, sPS = SS[27:32]
                sWLb = [sWL, SS[56]]
                sPSb = [sPS, SS[57]]
                b = snapd(sWL, sWC, sPM, sPE, sPS, SS[56], SS[57])
                with nc.Block() as blk:
                    @blk.sync
                    def _(s):
                        for wi, wsrc in enumerate([wqT, wkT, wvT]):
                            for c in range(NDC):
                                i = wi * NDC + c
                                if i >= 2:
                                    s.wait_ge(sWC.h, b[id(sWC)] + i - 1)
                                s.dma_start(out=wtm[c % 2][:, :],
                                            in_=wsrc[c * 128:(c + 1) * 128, :]
                                            ).then_inc(sWLb[i % 2].h, 16)
                                sWLb[i % 2].n += 16

                    @blk.vector
                    def _(v):
                        for wi in range(3):
                            for c in range(NDC):
                                i = wi * NDC + c
                                v.wait_ge(sWLb[i % 2].h,
                                          b[id(sWLb[i % 2])] + 16 * (i // 2 + 1))
                                if wi > 0:
                                    v.wait_ge(sPM.h, b[id(sPM)] + wi * 32)
                                v.tensor_copy(out=wbf[c][:, :],
                                              in_=wtm[c % 2][:, :]
                                              ).then_inc(sWC.h, 1)
                                sWC.n += 1
                            for j in range(32):
                                jj = wi * 32 + j
                                v.wait_ge(sPM.h, b[id(sPM)] + jj + 1)
                                if jj >= 2:
                                    v.wait_ge(sPSb[jj % 2].h,
                                              b[id(sPSb[jj % 2])] + 16 * (jj // 2))
                                if wi == 0:
                                    oc = j // 4
                                    ei = v.tensor_scalar(out=otile[j % 2][:, :],
                                                         in0=psP[j % 2][:, :],
                                                         scalar1=bqs[:, oc:oc + 1],
                                                         scalar2=None, op0=ALU.add)
                                elif wi == 1:
                                    oc = j // 4
                                    ei = v.tensor_scalar(out=otile[j % 2][:, :],
                                                         in0=psP[j % 2][:, :],
                                                         scalar1=0.125,
                                                         scalar2=bks[:, oc:oc + 1],
                                                         op0=ALU.mult, op1=ALU.add)
                                else:
                                    ei = v.tensor_copy(out=otile[j % 2][:, :],
                                                       in_=psP[j % 2][:, :])
                                ei.then_inc(sPE.h, 1)
                                sPE.n += 1

                    @blk.tensor
                    def _(tn):
                        tn.wait_ge(sWC.h, b[id(sWC)] + NDC)
                        for wi in range(3):
                            if wi > 0:
                                tn.wait_ge(sWC.h, b[id(sWC)] + (wi + 1) * NDC)
                            for j in range(32):
                                jj = wi * 32 + j
                                if j >= 2:
                                    tn.wait_ge(sPE.h, b[id(sPE)] + jj - 1)
                                if wi < 2:
                                    oc, p = divmod(j, 4)
                                    for ic in range(NDC):
                                        mi = tn.matmul(
                                            psP[j % 2][:, :],
                                            wbf[ic][:, oc * 128:(oc + 1) * 128],
                                            Xln[ic][:, p * 512:(p + 1) * 512],
                                            start=(ic == 0), stop=(ic == NDC - 1))
                                else:
                                    tc, p = divmod(j, 2)
                                    for dc in range(NDC):
                                        tn.matmul(
                                            psP[j % 2][:, :],
                                            Xln[dc][:, tc * 128:(tc + 1) * 128],
                                            wbf[dc][:, p * 512:(p + 1) * 512],
                                            start=(dc == 0), stop=False)
                                    mi = tn.matmul(psP[j % 2][:, :],
                                                   oner[0:1, 0:128],
                                                   bvr[0:1, p * 512:(p + 1) * 512],
                                                   start=False, stop=True)
                                mi.then_inc(sPM.h, 1)
                                sPM.n += 1

                    @blk.scalar
                    def _(s):
                        for wi in range(3):
                            for j in range(32):
                                jj = wi * 32 + j
                                s.wait_ge(sPE.h, b[id(sPE)] + jj + 1)
                                if wi == 0:
                                    oc, p = divmod(j, 4)
                                    dst = qdA[oc * 128:(oc + 1) * 128,
                                              p * 512:(p + 1) * 512]
                                elif wi == 1:
                                    oc, p = divmod(j, 4)
                                    dst = kdA[oc * 128:(oc + 1) * 128,
                                              p * 512:(p + 1) * 512]
                                else:
                                    tc, p = divmod(j, 2)
                                    dst = vdA[tc * 128:(tc + 1) * 128,
                                              p * 512:(p + 1) * 512]
                                s.dma_start(out=dst, in_=otile[j % 2][:, :]
                                            ).then_inc(sPSb[jj % 2].h, 16)
                                sPSb[jj % 2].n += 16
                        s.wait_ge(sPSb[0].h, sPSb[0].n)
                        s.wait_ge(sPSb[1].h, sPSb[1].n)

        with ExitStack() as sc:
            e2 = sc.enter_context
            HO = [e2(nc.sbuf_tensor(f"ho{i}", [128, T], BF16))
                  for i in range(NDC)]
            with ExitStack() as sca:
                e3 = sca.enter_context
                Qp = [e3(nc.sbuf_tensor(f"qp{i}", [128, T], BF16))
                      for i in range(2)]
                Kp = [e3(nc.sbuf_tensor(f"kp{i}", [128, T], BF16))
                      for i in range(2)]
                Vp = [e3(nc.sbuf_tensor(f"vp{i}", [128, 16 * 128], BF16))
                      for i in range(2)]
                scoA = [e3(nc.sbuf_tensor(f"sco{i}", [128, T], F32))
                        for i in range(2)]
                bndE = [e3(nc.sbuf_tensor(f"bnd{i}", [128, T], BF16))
                        for i in range(2)]
                bndF = [e3(nc.sbuf_tensor(f"bnf{i}", [128, T], FP16))
                        for i in range(2)]
                Ssb = [e3(nc.sbuf_tensor(f"ssb{i}", [128, 2047], FP16))
                       for i in range(2)]
                ATs = e3(nc.sbuf_tensor("ats", [128, T], BF16))
                DTs = e3(nc.sbuf_tensor("dts", [128, 2176], BF16))
                Adg = [e3(nc.sbuf_tensor(f"adg{i}", [128, 2176], BF16))
                       for i in range(2)]
                rsc = e3(nc.sbuf_tensor("rsc", [128, 4], F32))
                PS = [e3(nc.psum_tensor(f"aps{i}", [128, 512], F32))
                      for i in range(2)]
                psT = [e3(nc.psum_tensor(f"apt{i}", [128, 1024], BF16))
                       for i in range(4)]
                psO = [e3(nc.psum_tensor(f"apo{i}", [128, 128], F32))
                       for i in range(2)]
                (sQPL, sSR, sSRE, sSST, sBRD, sQK, sBAD, sEXP, sNRM, sAST,
                 sDRD, sTRG, sGEV, sAV, sOEV) = SS[32:47]
                sQPLb = [sQPL, SS[58]]
                sSSTb = [sSST, SS[59]]
                sBRDb = [sBRD, SS[60]]
                sASTb = [sAST, SS[61]]
                sDRDb = [sDRD, SS[62]]
                b = snapd(sQPL, sSR, sSRE, sSST, sBRD, sQK, sBAD, sEXP,
                          sNRM, sAST, sDRD, sTRG, sGEV, sAV, sOEV,
                          SS[58], SS[59], SS[60], SS[61], SS[62])

                def consumed_wait(eng, q):
                    prev = q - 2
                    if prev < 0:
                        return
                    gp, jp = divmod(prev, 8)
                    if jp < 4:
                        eng.wait_ge(sSRE.h, b[id(sSRE)] + 4 * gp + jp + 1)
                    else:
                        eng.wait_ge(sBAD.h, b[id(sBAD)] + 4 * gp + (jp - 4) + 1)

                with nc.Block() as blk:
                    @blk.sync
                    def _(s):
                        for g in range(NG):
                            h, t = divmod(g, NT)
                            if t == 0 and h % 2 == 0:
                                hp = h // 2
                                if h >= 4:
                                    s.wait_ge(sQK.h, b[id(sQK)] + 64 * (h - 2))
                                    s.wait_ge(sAV.h, b[id(sAV)] + NT * (h - 2))
                                sq = sQPLb[hp % 2]
                                s.dma_start(out=Qp[hp % 2][:, :],
                                            in_=qdA[h * 64:(h + 2) * 64, :]
                                            ).then_inc(sq.h, 16)
                                s.dma_start(out=Kp[hp % 2][:, :],
                                            in_=kdA[h * 64:(h + 2) * 64, :]
                                            ).then_inc(sq.h, 16)
                                s.dma_start(
                                    out=Vp[hp % 2][:, :].rearrange(
                                        "p (c k) -> p c k", k=128),
                                    in_=bass.AP(vd, h * 64,
                                                [[D, 128], [128 * D, 16],
                                                 [1, 128]])).then_inc(sq.h, 16)
                                sq.n += 48
                            s.wait_ge(sSSTb[g % 2].h,
                                      b[id(sSSTb[g % 2])] + 16 * (g // 2 + 1))
                            if g >= 2:
                                s.wait_ge(sNRM.h, b[id(sNRM)] + g - 1)
                            s.dma_start(out=bndF[g % 2][:, :],
                                        in_=bass.AP(spr[h % 2],
                                                    t * 128 * (WS - 1) + 2047,
                                                    [[WS - 1, 128], [1, 2048]])
                                        ).then_inc(sBRDb[g % 2].h, 16)
                            sBRDb[g % 2].n += 16
                            s.wait_ge(sASTb[g % 2].h,
                                      b[id(sASTb[g % 2])] + 16 * (g // 2 + 1))
                            if g >= 2:
                                s.wait_ge(sTRG.h, b[id(sTRG)] + 5 * (g - 1))
                            s.dma_start(out=Adg[g % 2][:, :],
                                        in_=bass.AP(esc[h % 2],
                                                    t * 128 * WA + 129,
                                                    [[WA + 1, 128], [1, 2176]])
                                        ).then_inc(sDRDb[g % 2].h, 16)
                            sDRDb[g % 2].n += 16

                    @blk.scalar
                    def _(s):
                        for g in range(NG):
                            h, t = divmod(g, NT)
                            s.wait_ge(sSRE.h, b[id(sSRE)] + 4 * g + 4)
                            if h >= 2:
                                gp = (h - 2) * NT + t
                                s.wait_ge(sBRDb[gp % 2].h,
                                          b[id(sBRDb[gp % 2])] + 16 * (gp // 2 + 1))
                            s.dma_start(out=bass.AP(spr[h % 2],
                                                    t * 128 * WS + 1024,
                                                    [[WS, 128], [1, 2047]]),
                                        in_=Ssb[g % 2][:, :]
                                        ).then_inc(sSSTb[g % 2].h, 16)
                            sSSTb[g % 2].n += 16
                            s.wait_ge(sBAD.h, b[id(sBAD)] + 4 * g + 4)
                            s.activation(out=bndE[g % 2][:, :],
                                         in_=scoA[g % 2][:, :], func=AF.Exp,
                                         bias=czs[:, 1:2], scale=1.0,
                                         accum_out=rsc[:, g % 2:g % 2 + 1]
                                         ).then_inc(sEXP.h, 1)
                            sEXP.n += 1
                            s.wait_ge(sNRM.h, b[id(sNRM)] + g + 1)
                            if h >= 2:
                                gp = (h - 2) * NT + t
                                s.wait_ge(sDRDb[gp % 2].h,
                                          b[id(sDRDb[gp % 2])] + 16 * (gp // 2 + 1))
                            Av = scoA[g % 2][:, :].bitcast(BF16)[:, 0:T]
                            s.dma_start(out=bass.AP(esc[h % 2],
                                                    t * 128 * WA + 256,
                                                    [[WA, 128], [1, 2048]]),
                                        in_=Av).then_inc(sASTb[g % 2].h, 16)
                            sASTb[g % 2].n += 16

                    @blk.tensor
                    def _(tn):
                        for g in range(NG + 1):
                            if g < NG:
                                h, t = divmod(g, NT)
                                off = 64 * (h % 2)
                                hp2 = (h // 2) % 2
                                tbl = tblA if t < 8 else tblB
                                lhs = Qp[hp2][off:off + 64,
                                              t * 128:(t + 1) * 128]
                                hp_ = h // 2
                                tn.wait_ge(sQPLb[hp_ % 2].h,
                                           b[id(sQPLb[hp_ % 2])]
                                           + 48 * (hp_ // 2 + 1))
                                for p, (o, w) in enumerate(PIECES):
                                    consumed_wait(tn, 8 * g + p)
                                    tn.matmul(PS[p % 2][:, 0:w], lhs,
                                              tbl[off:off + 64, o:o + w],
                                              start=True, stop=True
                                              ).then_inc(sSR.h, 1)
                                    sSR.n += 1
                                for p in range(4):
                                    consumed_wait(tn, 8 * g + 4 + p)
                                    tn.matmul(PS[p % 2][:, :], lhs,
                                              Kp[hp2][off:off + 64,
                                                      p * 512:(p + 1) * 512],
                                              start=True, stop=True
                                              ).then_inc(sQK.h, 1)
                                    sQK.n += 1
                            if g >= 1:
                                g1 = g - 1
                                h1, t1 = divmod(g1, NT)
                                off1 = 64 * (h1 % 2)
                                hp21 = (h1 // 2) % 2
                                Av1 = scoA[g1 % 2][:, :].bitcast(BF16)[:, 0:T]
                                tn.wait_ge(sNRM.h, b[id(sNRM)] + g1 + 1)
                                for grp in range(2):
                                    gg = 5 * g1 + grp
                                    if gg >= 4:
                                        tn.wait_ge(sGEV.h, b[id(sGEV)] + gg - 3)
                                    for bb in range(8):
                                        blkc = grp * 8 + bb
                                        ti = tn.transpose(
                                            psT[gg % 4][:, bb * 128:(bb + 1) * 128],
                                            Av1[:, blkc * 128:(blkc + 1) * 128],
                                            ident[:, :])
                                    ti.then_inc(sTRG.h, 1)
                                    sTRG.n += 1
                                tn.wait_ge(sDRDb[g1 % 2].h,
                                           b[id(sDRDb[g1 % 2])]
                                           + 16 * (g1 // 2 + 1))
                                for grp in range(3):
                                    gg = 5 * g1 + 2 + grp
                                    nb = 8 if grp < 2 else 1
                                    if gg >= 4:
                                        tn.wait_ge(sGEV.h, b[id(sGEV)] + gg - 3)
                                    for bb in range(nb):
                                        blkc = grp * 8 + bb
                                        ti = tn.transpose(
                                            psT[gg % 4][:, bb * 128:(bb + 1) * 128],
                                            Adg[g1 % 2][:, blkc * 128:(blkc + 1) * 128],
                                            ident[:, :])
                                    ti.then_inc(sTRG.h, 1)
                                    sTRG.n += 1
                                tn.wait_ge(sGEV.h, b[id(sGEV)] + 5 * g1 + 2)
                                if g1 >= 2:
                                    tn.wait_ge(sOEV.h, b[id(sOEV)] + g1 - 1)
                                for rc in range(16):
                                    tn.matmul(psO[g1 % 2][off1:off1 + 64, :],
                                              Vp[hp21][:, rc * 128 + off1:rc * 128 + off1 + 64],
                                              ATs[:, rc * 128:(rc + 1) * 128],
                                              start=(rc == 0), stop=False)
                                tn.wait_ge(sGEV.h, b[id(sGEV)] + 5 * g1 + 5)
                                for cc in range(17):
                                    gc = 15 - t1 + cc
                                    mi = tn.matmul(
                                        psO[g1 % 2][off1:off1 + 64, :],
                                        RVsb[:, gc * 64:(gc + 1) * 64],
                                        DTs[:, cc * 128:(cc + 1) * 128],
                                        start=False, stop=(cc == 16))
                                mi.then_inc(sAV.h, 1)
                                sAV.n += 1

                    @blk.vector
                    def _(v):
                        for g in range(NG + 1):
                            if g < NG:
                                for p, (o, w) in enumerate(PIECES):
                                    v.wait_ge(sSR.h, b[id(sSR)] + 4 * g + p + 1)
                                    if p == 0 and g >= 2:
                                        v.wait_ge(sSSTb[g % 2].h,
                                                  b[id(sSSTb[g % 2])]
                                                  + 16 * (g // 2))
                                    v.tensor_copy(out=Ssb[g % 2][:, o:o + w],
                                                  in_=PS[p % 2][:, 0:w]
                                                  ).then_inc(sSRE.h, 1)
                                    sSRE.n += 1
                                for p in range(4):
                                    v.wait_ge(sQK.h, b[id(sQK)] + 4 * g + p + 1)
                                    if p == 0:
                                        v.wait_ge(sBRDb[g % 2].h,
                                                  b[id(sBRDb[g % 2])]
                                                  + 16 * (g // 2 + 1))
                                        if g >= 2:
                                            v.wait_ge(sASTb[g % 2].h,
                                                      b[id(sASTb[g % 2])]
                                                      + 16 * (g // 2))
                                            v.wait_ge(sTRG.h,
                                                      b[id(sTRG)] + 5 * (g - 2) + 2)
                                    v.tensor_tensor(
                                        out=scoA[g % 2][:, p * 512:(p + 1) * 512],
                                        in0=PS[p % 2][:, :],
                                        in1=bndF[g % 2][:, p * 512:(p + 1) * 512],
                                        op=ALU.add).then_inc(sBAD.h, 1)
                                    sBAD.n += 1
                                v.wait_ge(sEXP.h, b[id(sEXP)] + g + 1)
                                v.reciprocal(out=rsc[:, 2 + g % 2:3 + g % 2],
                                             in_=rsc[:, g % 2:g % 2 + 1])
                            if g >= 1:
                                g1 = g - 1
                                for grp in range(2):
                                    gg = 5 * g1 + grp
                                    v.wait_ge(sTRG.h, b[id(sTRG)] + gg + 1)
                                    v.tensor_copy(
                                        out=ATs[:, grp * 1024:(grp + 1) * 1024],
                                        in_=psT[gg % 4][:, :]).then_inc(sGEV.h, 1)
                                    sGEV.n += 1
                            if g < NG:
                                if g == 0:
                                    v.tensor_copy(out=rsc[:, 3:4],
                                                  in_=czs[:, 0:1])
                                v.tensor_scalar(
                                    out=scoA[g % 2][:, :].bitcast(BF16)[:, 0:T],
                                    in0=bndE[g % 2][:, :],
                                    scalar1=rsc[:, 2 + g % 2:3 + g % 2],
                                    scalar2=None, op0=ALU.mult
                                    ).then_inc(sNRM.h, 1)
                                sNRM.n += 1
                            if g >= 1:
                                g1 = g - 1
                                h1, t1 = divmod(g1, NT)
                                off1 = 64 * (h1 % 2)
                                for grp in range(3):
                                    gg = 5 * g1 + 2 + grp
                                    nb = 8 if grp < 2 else 1
                                    v.wait_ge(sTRG.h, b[id(sTRG)] + gg + 1)
                                    v.tensor_copy(
                                        out=DTs[:, grp * 1024:grp * 1024 + nb * 128],
                                        in_=psT[gg % 4][:, 0:nb * 128]
                                        ).then_inc(sGEV.h, 1)
                                    sGEV.n += 1
                                v.wait_ge(sAV.h, b[id(sAV)] + g1 + 1)
                                v.tensor_copy(
                                    out=HO[h1 // 2][off1:off1 + 64,
                                                    t1 * 128:(t1 + 1) * 128],
                                    in_=psO[g1 % 2][off1:off1 + 64, :]
                                    ).then_inc(sOEV.h, 1)
                                sOEV.n += 1

                    @blk.gpsimd
                    def _(g_):
                        g_.wait_ge(sOEV.h, b[id(sOEV)] + NG)

            with ExitStack() as scw:
                e3 = scw.enter_context
                wbf2 = [e3(nc.sbuf_tensor(f"wo{i}", [128, D], BF16))
                        for i in range(NDC)]
                wtm2 = [e3(nc.sbuf_tensor(f"wotm{i}", [128, D], F32))
                        for i in range(2)]
                psW = [e3(nc.psum_tensor(f"psw{i}", [128, 512], F32))
                       for i in range(2)]
                sWL, sWC, sWM, sWA = SS[47:51]
                sWLb = [sWL, SS[63]]
                b = snapd(sWL, sWC, sWM, sWA, SS[63])
                with nc.Block() as blk:
                    @blk.sync
                    def _(s):
                        for c in range(NDC):
                            if c >= 2:
                                s.wait_ge(sWC.h, b[id(sWC)] + c - 1)
                            s.dma_start(out=wtm2[c % 2][:, :],
                                        in_=woT[c * 128:(c + 1) * 128, :]
                                        ).then_inc(sWLb[c % 2].h, 16)
                            sWLb[c % 2].n += 16

                    @blk.vector
                    def _(v):
                        for c in range(NDC):
                            v.wait_ge(sWLb[c % 2].h,
                                      b[id(sWLb[c % 2])] + 16 * (c // 2 + 1))
                            v.tensor_copy(out=wbf2[c][:, :],
                                          in_=wtm2[c % 2][:, :]).then_inc(sWC.h, 1)
                            sWC.n += 1
                        for j in range(32):
                            oc, p = divmod(j, 4)
                            v.wait_ge(sWM.h, b[id(sWM)] + j + 1)
                            xsl = X[oc][:, p * 512:(p + 1) * 512]
                            v.scalar_tensor_tensor(out=xsl, in0=psW[j % 2][:, :],
                                                   scalar=bos[:, oc:oc + 1],
                                                   in1=xsl, op0=ALU.add,
                                                   op1=ALU.add).then_inc(sWA.h, 1)
                            sWA.n += 1

                    @blk.tensor
                    def _(tn):
                        tn.wait_ge(sWC.h, b[id(sWC)] + NDC)
                        for j in range(32):
                            oc, p = divmod(j, 4)
                            if j >= 2:
                                tn.wait_ge(sWA.h, b[id(sWA)] + j - 1)
                            for cc in range(NDC):
                                mi = tn.matmul(
                                    psW[j % 2][:, :],
                                    wbf2[cc][:, oc * 128:(oc + 1) * 128],
                                    HO[cc][:, p * 512:(p + 1) * 512],
                                    start=(cc == 0), stop=(cc == NDC - 1))
                            mi.then_inc(sWM.h, 1)
                            sWM.n += 1

                    @blk.gpsimd
                    def _(g):
                        g.wait_ge(sWA.h, b[id(sWA)] + 32)
                        if dbg:
                            sst = SS[51]
                            for i in range(NDC):
                                g.dma_start(
                                    out=dbg_t["dbg_attn"][i * 128:(i + 1) * 128, :],
                                    in_=X[i][:, :]).then_inc(sst.h, 16)
                                sst.n += 16
                            for i in range(NDC):
                                g.dma_start(
                                    out=dbg_t["dbg_ho"][i * 128:(i + 1) * 128, :]
                                    .bitcast(BF16)[:, 0:T],
                                    in_=HO[i][:, :]).then_inc(sst.h, 16)
                                sst.n += 16
                            g.wait_ge(sst.h, sst.n)

        with ExitStack() as sc:
            e2 = sc.enter_context
            Xln = [e2(nc.sbuf_tensor(f"xln3{i}", [128, T], BF16))
                   for i in range(NDC)]
            psM = [e2(nc.psum_tensor(f"psm3{i}", [128, 512], F32))
                   for i in range(4)]
            psS = [e2(nc.psum_tensor(f"pss3{i}", [128, 512], F32))
                   for i in range(4)]
            with ExitStack() as scl:
                e3 = scl.enter_context
                rowsA = e3(nc.sbuf_tensor("rowsA3", [1, T], F32))
                rowsB = e3(nc.sbuf_tensor("rowsB3", [1, T], F32))
                bcA = e3(nc.sbuf_tensor("bcA3", [128, T], F32))
                bcB = e3(nc.sbuf_tensor("bcB3", [128, T], F32))
                xsq = [e3(nc.sbuf_tensor(f"xsq3{i}", [128, T], BF16))
                       for i in range(2)]
                vrow = e3(nc.sbuf_tensor("vrow3", [1, T], F32))
                tmpf = e3(nc.sbuf_tensor("tmpf3", [128, T], F32))
                with nc.Block() as blk:
                    ln_phase(blk, Xln, psM, psS, rowsA, rowsB, vrow, bcA, bcB,
                             xsq, tmpf)
            with ExitStack() as scf:
                e3 = scf.enter_context
                H = [e3(nc.sbuf_tensor(f"hb_{i}", [128, 1024], BF16))
                     for i in range(NFC)]
                w1s = [e3(nc.sbuf_tensor(f"w1sb_{i}", [128, 8, 128], BF16))
                       for i in range(2)]
                w2s = [e3(nc.sbuf_tensor(f"w2sb_{i}", [128, 32, 128], BF16))
                       for i in range(2)]
                with nc.Block() as blk:
                    ffn_phase(blk, Xln, H, w1s, w2s, psM, lambda g: None)

        with nc.Block() as blk:
            @blk.gpsimd
            def _(g):
                sst = SS[51]
                for i in range(NDC):
                    g.dma_start(out=yT[i * 128:(i + 1) * 128, :],
                                in_=X[i][:, :]).then_inc(sst.h, 16)
                    sst.n += 16
                g.wait_ge(sst.h, sst.n)

    return nc


def _prep_inputs(inputs):
    f = np.float32
    x = np.asarray(inputs["L0_attn_x"], f)

    def chunks(bv, n):
        return np.ascontiguousarray(np.asarray(bv, f).reshape(n, 128).T)

    relv = np.asarray(inputs["L0_attn_relative_v"], f)
    common = {
        "wqT": np.ascontiguousarray(np.asarray(inputs["L0_attn_Wq"], f).T),
        "wkT": np.ascontiguousarray(np.asarray(inputs["L0_attn_Wk"], f).T),
        "wvT": np.ascontiguousarray(np.asarray(inputs["L0_attn_Wv"], f).T),
        "woT": np.ascontiguousarray(np.asarray(inputs["L0_attn_Wo"], f).T),
        "w1T": np.ascontiguousarray(np.asarray(inputs["L0_ffn_W1"], f).T),
        "w2T": np.ascontiguousarray(np.asarray(inputs["L0_ffn_W2"], f).T),
        "bq_c": chunks(inputs["L0_attn_bq"], NDC),
        "bk_c": chunks(inputs["L0_attn_bk"], NDC),
        "bo_c": chunks(inputs["L0_attn_bo"], NDC),
        "b1_c": chunks(inputs["L0_ffn_b1"], NFC),
        "bv_r": np.asarray(inputs["L0_attn_bv"], f).reshape(1, D),
        "b2_r": np.asarray(inputs["L0_ffn_b2"], f).reshape(1, D),
        "relkT": np.ascontiguousarray(
            np.asarray(inputs["L0_attn_relative_k"], f).T),
        "rvpad": np.ascontiguousarray(
            relv[np.clip(np.arange(4096) - 1024, 0, 2046)]),
    }
    in_maps = []
    for c in range(8):
        m = dict(common)
        m["xT"] = np.ascontiguousarray(x[c].T)
        in_maps.append(m)
    return in_maps


def kernel(**inputs):
    in_maps = _prep_inputs(inputs)
    nc = build_nc(dbg=False)
    res = run_bass_kernel_spmd(nc, in_maps, core_ids=list(range(8)))
    out = np.stack([np.ascontiguousarray(r["yT"].T) for r in res.results])
    return out.astype(np.float32)


# version 11
# speedup vs baseline: 1.5529x; 1.0039x over previous
import numpy as np
from contextlib import ExitStack

import concourse.bass as bass
import concourse.mybir as mybir
from concourse.bass_utils import run_bass_kernel_spmd

F32 = mybir.dt.float32
BF16 = mybir.dt.bfloat16
FP16 = mybir.dt.float16
AF = mybir.ActivationFunctionType
ALU = mybir.AluOpType
AX = mybir.AxisListType

T = 2048
D = 1024
NH = 16
DK = 64
DF = 4096
NDC = D // 128
NFC = DF // 128
NT = T // 128
CSH = 32.0
EPS = 1e-5
WS = 4095
WA = 2560
NG = NH * NT
PIECES = [(0, 512), (512, 512), (1024, 512), (1536, 511)]


class Ctr:
    def __init__(self, h):
        self.h = h
        self.n = 0


def build_nc(dbg=False):
    nc = bass.Bass("TRN2", target_bir_lowering=False, detect_race_conditions=False)

    xT = nc.dram_tensor("xT", [D, T], F32, kind="ExternalInput").ap()
    wqT = nc.dram_tensor("wqT", [D, D], F32, kind="ExternalInput").ap()
    wkT = nc.dram_tensor("wkT", [D, D], F32, kind="ExternalInput").ap()
    wvT = nc.dram_tensor("wvT", [D, D], F32, kind="ExternalInput").ap()
    woT = nc.dram_tensor("woT", [D, D], F32, kind="ExternalInput").ap()
    w1T = nc.dram_tensor("w1T", [D, DF], F32, kind="ExternalInput").ap()
    w2T = nc.dram_tensor("w2T", [DF, D], F32, kind="ExternalInput").ap()
    bq_c = nc.dram_tensor("bq_c", [128, NDC], F32, kind="ExternalInput").ap()
    bk_c = nc.dram_tensor("bk_c", [128, NDC], F32, kind="ExternalInput").ap()
    bo_c = nc.dram_tensor("bo_c", [128, NDC], F32, kind="ExternalInput").ap()
    b1_c = nc.dram_tensor("b1_c", [128, NFC], F32, kind="ExternalInput").ap()
    bv_r = nc.dram_tensor("bv_r", [1, D], F32, kind="ExternalInput").ap()
    b2_r = nc.dram_tensor("b2_r", [1, D], F32, kind="ExternalInput").ap()
    relkT = nc.dram_tensor("relkT", [64, 2047], F32, kind="ExternalInput").ap()
    rvpad = nc.dram_tensor("rvpad", [4096, 64], F32, kind="ExternalInput").ap()
    yT = nc.dram_tensor("yT", [D, T], F32, kind="ExternalOutput").ap()
    dbg_t = {}
    if dbg:
        for name in ["dbg_ffn1", "dbg_ln2", "dbg_attn", "dbg_ho",
                     "dbg_x0", "dbg_xln1", "dbg_w1q"]:
            dbg_t[name] = nc.dram_tensor(name, [D, T], F32,
                                         kind="ExternalOutput").ap()
    qd = nc.dram_tensor("qd", [D, T], BF16, kind="Internal")
    kd = nc.dram_tensor("kd", [D, T], BF16, kind="Internal")
    vd = nc.dram_tensor("vd", [T, D], BF16, kind="Internal")
    w1q = nc.dram_tensor("w1q", [D, DF], BF16, kind="Internal")
    w2q = nc.dram_tensor("w2q", [DF, D], BF16, kind="Internal")
    spr = [nc.dram_tensor(f"spr{i}", [T * WS], FP16, kind="Internal")
           for i in range(2)]
    esc = [nc.dram_tensor(f"esc{i}", [T * WA], BF16, kind="Internal")
           for i in range(2)]
    rowscr = nc.dram_tensor("rowscr", [2 * T], F32, kind="Internal")
    dbg_rows = None
    if dbg:
        dbg_rows = nc.dram_tensor("dbg_rows", [2, T], F32,
                                  kind="ExternalOutput").ap()

    qdA = qd.ap()
    kdA = kd.ap()
    vdA = vd.ap()
    w1qA = w1q.ap()
    w2qA = w2q.ap()

    with ExitStack() as top:
        en = top.enter_context
        X = [en(nc.sbuf_tensor(f"x{i}", [128, T], F32)) for i in range(NDC)]
        tblA = en(nc.sbuf_tensor("tblA", [128, 2047], BF16))
        tblB = en(nc.sbuf_tensor("tblB", [128, 2047], BF16))
        RVsb = en(nc.sbuf_tensor("rvsb", [128, 32 * 64], BF16))
        ident = en(nc.sbuf_tensor("idnt", [128, 128], BF16))
        onec = en(nc.sbuf_tensor("onec", [128, 2], BF16))
        oner = en(nc.sbuf_tensor("oner", [1, 512], BF16))
        onerf = en(nc.sbuf_tensor("onerf", [1, 128], F32))
        bqs = en(nc.sbuf_tensor("bqs", [128, NDC], F32))
        bks = en(nc.sbuf_tensor("bks", [128, NDC], F32))
        bos = en(nc.sbuf_tensor("bos", [128, NDC], F32))
        b1s = en(nc.sbuf_tensor("b1s", [128, NFC], F32))
        bvr = en(nc.sbuf_tensor("bvr", [1, D], BF16))
        hb2r = en(nc.sbuf_tensor("hb2r", [1, D], BF16))
        thrs = en(nc.sbuf_tensor("thrs", [128, 4], F32))
        czs = en(nc.sbuf_tensor("czs", [128, 4], F32))
        SS = [Ctr(en(nc.semaphore(f"s{i}"))) for i in range(64)]

        def snap():
            return [s.n for s in SS]

        def snapd(*sems):
            return {id(s): s.n for s in sems}

        with ExitStack() as sc:
            e2 = sc.enter_context
            ztile = e2(nc.sbuf_tensor("ztile", [128, 1024], BF16))
            rtmp = e2(nc.sbuf_tensor("rtmp0", [128, 2048], F32))
            itmp = e2(nc.sbuf_tensor("itmp0", [128, 256], F32))
            rwb1 = e2(nc.sbuf_tensor("rwb1", [1, D], F32))
            rwb2 = e2(nc.sbuf_tensor("rwb2", [1, D], F32))
            s0, s1, s2, s3, s4 = SS[0], SS[1], SS[2], SS[3], SS[4]
            b = snap()
            with nc.Block() as blk:
                @blk.gpsimd
                def _(g):
                    for i in range(NDC):
                        g.dma_start(out=X[i][:, :],
                                    in_=xT[i * 128:(i + 1) * 128, :]
                                    ).then_inc(s0.h, 16)
                        s0.n += 16
                    for dst, srcd in [(bqs, bq_c), (bks, bk_c), (bos, bo_c),
                                      (b1s, b1_c)]:
                        g.dma_start(out=dst[:, :], in_=srcd).then_inc(s0.h, 16)
                        s0.n += 16
                    g.dma_start(out=rwb1[0:1, :], in_=bv_r).then_inc(s0.h, 16)
                    s0.n += 16
                    g.dma_start(out=rwb2[0:1, :], in_=b2_r).then_inc(s0.h, 16)
                    s0.n += 16
                    g.iota(itmp[:, 0:128], [[1, 128]], channel_multiplier=0,
                           allow_small_or_imprecise_dtypes=True)
                    g.iota(itmp[:, 128:256], [[0, 128]], channel_multiplier=1,
                           allow_small_or_imprecise_dtypes=True)
                    g.memset(ztile[:, :], 0)
                    g.memset(onec[:, :], 1.0)
                    g.memset(oner[:, :], 1.0)
                    g.memset(czs[:, 0:1], 0.0)
                    g.memset(czs[:, 1:2], -CSH)
                    g.memset(czs[:, 2:3], EPS)
                    g.memset(onerf[:, :], 1.0).then_inc(s1.h, 1)
                    s1.n += 1
                    g.wait_ge(s0.h, s0.n)

                @blk.sync
                def _(s):
                    s.dma_start(out=rtmp[0:64, 0:2047],
                                in_=relkT).then_inc(s2.h, 16)
                    s2.n += 16
                    s.dma_start(out=rtmp[64:128, 0:2047],
                                in_=relkT).then_inc(s2.h, 16)
                    s2.n += 16

                @blk.vector
                def _(v):
                    v.wait_ge(s1.h, b[1] + 1)
                    v.tensor_tensor(out=ident[:, :], in0=itmp[:, 0:128],
                                    in1=itmp[:, 128:256], op=ALU.is_equal)
                    v.wait_ge(s0.h, b[0] + 16 * 14)
                    v.tensor_scalar(out=bks[:, :], in0=bks[:, :], scalar1=0.125,
                                    scalar2=None, op0=ALU.mult)
                    v.tensor_copy(out=bvr[:, :], in_=rwb1[0:1, :])
                    v.tensor_scalar(out=hb2r[:, :], in0=rwb2[0:1, :],
                                    scalar1=0.5, scalar2=None, op0=ALU.mult)
                    v.wait_ge(s2.h, b[2] + 32)
                    v.scalar_tensor_tensor(
                        out=tblA[:, :], in0=rtmp[:, 0:2047], scalar=1.0,
                        in1=rtmp[:, 2046:2047].to_broadcast([128, 2047]),
                        op0=ALU.mult, op1=ALU.subtract)
                    v.scalar_tensor_tensor(
                        out=tblB[:, :], in0=rtmp[:, 0:2047], scalar=1.0,
                        in1=rtmp[:, 0:1].to_broadcast([128, 2047]),
                        op0=ALU.mult, op1=ALU.subtract).then_inc(s3.h, 1)
                    s3.n += 1
                    v.wait_ge(s2.h, b[2] + 48)
                    v.tensor_copy(out=RVsb[:, :], in_=rtmp[:, 0:2048]).then_inc(
                        s3.h, 1)
                    s3.n += 1

                @blk.sync
                def _(s):
                    s.wait_ge(s3.h, b[3] + 1)
                    s.dma_start(
                        out=rtmp[:, 0:2048].rearrange("p (c k) -> p c k", k=64),
                        in_=bass.AP(rvpad.tensor, 0, [[64, 128], [128 * 64, 32], [1, 64]])
                    ).then_inc(s2.h, 16)
                    s2.n += 16

                @blk.scalar
                def _(s):
                    pass
                    s.wait_ge(s1.h, b[1] + 1)
                    for bi in range(2):
                        for rt in range(NT):
                            for off_, wid_, tsr in [
                                    (rt * 128 * WS, 1024, spr[bi]),
                                    (rt * 128 * WS + 3071, 1024, spr[bi]),
                                    (rt * 128 * WA, 256, esc[bi]),
                                    (rt * 128 * WA + 2304, 256, esc[bi])]:
                                zsrc = (ztile[:, 0:wid_].bitcast(FP16)[:, 0:wid_]
                                        if wid_ == 1024 else ztile[:, 0:wid_])
                                s.dma_start(out=bass.AP(tsr, off_,
                                                        [[WS if wid_ == 1024 else WA, 128],
                                                         [1, wid_]]),
                                            in_=zsrc).then_inc(s4.h, 16)
                                s4.n += 16
                    s.wait_ge(s4.h, s4.n)
                    s.wait_ge(s3.h, b[3] + 2)

        with ExitStack() as sc:
            e2 = sc.enter_context
            wtmp = [e2(nc.sbuf_tensor(f"wt{i}", [128, DF], F32)) for i in range(2)]
            wabs = e2(nc.sbuf_tensor("wabs", [128, DF], FP16))
            wqo = [e2(nc.sbuf_tensor(f"wqo{i}", [128, DF], BF16)) for i in range(2)]
            qrows = e2(nc.sbuf_tensor("qrows", [1, 600], F32))
            psQ = e2(nc.psum_tensor("psq", [1, 512], F32))
            psB = e2(nc.psum_tensor("psb", [128, 2], F32))

            def quant_pass(blk, which):
                src, dstA = (w1T, w1qA) if which == 0 else (w2T, w2qA)
                nch = NDC if which == 0 else NFC
                wid = DF if which == 0 else D
                npc = wid // 512
                ntot = float(D * DF)
                thr_col = thrs[:, 2 * which:2 * which + 1]
                nthr_col = thrs[:, 2 * which + 1:2 * which + 2]
                sLD, sAB, sMM, sRW, sBC, sTH, sQO, sSO = SS[5:13]
                sLDb = [sLD, SS[52]]
                sSOb = [sSO, SS[53]]
                b = snapd(sLD, sAB, sMM, sRW, sBC, sTH, sQO, sSO,
                          SS[52], SS[53])

                @blk.sync
                def _(s):
                    for c in range(nch):
                        if c >= 2:
                            s.wait_ge(sAB.h, b[id(sAB)] + c - 1)
                        sl = sLDb[c % 2]
                        s.dma_start(out=wtmp[c % 2][:, 0:wid],
                                    in_=src[c * 128:(c + 1) * 128, :]
                                    ).then_inc(sl.h, 16)
                        sl.n += 16
                    for c in range(nch):
                        s.wait_ge(sTH.h, b[id(sTH)] + 1)
                        if c >= 2:
                            s.wait_ge(sQO.h, b[id(sQO)] + c - 1)
                        sl = sLDb[c % 2]
                        s.dma_start(out=wtmp[c % 2][:, 0:wid],
                                    in_=src[c * 128:(c + 1) * 128, :]
                                    ).then_inc(sl.h, 16)
                        sl.n += 16

                @blk.scalar
                def _(s):
                    for c in range(nch):
                        s.wait_ge(sLDb[c % 2].h,
                                  b[id(sLDb[c % 2])] + 16 * (c // 2 + 1))
                        if c >= 1:
                            s.wait_ge(sMM.h, b[id(sMM)] + c * npc)
                        s.activation(out=wabs[:, 0:wid], in_=wtmp[c % 2][:, 0:wid],
                                     func=AF.Abs, bias=czs[:, 0:1], scale=1.0
                                     ).then_inc(sAB.h, 1)
                        sAB.n += 1

                @blk.tensor
                def _(tn):
                    for c in range(nch):
                        tn.wait_ge(sAB.h, b[id(sAB)] + c + 1)
                        for p in range(npc):
                            tn.matmul(psQ[0:1, :],
                                      onec[:, 0:1], wabs[:, p * 512:(p + 1) * 512],
                                      start=(c == 0 and p == 0),
                                      stop=(c == nch - 1 and p == npc - 1)
                                      ).then_inc(sMM.h, 1)
                            sMM.n += 1
                    tn.wait_ge(sRW.h, b[id(sRW)] + 1)
                    tn.matmul(psB[:, 0:1], onerf[0:1, :], qrows[0:1, 577:578],
                              start=True, stop=True)
                    tn.matmul(psB[:, 1:2], onerf[0:1, :], qrows[0:1, 578:579],
                              start=True, stop=True).then_inc(sBC.h, 1)
                    sBC.n += 1

                @blk.vector
                def _(v):
                    def gap():
                        v.tensor_copy(out=czs[:, 3:4], in_=czs[:, 0:1])
                        v.tensor_copy(out=czs[:, 3:4], in_=czs[:, 0:1])
                    v.wait_ge(sMM.h, b[id(sMM)] + nch * npc)
                    v.reduce_sum(out=qrows[0:1, 0:64],
                                 in_=psQ[0:1, :].rearrange(
                                     "p (a c) -> p a c", c=8), axis=AX.X)
                    gap()
                    v.reduce_sum(out=qrows[0:1, 512:520],
                                 in_=qrows[0:1, 0:64].rearrange(
                                     "p (a c) -> p a c", c=8), axis=AX.X)
                    gap()
                    v.reduce_sum(out=qrows[0:1, 576:577],
                                 in_=qrows[0:1, 512:520], axis=AX.X)
                    gap()
                    v.tensor_scalar(out=qrows[0:1, 577:578],
                                    in0=qrows[0:1, 576:577],
                                    scalar1=0.5 / ntot, scalar2=0.5 * EPS,
                                    op0=ALU.mult, op1=ALU.add)
                    v.tensor_scalar(out=qrows[0:1, 578:579],
                                    in0=qrows[0:1, 576:577],
                                    scalar1=-0.5 / ntot, scalar2=-0.5 * EPS,
                                    op0=ALU.mult, op1=ALU.add).then_inc(sRW.h, 1)
                    sRW.n += 1
                    v.wait_ge(sBC.h, b[id(sBC)] + 1)
                    v.tensor_copy(out=thr_col, in_=psB[:, 0:1])
                    v.tensor_copy(out=nthr_col, in_=psB[:, 1:2]).then_inc(sTH.h, 1)
                    sTH.n += 1
                    sca = 0.5 if which == 1 else 1.0
                    for c in range(nch):
                        v.wait_ge(sLDb[c % 2].h,
                                  b[id(sLDb[c % 2])]
                                  + 16 * (nch // 2 + c // 2 + 1))
                        if c >= 2:
                            v.wait_ge(sSOb[c % 2].h,
                                      b[id(sSOb[c % 2])] + 16 * (c // 2))
                        v.tensor_scalar(out=wabs[:, 0:wid],
                                        in0=wtmp[c % 2][:, 0:wid],
                                        scalar1=thr_col, scalar2=sca,
                                        op0=ALU.is_gt, op1=ALU.mult)
                        v.tensor_scalar(out=wqo[c % 2][:, 0:wid],
                                        in0=wtmp[c % 2][:, 0:wid],
                                        scalar1=nthr_col, scalar2=sca,
                                        op0=ALU.is_lt, op1=ALU.mult)
                        v.tensor_tensor(out=wqo[c % 2][:, 0:wid],
                                        in0=wabs[:, 0:wid],
                                        in1=wqo[c % 2][:, 0:wid],
                                        op=ALU.subtract).then_inc(sQO.h, 1)
                        sQO.n += 1

                @blk.gpsimd
                def _(g):
                    for c in range(nch):
                        g.wait_ge(sQO.h, b[id(sQO)] + c + 1)
                        so_ = sSOb[c % 2]
                        g.dma_start(out=dstA[c * 128:(c + 1) * 128, :],
                                    in_=wqo[c % 2][:, 0:wid]).then_inc(so_.h, 16)
                        so_.n += 16
                    g.wait_ge(sSOb[0].h, sSOb[0].n)
                    g.wait_ge(sSOb[1].h, sSOb[1].n)

            with nc.Block() as blk:
                quant_pass(blk, 0)
            with nc.Block() as blk:
                quant_pass(blk, 1)
            if dbg:
                with nc.Block() as blk:
                    @blk.gpsimd
                    def _(g):
                        sst = SS[51]
                        for i in range(NDC):
                            g.dma_start(out=dbg_t["dbg_x0"][i * 128:(i + 1) * 128, :],
                                        in_=X[i][:, :]).then_inc(sst.h, 16)
                            sst.n += 16
                        g.dma_start(out=dbg_t["dbg_w1q"][0:128, :].bitcast(BF16)[:, 0:T],
                                    in_=w1qA[0:128, 0:T]).then_inc(sst.h, 16)
                        sst.n += 16
                        g.wait_ge(sst.h, sst.n)

        def ln_phase(blk, Xln, psM, psS, rowsA, rowsB, vrow, bcA, bcB,
                     xsq, tmpf):
            sCS, sMM, sVAR, sRSD, sMR, sRST, sBCL, sAPP = SS[13:21]
            b = snapd(sCS, sMM, sVAR, sRSD, sMR, sRST, sBCL, sAPP)

            @blk.scalar
            def _(s):
                for dc in range(NDC):
                    if dc >= 2:
                        s.wait_ge(sMM.h, b[id(sMM)] + 4 * (dc - 1))
                    s.copy(out=Xln[dc][:, :], in_=X[dc][:, :])
                    s.activation(out=xsq[dc % 2][:, :], in_=X[dc][:, :],
                                 func=AF.Square, bias=czs[:, 0:1], scale=1.0
                                 ).then_inc(sCS.h, 1)
                    sCS.n += 1
                s.wait_ge(sVAR.h, b[id(sVAR)] + 1)
                s.activation(out=rowsB[0:1, :], in_=vrow[0:1, :], func=AF.Ln,
                             bias=czs[0:1, 2:3], scale=1.0)
                s.activation(out=rowsB[0:1, :], in_=rowsB[0:1, :], func=AF.Exp,
                             bias=czs[0:1, 0:1], scale=-0.5).then_inc(sRSD.h, 1)
                sRSD.n += 1
                s.dma_start(out=bass.AP(rowscr, 0, [[1, 1], [1, T]]),
                            in_=rowsB[0:1, :]).then_inc(sRST.h, 16)
                sRST.n += 16
                s.dma_start(out=bass.AP(rowscr, T, [[1, 1], [1, T]]),
                            in_=vrow[0:1, :]).then_inc(sRST.h, 16)
                sRST.n += 16
                s.wait_ge(sRST.h, sRST.n)

            @blk.tensor
            def _(tn):
                for dc in range(NDC):
                    tn.wait_ge(sCS.h, b[id(sCS)] + dc + 1)
                    for p in range(4):
                        tn.matmul(psM[p][0:1, :], onec[:, 0:1],
                                  Xln[dc][:, p * 512:(p + 1) * 512],
                                  start=(dc == 0), stop=(dc == NDC - 1))
                        tn.matmul(psS[p][0:1, :], onec[:, 0:1],
                                  xsq[dc % 2][:, p * 512:(p + 1) * 512],
                                  start=(dc == 0), stop=(dc == NDC - 1)
                                  ).then_inc(sMM.h, 1)
                        sMM.n += 1
                tn.wait_ge(sMR.h, b[id(sMR)] + 1)
                for p in range(4):
                    tn.matmul(psM[p][:, :], onerf[0:1, :],
                              rowsB[0:1, p * 512:(p + 1) * 512],
                              start=True, stop=True)
                    tn.matmul(psS[p][:, :], onerf[0:1, :],
                              rowsA[0:1, p * 512:(p + 1) * 512],
                              start=True, stop=True).then_inc(sBCL.h, 1)
                    sBCL.n += 1

            @blk.vector
            def _(v):
                def gap():
                    v.tensor_copy(out=czs[:, 3:4], in_=czs[:, 0:1])
                    v.tensor_copy(out=czs[:, 3:4], in_=czs[:, 0:1])
                v.wait_ge(sMM.h, b[id(sMM)] + 4 * NDC)
                for p in range(4):
                    v.tensor_scalar(out=rowsA[0:1, p * 512:(p + 1) * 512],
                                    in0=psM[p][0:1, :], scalar1=1.0 / D,
                                    scalar2=None, op0=ALU.mult)
                gap()
                v.tensor_tensor(out=rowsB[0:1, :], in0=rowsA[0:1, :],
                                in1=rowsA[0:1, :], op=ALU.mult)
                gap()
                for p in range(4):
                    inst = v.scalar_tensor_tensor(
                        out=vrow[0:1, p * 512:(p + 1) * 512],
                        in0=psS[p][0:1, :], scalar=1.0 / D,
                        in1=rowsB[0:1, p * 512:(p + 1) * 512],
                        op0=ALU.mult, op1=ALU.subtract)
                gap()
                inst.then_inc(sVAR.h, 1)
                sVAR.n += 1
                v.wait_ge(sRSD.h, b[id(sRSD)] + 1)
                v.tensor_tensor(out=rowsA[0:1, :], in0=rowsA[0:1, :],
                                in1=rowsB[0:1, :], op=ALU.mult)
                gap()
                nc.vector.sem_inc(sMR.h, 1)
                sMR.n += 1
                v.wait_ge(sBCL.h, b[id(sBCL)] + 4)
                for p in range(4):
                    v.tensor_copy(out=bcA[:, p * 512:(p + 1) * 512],
                                  in_=psM[p][:, :])
                    v.tensor_copy(out=bcB[:, p * 512:(p + 1) * 512],
                                  in_=psS[p][:, :])
                for dc in range(NDC):
                    v.tensor_tensor(out=tmpf[:, :], in0=X[dc][:, :],
                                    in1=bcA[:, :], op=ALU.mult)
                    v.tensor_tensor(out=Xln[dc][:, :], in0=tmpf[:, :],
                                    in1=bcB[:, :], op=ALU.subtract
                                    ).then_inc(sAPP.h, 1)
                    sAPP.n += 1

            @blk.gpsimd
            def _(g):
                g.wait_ge(sAPP.h, b[id(sAPP)] + NDC)

        def ffn_phase(blk, Xln, H, w1s, w2s, psF, evt):
            sL1, sL2, sM1, sRL, sM2, sXA = SS[21:27]
            sL1b = [sL1, SS[54]]
            sL2b = [sL2, SS[55]]
            b = snapd(sL1, sL2, sM1, sRL, sM2, sXA, SS[54], SS[55])

            @blk.sync
            def _(s):
                for th in range(2):
                    for fc in range(NFC):
                        i = th * NFC + fc
                        if i >= 2:
                            s.wait_ge(sM1.h, b[id(sM1)] + 2 * (i - 1))
                        s.dma_start(out=w1s[i % 2][:, :, :],
                                    in_=bass.AP(w1q, 128 * fc,
                                                [[DF, 128], [128 * DF, 8],
                                                 [1, 128]])).then_inc(
                                                     sL1b[i % 2].h, 16)
                        sL1b[i % 2].n += 16
                    for dc2 in range(NDC):
                        i = th * NDC + dc2
                        if i >= 2:
                            s.wait_ge(sM2.h, b[id(sM2)] + 2 * (i - 1))
                        s.dma_start(out=w2s[i % 2][:, :, :],
                                    in_=bass.AP(w2q, 128 * dc2,
                                                [[D, 128], [128 * D, 32],
                                                 [1, 128]])).then_inc(
                                                     sL2b[i % 2].h, 16)
                        sL2b[i % 2].n += 16

            @blk.tensor
            def _(tn):
                for th in range(2):
                    for fc in range(NFC):
                        i = th * NFC + fc
                        tn.wait_ge(sL1b[i % 2].h,
                                   b[id(sL1b[i % 2])] + 16 * (i // 2 + 1))
                        for p in range(2):
                            if i >= 1:
                                tn.wait_ge(sRL.h, b[id(sRL)] + 2 * i - 1 + p)
                            for dc in range(NDC):
                                mi = tn.matmul(
                                    psF[p][:, :], w1s[i % 2][:, dc, :],
                                    Xln[dc][:, th * 1024 + p * 512:
                                            th * 1024 + (p + 1) * 512],
                                    start=(dc == 0), stop=(dc == NDC - 1))
                            mi.then_inc(sM1.h, 1)
                            sM1.n += 1
                    for dc2 in range(NDC):
                        i = th * NDC + dc2
                        tn.wait_ge(sL2b[i % 2].h,
                                   b[id(sL2b[i % 2])] + 16 * (i // 2 + 1))
                        tn.wait_ge(sRL.h, b[id(sRL)] + 64 * (th + 1))
                        for p in range(2):
                            if i >= 1:
                                tn.wait_ge(sXA.h, b[id(sXA)] + 2 * i - 1 + p)
                            for fc in range(NFC):
                                tn.matmul(psF[2 + p][:, :], w2s[i % 2][:, fc, :],
                                          H[fc][:, p * 512:(p + 1) * 512],
                                          start=(fc == 0), stop=False)
                            tn.matmul(psF[2 + p][:, :],
                                      hb2r[0:1, dc2 * 128:(dc2 + 1) * 128],
                                      oner[0:1, :], start=False, stop=True
                                      ).then_inc(sM2.h, 1)
                            sM2.n += 1

            @blk.scalar
            def _(s):
                for th in range(2):
                    if th == 1:
                        s.wait_ge(sM2.h, b[id(sM2)] + 16)
                    for fc in range(NFC):
                        i = th * NFC + fc
                        for p in range(2):
                            s.wait_ge(sM1.h, b[id(sM1)] + 2 * i + p + 1)
                            s.activation(out=H[fc][:, p * 512:(p + 1) * 512],
                                         in_=psF[p][:, :], func=AF.Relu,
                                         bias=b1s[:, fc:fc + 1], scale=1.0
                                         ).then_inc(sRL.h, 1)
                            sRL.n += 1

            @blk.vector
            def _(v):
                for th in range(2):
                    for dc2 in range(NDC):
                        i = th * NDC + dc2
                        for p in range(2):
                            v.wait_ge(sM2.h, b[id(sM2)] + 2 * i + p + 1)
                            xsl = X[dc2][:, th * 1024 + p * 512:
                                         th * 1024 + (p + 1) * 512]
                            v.tensor_tensor(out=xsl, in0=psF[2 + p][:, :],
                                            in1=xsl, op=ALU.add).then_inc(sXA.h, 1)
                            sXA.n += 1

            @blk.gpsimd
            def _(g):
                g.wait_ge(sXA.h, b[id(sXA)] + 32)
                evt(g)

        with ExitStack() as sc:
            e2 = sc.enter_context
            Xln = [e2(nc.sbuf_tensor(f"xln{i}", [128, T], BF16))
                   for i in range(NDC)]
            psM = [e2(nc.psum_tensor(f"psm{i}", [128, 512], F32))
                   for i in range(4)]
            psS = [e2(nc.psum_tensor(f"pss{i}", [128, 512], F32))
                   for i in range(4)]

            lnctr = [0]

            def run_ln():
                lnctr[0] += 1
                u = lnctr[0]
                with ExitStack() as scl:
                    e3 = scl.enter_context
                    rowsA = e3(nc.sbuf_tensor(f"rowsA_{u}", [1, T], F32))
                    rowsB = e3(nc.sbuf_tensor(f"rowsB_{u}", [1, T], F32))
                    bcA = e3(nc.sbuf_tensor(f"bcA_{u}", [128, T], F32))
                    bcB = e3(nc.sbuf_tensor(f"bcB_{u}", [128, T], F32))
                    xsq = [e3(nc.sbuf_tensor(f"xsq_{u}_{i}", [128, T], BF16))
                           for i in range(2)]
                    vrow = e3(nc.sbuf_tensor(f"vrow_{u}", [1, T], F32))
                    tmpf = e3(nc.sbuf_tensor(f"tmpfl_{u}", [128, T], F32))
                    with nc.Block() as blk:
                        ln_phase(blk, Xln, psM, psS, rowsA, rowsB, vrow,
                                 bcA, bcB, xsq, tmpf)

            run_ln()
            if dbg:
                with nc.Block() as blk:
                    @blk.gpsimd
                    def _(g):
                        sst = SS[51]
                        for i in range(NDC):
                            g.dma_start(out=dbg_t["dbg_xln1"][i * 128:(i + 1) * 128, :]
                                        .bitcast(BF16)[:, 0:T],
                                        in_=Xln[i][:, :]).then_inc(sst.h, 16)
                            sst.n += 16
                        g.dma_start(out=dbg_rows,
                                    in_=bass.AP(rowscr, 0, [[T, 2], [1, T]])
                                    ).then_inc(sst.h, 16)
                        sst.n += 16
                        g.wait_ge(sst.h, sst.n)
            with ExitStack() as scf:
                e3 = scf.enter_context
                H = [e3(nc.sbuf_tensor(f"ha_{i}", [128, 1024], BF16))
                     for i in range(NFC)]
                w1s = [e3(nc.sbuf_tensor(f"w1s{i}", [128, 8, 128], BF16))
                       for i in range(2)]
                w2s = [e3(nc.sbuf_tensor(f"w2s{i}", [128, 32, 128], BF16))
                       for i in range(2)]

                def evt1(g):
                    if dbg:
                        sst = SS[51]
                        for i in range(NDC):
                            g.dma_start(out=dbg_t["dbg_ffn1"][i * 128:(i + 1) * 128, :],
                                        in_=X[i][:, :]).then_inc(sst.h, 16)
                            sst.n += 16
                        g.wait_ge(sst.h, sst.n)
                with nc.Block() as blk:
                    ffn_phase(blk, Xln, H, w1s, w2s, psM, evt1)

            run_ln()
            if dbg:
                with nc.Block() as blk:
                    @blk.gpsimd
                    def _(g):
                        sst = SS[51]
                        for i in range(NDC):
                            g.dma_start(out=dbg_t["dbg_ln2"][i * 128:(i + 1) * 128, :]
                                        .bitcast(BF16)[:, 0:T],
                                        in_=Xln[i][:, :]).then_inc(sst.h, 16)
                            sst.n += 16
                        g.wait_ge(sst.h, sst.n)

            with ExitStack() as scp:
                e3 = scp.enter_context
                wbf = [e3(nc.sbuf_tensor(f"wbf{i}", [128, D], BF16))
                       for i in range(NDC)]
                wtm = [e3(nc.sbuf_tensor(f"wtm{i}", [128, D], F32))
                       for i in range(2)]
                otile = [e3(nc.sbuf_tensor(f"ot{i}", [128, 512], BF16))
                         for i in range(2)]
                psP = [psM[0], psM[1]]
                sWL, sWC, sPM, sPE, sPS = SS[27:32]
                sWLb = [sWL, SS[56]]
                sPSb = [sPS, SS[57]]
                b = snapd(sWL, sWC, sPM, sPE, sPS, SS[56], SS[57])
                with nc.Block() as blk:
                    @blk.sync
                    def _(s):
                        for wi, wsrc in enumerate([wqT, wkT, wvT]):
                            for c in range(NDC):
                                i = wi * NDC + c
                                if i >= 2:
                                    s.wait_ge(sWC.h, b[id(sWC)] + i - 1)
                                s.dma_start(out=wtm[c % 2][:, :],
                                            in_=wsrc[c * 128:(c + 1) * 128, :]
                                            ).then_inc(sWLb[i % 2].h, 16)
                                sWLb[i % 2].n += 16

                    @blk.vector
                    def _(v):
                        for wi in range(3):
                            for c in range(NDC):
                                i = wi * NDC + c
                                v.wait_ge(sWLb[i % 2].h,
                                          b[id(sWLb[i % 2])] + 16 * (i // 2 + 1))
                                if wi > 0:
                                    v.wait_ge(sPM.h, b[id(sPM)] + wi * 32)
                                v.tensor_copy(out=wbf[c][:, :],
                                              in_=wtm[c % 2][:, :]
                                              ).then_inc(sWC.h, 1)
                                sWC.n += 1
                            for j in range(32):
                                jj = wi * 32 + j
                                v.wait_ge(sPM.h, b[id(sPM)] + jj + 1)
                                if jj >= 2:
                                    v.wait_ge(sPSb[jj % 2].h,
                                              b[id(sPSb[jj % 2])] + 16 * (jj // 2))
                                if wi == 0:
                                    oc = j // 4
                                    ei = v.tensor_scalar(out=otile[j % 2][:, :],
                                                         in0=psP[j % 2][:, :],
                                                         scalar1=bqs[:, oc:oc + 1],
                                                         scalar2=None, op0=ALU.add)
                                elif wi == 1:
                                    oc = j // 4
                                    ei = v.tensor_scalar(out=otile[j % 2][:, :],
                                                         in0=psP[j % 2][:, :],
                                                         scalar1=0.125,
                                                         scalar2=bks[:, oc:oc + 1],
                                                         op0=ALU.mult, op1=ALU.add)
                                else:
                                    ei = v.tensor_copy(out=otile[j % 2][:, :],
                                                       in_=psP[j % 2][:, :])
                                ei.then_inc(sPE.h, 1)
                                sPE.n += 1

                    @blk.tensor
                    def _(tn):
                        tn.wait_ge(sWC.h, b[id(sWC)] + NDC)
                        for wi in range(3):
                            if wi > 0:
                                tn.wait_ge(sWC.h, b[id(sWC)] + (wi + 1) * NDC)
                            for j in range(32):
                                jj = wi * 32 + j
                                if j >= 2:
                                    tn.wait_ge(sPE.h, b[id(sPE)] + jj - 1)
                                if wi < 2:
                                    oc, p = divmod(j, 4)
                                    for ic in range(NDC):
                                        mi = tn.matmul(
                                            psP[j % 2][:, :],
                                            wbf[ic][:, oc * 128:(oc + 1) * 128],
                                            Xln[ic][:, p * 512:(p + 1) * 512],
                                            start=(ic == 0), stop=(ic == NDC - 1))
                                else:
                                    tc, p = divmod(j, 2)
                                    for dc in range(NDC):
                                        tn.matmul(
                                            psP[j % 2][:, :],
                                            Xln[dc][:, tc * 128:(tc + 1) * 128],
                                            wbf[dc][:, p * 512:(p + 1) * 512],
                                            start=(dc == 0), stop=False)
                                    mi = tn.matmul(psP[j % 2][:, :],
                                                   oner[0:1, 0:128],
                                                   bvr[0:1, p * 512:(p + 1) * 512],
                                                   start=False, stop=True)
                                mi.then_inc(sPM.h, 1)
                                sPM.n += 1

                    @blk.scalar
                    def _(s):
                        for wi in range(3):
                            for j in range(32):
                                jj = wi * 32 + j
                                s.wait_ge(sPE.h, b[id(sPE)] + jj + 1)
                                if wi == 0:
                                    oc, p = divmod(j, 4)
                                    dst = qdA[oc * 128:(oc + 1) * 128,
                                              p * 512:(p + 1) * 512]
                                elif wi == 1:
                                    oc, p = divmod(j, 4)
                                    dst = kdA[oc * 128:(oc + 1) * 128,
                                              p * 512:(p + 1) * 512]
                                else:
                                    tc, p = divmod(j, 2)
                                    dst = vdA[tc * 128:(tc + 1) * 128,
                                              p * 512:(p + 1) * 512]
                                s.dma_start(out=dst, in_=otile[j % 2][:, :]
                                            ).then_inc(sPSb[jj % 2].h, 16)
                                sPSb[jj % 2].n += 16
                        s.wait_ge(sPSb[0].h, sPSb[0].n)
                        s.wait_ge(sPSb[1].h, sPSb[1].n)

        with ExitStack() as sc:
            e2 = sc.enter_context
            HO = [e2(nc.sbuf_tensor(f"ho{i}", [128, T], BF16))
                  for i in range(NDC)]
            with ExitStack() as sca:
                e3 = sca.enter_context
                Qp = [e3(nc.sbuf_tensor(f"qp{i}", [128, T], BF16))
                      for i in range(2)]
                Kp = [e3(nc.sbuf_tensor(f"kp{i}", [128, T], BF16))
                      for i in range(2)]
                Vp = [e3(nc.sbuf_tensor(f"vp{i}", [128, 16 * 128], BF16))
                      for i in range(2)]
                scoA = [e3(nc.sbuf_tensor(f"sco{i}", [128, T], F32))
                        for i in range(2)]
                bndE = [e3(nc.sbuf_tensor(f"bnd{i}", [128, T], BF16))
                        for i in range(2)]
                bndF = [e3(nc.sbuf_tensor(f"bnf{i}", [128, T], FP16))
                        for i in range(2)]
                Ssb = [e3(nc.sbuf_tensor(f"ssb{i}", [128, 2047], FP16))
                       for i in range(2)]
                ATs = e3(nc.sbuf_tensor("ats", [128, T], BF16))
                DTs = e3(nc.sbuf_tensor("dts", [128, 2176], BF16))
                Adg = [e3(nc.sbuf_tensor(f"adg{i}", [128, 2176], BF16))
                       for i in range(2)]
                rsc = e3(nc.sbuf_tensor("rsc", [128, 4], F32))
                PS = [e3(nc.psum_tensor(f"aps{i}", [128, 512], F32))
                      for i in range(2)]
                psT = [e3(nc.psum_tensor(f"apt{i}", [128, 1024], BF16))
                       for i in range(4)]
                psO = [e3(nc.psum_tensor(f"apo{i}", [128, 128], F32))
                       for i in range(2)]
                (sQPL, sSR, sSRE, sSST, sBRD, sQK, sBAD, sEXP, sNRM, sAST,
                 sDRD, sTRG, sGEV, sAV, sOEV) = SS[32:47]
                sQPLb = [sQPL, SS[58]]
                sSSTb = [sSST, SS[59]]
                sBRDb = [sBRD, SS[60]]
                sASTb = [sAST, SS[61]]
                sDRDb = [sDRD, SS[62]]
                b = snapd(sQPL, sSR, sSRE, sSST, sBRD, sQK, sBAD, sEXP,
                          sNRM, sAST, sDRD, sTRG, sGEV, sAV, sOEV,
                          SS[58], SS[59], SS[60], SS[61], SS[62])

                def consumed_wait(eng, q):
                    prev = q - 2
                    if prev < 0:
                        return
                    gp, jp = divmod(prev, 8)
                    if jp < 4:
                        eng.wait_ge(sSRE.h, b[id(sSRE)] + 4 * gp + jp + 1)
                    else:
                        eng.wait_ge(sBAD.h, b[id(sBAD)] + 4 * gp + (jp - 4) + 1)

                with nc.Block() as blk:
                    @blk.sync
                    def _(s):
                        for g in range(NG):
                            h, t = divmod(g, NT)
                            if t == 0 and h % 2 == 0:
                                hp = h // 2
                                if h >= 4:
                                    s.wait_ge(sQK.h, b[id(sQK)] + 64 * (h - 2))
                                    s.wait_ge(sAV.h, b[id(sAV)] + NT * (h - 2))
                                sq = sQPLb[hp % 2]
                                s.dma_start(out=Qp[hp % 2][:, :],
                                            in_=qdA[h * 64:(h + 2) * 64, :]
                                            ).then_inc(sq.h, 16)
                                s.dma_start(out=Kp[hp % 2][:, :],
                                            in_=kdA[h * 64:(h + 2) * 64, :]
                                            ).then_inc(sq.h, 16)
                                s.dma_start(
                                    out=Vp[hp % 2][:, :].rearrange(
                                        "p (c k) -> p c k", k=128),
                                    in_=bass.AP(vd, h * 64,
                                                [[D, 128], [128 * D, 16],
                                                 [1, 128]])).then_inc(sq.h, 16)
                                sq.n += 48
                            s.wait_ge(sSSTb[g % 2].h,
                                      b[id(sSSTb[g % 2])] + 16 * (g // 2 + 1))
                            if g >= 2:
                                s.wait_ge(sBAD.h, b[id(sBAD)] + 4 * g - 4)
                            s.dma_start(out=bndF[g % 2][:, :],
                                        in_=bass.AP(spr[h % 2],
                                                    t * 128 * (WS - 1) + 2047,
                                                    [[WS - 1, 128], [1, 2048]])
                                        ).then_inc(sBRDb[g % 2].h, 16)
                            sBRDb[g % 2].n += 16
                            s.wait_ge(sASTb[g % 2].h,
                                      b[id(sASTb[g % 2])] + 16 * (g // 2 + 1))
                            if g >= 2:
                                s.wait_ge(sTRG.h, b[id(sTRG)] + 5 * (g - 1))
                            s.dma_start(out=Adg[g % 2][:, :],
                                        in_=bass.AP(esc[h % 2],
                                                    t * 128 * WA + 129,
                                                    [[WA + 1, 128], [1, 2176]])
                                        ).then_inc(sDRDb[g % 2].h, 16)
                            sDRDb[g % 2].n += 16

                    @blk.scalar
                    def _(s):
                        for g in range(NG):
                            h, t = divmod(g, NT)
                            s.wait_ge(sSRE.h, b[id(sSRE)] + 4 * g + 4)
                            if h >= 2:
                                gp = (h - 2) * NT + t
                                s.wait_ge(sBRDb[gp % 2].h,
                                          b[id(sBRDb[gp % 2])] + 16 * (gp // 2 + 1))
                            s.dma_start(out=bass.AP(spr[h % 2],
                                                    t * 128 * WS + 1024,
                                                    [[WS, 128], [1, 2047]]),
                                        in_=Ssb[g % 2][:, :]
                                        ).then_inc(sSSTb[g % 2].h, 16)
                            sSSTb[g % 2].n += 16
                            s.wait_ge(sBAD.h, b[id(sBAD)] + 4 * g + 4)
                            s.activation(out=bndE[g % 2][:, :],
                                         in_=scoA[g % 2][:, :], func=AF.Exp,
                                         bias=czs[:, 1:2], scale=1.0,
                                         accum_out=rsc[:, g % 2:g % 2 + 1]
                                         ).then_inc(sEXP.h, 1)
                            sEXP.n += 1
                            s.wait_ge(sNRM.h, b[id(sNRM)] + g + 1)
                            if h >= 2:
                                gp = (h - 2) * NT + t
                                s.wait_ge(sDRDb[gp % 2].h,
                                          b[id(sDRDb[gp % 2])] + 16 * (gp // 2 + 1))
                            Av = scoA[g % 2][:, :].bitcast(BF16)[:, 0:T]
                            s.dma_start(out=bass.AP(esc[h % 2],
                                                    t * 128 * WA + 256,
                                                    [[WA, 128], [1, 2048]]),
                                        in_=Av).then_inc(sASTb[g % 2].h, 16)
                            sASTb[g % 2].n += 16

                    @blk.tensor
                    def _(tn):
                        for g in range(NG + 1):
                            if g < NG:
                                h, t = divmod(g, NT)
                                off = 64 * (h % 2)
                                hp2 = (h // 2) % 2
                                tbl = tblA if t < 8 else tblB
                                lhs = Qp[hp2][off:off + 64,
                                              t * 128:(t + 1) * 128]
                                hp_ = h // 2
                                tn.wait_ge(sQPLb[hp_ % 2].h,
                                           b[id(sQPLb[hp_ % 2])]
                                           + 48 * (hp_ // 2 + 1))
                                for p, (o, w) in enumerate(PIECES):
                                    consumed_wait(tn, 8 * g + p)
                                    tn.matmul(PS[p % 2][:, 0:w], lhs,
                                              tbl[off:off + 64, o:o + w],
                                              start=True, stop=True
                                              ).then_inc(sSR.h, 1)
                                    sSR.n += 1
                                for p in range(4):
                                    consumed_wait(tn, 8 * g + 4 + p)
                                    tn.matmul(PS[p % 2][:, :], lhs,
                                              Kp[hp2][off:off + 64,
                                                      p * 512:(p + 1) * 512],
                                              start=True, stop=True
                                              ).then_inc(sQK.h, 1)
                                    sQK.n += 1
                            if g >= 1:
                                g1 = g - 1
                                h1, t1 = divmod(g1, NT)
                                off1 = 64 * (h1 % 2)
                                hp21 = (h1 // 2) % 2
                                Av1 = scoA[g1 % 2][:, :].bitcast(BF16)[:, 0:T]
                                tn.wait_ge(sNRM.h, b[id(sNRM)] + g1 + 1)
                                for grp in range(2):
                                    gg = 5 * g1 + grp
                                    if gg >= 4:
                                        tn.wait_ge(sGEV.h, b[id(sGEV)] + gg - 3)
                                    for bb in range(8):
                                        blkc = grp * 8 + bb
                                        ti = tn.transpose(
                                            psT[gg % 4][:, bb * 128:(bb + 1) * 128],
                                            Av1[:, blkc * 128:(blkc + 1) * 128],
                                            ident[:, :])
                                    ti.then_inc(sTRG.h, 1)
                                    sTRG.n += 1
                                tn.wait_ge(sDRDb[g1 % 2].h,
                                           b[id(sDRDb[g1 % 2])]
                                           + 16 * (g1 // 2 + 1))
                                for grp in range(3):
                                    gg = 5 * g1 + 2 + grp
                                    nb = 8 if grp < 2 else 1
                                    if gg >= 4:
                                        tn.wait_ge(sGEV.h, b[id(sGEV)] + gg - 3)
                                    for bb in range(nb):
                                        blkc = grp * 8 + bb
                                        ti = tn.transpose(
                                            psT[gg % 4][:, bb * 128:(bb + 1) * 128],
                                            Adg[g1 % 2][:, blkc * 128:(blkc + 1) * 128],
                                            ident[:, :])
                                    ti.then_inc(sTRG.h, 1)
                                    sTRG.n += 1
                                tn.wait_ge(sGEV.h, b[id(sGEV)] + 5 * g1 + 2)
                                if g1 >= 2:
                                    tn.wait_ge(sOEV.h, b[id(sOEV)] + g1 - 1)
                                for rc in range(16):
                                    tn.matmul(psO[g1 % 2][off1:off1 + 64, :],
                                              Vp[hp21][:, rc * 128 + off1:rc * 128 + off1 + 64],
                                              ATs[:, rc * 128:(rc + 1) * 128],
                                              start=(rc == 0), stop=False)
                                tn.wait_ge(sGEV.h, b[id(sGEV)] + 5 * g1 + 5)
                                for cc in range(17):
                                    gc = 15 - t1 + cc
                                    mi = tn.matmul(
                                        psO[g1 % 2][off1:off1 + 64, :],
                                        RVsb[:, gc * 64:(gc + 1) * 64],
                                        DTs[:, cc * 128:(cc + 1) * 128],
                                        start=False, stop=(cc == 16))
                                mi.then_inc(sAV.h, 1)
                                sAV.n += 1

                    @blk.vector
                    def _(v):
                        for g in range(NG + 1):
                            if g < NG:
                                for p, (o, w) in enumerate(PIECES):
                                    v.wait_ge(sSR.h, b[id(sSR)] + 4 * g + p + 1)
                                    if p == 0 and g >= 2:
                                        v.wait_ge(sSSTb[g % 2].h,
                                                  b[id(sSSTb[g % 2])]
                                                  + 16 * (g // 2))
                                    v.tensor_copy(out=Ssb[g % 2][:, o:o + w],
                                                  in_=PS[p % 2][:, 0:w]
                                                  ).then_inc(sSRE.h, 1)
                                    sSRE.n += 1
                                for p in range(4):
                                    v.wait_ge(sQK.h, b[id(sQK)] + 4 * g + p + 1)
                                    if p == 0:
                                        v.wait_ge(sBRDb[g % 2].h,
                                                  b[id(sBRDb[g % 2])]
                                                  + 16 * (g // 2 + 1))
                                        if g >= 2:
                                            v.wait_ge(sASTb[g % 2].h,
                                                      b[id(sASTb[g % 2])]
                                                      + 16 * (g // 2))
                                            v.wait_ge(sTRG.h,
                                                      b[id(sTRG)] + 5 * (g - 2) + 2)
                                    v.tensor_tensor(
                                        out=scoA[g % 2][:, p * 512:(p + 1) * 512],
                                        in0=PS[p % 2][:, :],
                                        in1=bndF[g % 2][:, p * 512:(p + 1) * 512],
                                        op=ALU.add).then_inc(sBAD.h, 1)
                                    sBAD.n += 1
                                v.wait_ge(sEXP.h, b[id(sEXP)] + g + 1)
                                v.reciprocal(out=rsc[:, 2 + g % 2:3 + g % 2],
                                             in_=rsc[:, g % 2:g % 2 + 1])
                            if g >= 1:
                                g1 = g - 1
                                for grp in range(2):
                                    gg = 5 * g1 + grp
                                    v.wait_ge(sTRG.h, b[id(sTRG)] + gg + 1)
                                    v.tensor_copy(
                                        out=ATs[:, grp * 1024:(grp + 1) * 1024],
                                        in_=psT[gg % 4][:, :]).then_inc(sGEV.h, 1)
                                    sGEV.n += 1
                            if g < NG:
                                if g == 0:
                                    v.tensor_copy(out=rsc[:, 3:4],
                                                  in_=czs[:, 0:1])
                                v.tensor_scalar(
                                    out=scoA[g % 2][:, :].bitcast(BF16)[:, 0:T],
                                    in0=bndE[g % 2][:, :],
                                    scalar1=rsc[:, 2 + g % 2:3 + g % 2],
                                    scalar2=None, op0=ALU.mult
                                    ).then_inc(sNRM.h, 1)
                                sNRM.n += 1
                            if g >= 1:
                                g1 = g - 1
                                h1, t1 = divmod(g1, NT)
                                off1 = 64 * (h1 % 2)
                                for grp in range(3):
                                    gg = 5 * g1 + 2 + grp
                                    nb = 8 if grp < 2 else 1
                                    v.wait_ge(sTRG.h, b[id(sTRG)] + gg + 1)
                                    v.tensor_copy(
                                        out=DTs[:, grp * 1024:grp * 1024 + nb * 128],
                                        in_=psT[gg % 4][:, 0:nb * 128]
                                        ).then_inc(sGEV.h, 1)
                                    sGEV.n += 1
                                v.wait_ge(sAV.h, b[id(sAV)] + g1 + 1)
                                v.tensor_copy(
                                    out=HO[h1 // 2][off1:off1 + 64,
                                                    t1 * 128:(t1 + 1) * 128],
                                    in_=psO[g1 % 2][off1:off1 + 64, :]
                                    ).then_inc(sOEV.h, 1)
                                sOEV.n += 1

                    @blk.gpsimd
                    def _(g_):
                        g_.wait_ge(sOEV.h, b[id(sOEV)] + NG)

            with ExitStack() as scw:
                e3 = scw.enter_context
                wbf2 = [e3(nc.sbuf_tensor(f"wo{i}", [128, D], BF16))
                        for i in range(NDC)]
                wtm2 = [e3(nc.sbuf_tensor(f"wotm{i}", [128, D], F32))
                        for i in range(2)]
                psW = [e3(nc.psum_tensor(f"psw{i}", [128, 512], F32))
                       for i in range(2)]
                sWL, sWC, sWM, sWA = SS[47:51]
                sWLb = [sWL, SS[63]]
                b = snapd(sWL, sWC, sWM, sWA, SS[63])
                with nc.Block() as blk:
                    @blk.sync
                    def _(s):
                        for c in range(NDC):
                            if c >= 2:
                                s.wait_ge(sWC.h, b[id(sWC)] + c - 1)
                            s.dma_start(out=wtm2[c % 2][:, :],
                                        in_=woT[c * 128:(c + 1) * 128, :]
                                        ).then_inc(sWLb[c % 2].h, 16)
                            sWLb[c % 2].n += 16

                    @blk.vector
                    def _(v):
                        for c in range(NDC):
                            v.wait_ge(sWLb[c % 2].h,
                                      b[id(sWLb[c % 2])] + 16 * (c // 2 + 1))
                            v.tensor_copy(out=wbf2[c][:, :],
                                          in_=wtm2[c % 2][:, :]).then_inc(sWC.h, 1)
                            sWC.n += 1
                        for j in range(32):
                            oc, p = divmod(j, 4)
                            v.wait_ge(sWM.h, b[id(sWM)] + j + 1)
                            xsl = X[oc][:, p * 512:(p + 1) * 512]
                            v.scalar_tensor_tensor(out=xsl, in0=psW[j % 2][:, :],
                                                   scalar=bos[:, oc:oc + 1],
                                                   in1=xsl, op0=ALU.add,
                                                   op1=ALU.add).then_inc(sWA.h, 1)
                            sWA.n += 1

                    @blk.tensor
                    def _(tn):
                        tn.wait_ge(sWC.h, b[id(sWC)] + NDC)
                        for j in range(32):
                            oc, p = divmod(j, 4)
                            if j >= 2:
                                tn.wait_ge(sWA.h, b[id(sWA)] + j - 1)
                            for cc in range(NDC):
                                mi = tn.matmul(
                                    psW[j % 2][:, :],
                                    wbf2[cc][:, oc * 128:(oc + 1) * 128],
                                    HO[cc][:, p * 512:(p + 1) * 512],
                                    start=(cc == 0), stop=(cc == NDC - 1))
                            mi.then_inc(sWM.h, 1)
                            sWM.n += 1

                    @blk.gpsimd
                    def _(g):
                        g.wait_ge(sWA.h, b[id(sWA)] + 32)
                        if dbg:
                            sst = SS[51]
                            for i in range(NDC):
                                g.dma_start(
                                    out=dbg_t["dbg_attn"][i * 128:(i + 1) * 128, :],
                                    in_=X[i][:, :]).then_inc(sst.h, 16)
                                sst.n += 16
                            for i in range(NDC):
                                g.dma_start(
                                    out=dbg_t["dbg_ho"][i * 128:(i + 1) * 128, :]
                                    .bitcast(BF16)[:, 0:T],
                                    in_=HO[i][:, :]).then_inc(sst.h, 16)
                                sst.n += 16
                            g.wait_ge(sst.h, sst.n)

        with ExitStack() as sc:
            e2 = sc.enter_context
            Xln = [e2(nc.sbuf_tensor(f"xln3{i}", [128, T], BF16))
                   for i in range(NDC)]
            psM = [e2(nc.psum_tensor(f"psm3{i}", [128, 512], F32))
                   for i in range(4)]
            psS = [e2(nc.psum_tensor(f"pss3{i}", [128, 512], F32))
                   for i in range(4)]
            with ExitStack() as scl:
                e3 = scl.enter_context
                rowsA = e3(nc.sbuf_tensor("rowsA3", [1, T], F32))
                rowsB = e3(nc.sbuf_tensor("rowsB3", [1, T], F32))
                bcA = e3(nc.sbuf_tensor("bcA3", [128, T], F32))
                bcB = e3(nc.sbuf_tensor("bcB3", [128, T], F32))
                xsq = [e3(nc.sbuf_tensor(f"xsq3{i}", [128, T], BF16))
                       for i in range(2)]
                vrow = e3(nc.sbuf_tensor("vrow3", [1, T], F32))
                tmpf = e3(nc.sbuf_tensor("tmpf3", [128, T], F32))
                with nc.Block() as blk:
                    ln_phase(blk, Xln, psM, psS, rowsA, rowsB, vrow, bcA, bcB,
                             xsq, tmpf)
            with ExitStack() as scf:
                e3 = scf.enter_context
                H = [e3(nc.sbuf_tensor(f"hb_{i}", [128, 1024], BF16))
                     for i in range(NFC)]
                w1s = [e3(nc.sbuf_tensor(f"w1sb_{i}", [128, 8, 128], BF16))
                       for i in range(2)]
                w2s = [e3(nc.sbuf_tensor(f"w2sb_{i}", [128, 32, 128], BF16))
                       for i in range(2)]
                with nc.Block() as blk:
                    ffn_phase(blk, Xln, H, w1s, w2s, psM, lambda g: None)

        with nc.Block() as blk:
            @blk.gpsimd
            def _(g):
                sst = SS[51]
                for i in range(NDC):
                    g.dma_start(out=yT[i * 128:(i + 1) * 128, :],
                                in_=X[i][:, :]).then_inc(sst.h, 16)
                    sst.n += 16
                g.wait_ge(sst.h, sst.n)

    return nc


def _prep_inputs(inputs):
    f = np.float32
    x = np.asarray(inputs["L0_attn_x"], f)

    def chunks(bv, n):
        return np.ascontiguousarray(np.asarray(bv, f).reshape(n, 128).T)

    relv = np.asarray(inputs["L0_attn_relative_v"], f)
    common = {
        "wqT": np.ascontiguousarray(np.asarray(inputs["L0_attn_Wq"], f).T),
        "wkT": np.ascontiguousarray(np.asarray(inputs["L0_attn_Wk"], f).T),
        "wvT": np.ascontiguousarray(np.asarray(inputs["L0_attn_Wv"], f).T),
        "woT": np.ascontiguousarray(np.asarray(inputs["L0_attn_Wo"], f).T),
        "w1T": np.ascontiguousarray(np.asarray(inputs["L0_ffn_W1"], f).T),
        "w2T": np.ascontiguousarray(np.asarray(inputs["L0_ffn_W2"], f).T),
        "bq_c": chunks(inputs["L0_attn_bq"], NDC),
        "bk_c": chunks(inputs["L0_attn_bk"], NDC),
        "bo_c": chunks(inputs["L0_attn_bo"], NDC),
        "b1_c": chunks(inputs["L0_ffn_b1"], NFC),
        "bv_r": np.asarray(inputs["L0_attn_bv"], f).reshape(1, D),
        "b2_r": np.asarray(inputs["L0_ffn_b2"], f).reshape(1, D),
        "relkT": np.ascontiguousarray(
            np.asarray(inputs["L0_attn_relative_k"], f).T),
        "rvpad": np.ascontiguousarray(
            relv[np.clip(np.arange(4096) - 1024, 0, 2046)]),
    }
    in_maps = []
    for c in range(8):
        m = dict(common)
        m["xT"] = np.ascontiguousarray(x[c].T)
        in_maps.append(m)
    return in_maps


def kernel(**inputs):
    in_maps = _prep_inputs(inputs)
    nc = build_nc(dbg=False)
    res = run_bass_kernel_spmd(nc, in_maps, core_ids=list(range(8)))
    out = np.stack([np.ascontiguousarray(r["yT"].T) for r in res.results])
    return out.astype(np.float32)
